# version 23
# baseline (speedup 1.0000x reference)
import math
from contextlib import ExitStack

import numpy as np
import ml_dtypes
import concourse.bass as bass
import concourse.mybir as mybir
from concourse.bass_utils import run_bass_kernel_spmd

F32 = mybir.dt.float32
BF16 = mybir.dt.bfloat16
AF = mybir.ActivationFunctionType
ALU = mybir.AluOpType
AX = mybir.AxisListType

T = 16384
D = 1024
NT = T // 128
NCORES = 8
TPC = NT // NCORES
TOK = TPC * 128
NH = 8
HD = 128
SCALE = 1.0 / math.sqrt(HD)
E0 = 16
K0 = 4
E1 = 8
K1 = 2
FF1 = 4096
MLPF = 2048
LNEPS = 1e-5
MASKNEG = -30000.0

DK = D // 128


def own_tiles(c):
    return [c + NCORES * j for j in range(TPC)]



def _pe_table(length, dim):
    pos = np.arange(length, dtype=np.float32)[:, None]
    inv = np.exp(-np.log(np.float32(10000.0)) * (np.arange(0, dim, 2, dtype=np.float32) / dim))
    ang = pos * inv[None, :]
    pe = np.zeros((length, dim), dtype=np.float32)
    pe[:, 0::2] = np.sin(ang)
    pe[:, 1::2] = np.cos(ang)
    return pe


def host_prepare(inp):
    tt = lambda a: np.ascontiguousarray(np.swapaxes(a, -1, -2))
    sh = {}
    sh["x"] = np.ascontiguousarray(inp["L0_attn_x"].reshape(T, D))
    sh["WqT"] = tt(inp["L0_attn_Wq"])
    sh["WkT"] = tt(inp["L0_attn_Wk"])
    sh["WvT"] = tt(inp["L0_attn_Wv"])
    sh["WoT"] = tt(inp["L0_attn_Wo"])
    sh["gwT"] = tt(inp["L0_ffn_gate_w"])
    sh["ebias"] = inp["L0_ffn_expert_bias"].reshape(1, E0)
    sh["w1T"] = tt(inp["L0_ffn_w1"])
    sh["w2T"] = tt(inp["L0_ffn_w2"])
    sh["w3T"] = tt(inp["L0_ffn_w3"])
    sh["sw1T"] = tt(inp["L0_ffn_sw1"])
    sh["sw2T"] = tt(inp["L0_ffn_sw2"])
    sh["sw3T"] = tt(inp["L0_ffn_sw3"])
    sh["winT"] = tt(inp["L1_attn_w_in"])
    sh["woutT"] = tt(inp["L1_attn_w_out"])
    sh["m1T"] = tt(inp["L1_attn_w1"])
    sh["m2T"] = tt(inp["L1_attn_w2"])
    col = lambda v: np.ascontiguousarray(v.reshape(-1, 128).T)
    sh["loglam_c"] = col(inp["L1_attn_log_lambda"])
    sh["dskip_r"] = inp["L1_attn_D_skip"].reshape(1, D)
    sh["ln1g_c"] = col(inp["L1_attn_ln1_g"])
    sh["ln1b_c"] = col(inp["L1_attn_ln1_b"])
    sh["ln1g_r"] = inp["L1_attn_ln1_g"].reshape(1, D)
    sh["ln1b_r"] = inp["L1_attn_ln1_b"].reshape(1, D)
    sh["ln2g_c"] = col(inp["L1_attn_ln2_g"])
    sh["ln2b_c"] = col(inp["L1_attn_ln2_b"])
    sh["b1_c"] = col(inp["L1_attn_b1"])
    sh["b2_r"] = inp["L1_attn_b2"].reshape(1, D)
    sh["rwT"] = tt(inp["L1_ffn_router_w"])
    sh["wg"] = inp["L1_ffn_w_gate"]
    sh["wu"] = inp["L1_ffn_w_up"]
    sh["wd"] = inp["L1_ffn_w_down"]
    sh["ident"] = np.eye(128, dtype=ml_dtypes.bfloat16)
    sh["pek_c"] = _pe_table(1, HD).reshape(HD, 1)
    m = np.where(np.arange(128)[:, None] > np.arange(128)[None, :], MASKNEG, 0.0).astype(np.float32)
    sh["maskdiag"] = np.ascontiguousarray(np.tile(m, (1, 4)))
    sh["identf"] = np.eye(128, dtype=np.float32)
    per = []
    pe = _pe_table(T, HD)
    diag = np.tile(m, (1, 4))
    for c in range(NCORES):
        d = {}
        rows = np.concatenate([np.arange(t * 128, (t + 1) * 128) for t in own_tiles(c)])
        d["peqT"] = np.ascontiguousarray(pe[rows].T)
        d["x_own"] = np.ascontiguousarray(sh["x"][rows])
        mk = np.zeros((128, 8 * 512), dtype=np.float32)
        for p in range(8):
            if p == c:
                mk[:, p * 512:(p + 1) * 512] = diag
            elif p > c:
                mk[:, p * 512:(p + 1) * 512] = MASKNEG
        d["mask8"] = mk
        selc = np.zeros((128, TPC), dtype=ml_dtypes.bfloat16)
        for j in range(TPC):
            g = c + 8 * j - 1
            if g >= 0:
                selc[g, j] = 1.0
        d["selC"] = selc
        per.append(d)
    return sh, per



ENGS = ("sync", "tensor", "vector", "scalar", "gpsimd")


class Sem:
    __slots__ = ("h", "val")

    def __init__(self, h):
        self.h = h
        self.val = 0


class Prog:

    def __init__(self, nc, stack):
        self.nc = nc
        self.stack = stack
        self.ops = {e: [] for e in ENGS}
        self._semid = 0
        self.chain = {e: self.new_sem(f"chain_{e}") for e in ("vector", "scalar", "gpsimd")}
        self.nops = 0

    def new_sem(self, name=None):
        self._semid += 1
        nm = f"{name or 's'}_{self._semid}"
        h = self.stack.enter_context(self.nc.semaphore(nm))
        return Sem(h)

    def barrier(self):
        snap = [(self.chain["vector"], self.chain["vector"].val),
                (self.chain["scalar"], self.chain["scalar"].val),
                (self.chain["gpsimd"], self.chain["gpsimd"].val),
                (self.sem_pe, self.sem_pe.val)]
        self._barrier = {e: snap for e in ENGS}

    def emit(self, eng, fn, waits=(), incs=(), chain=True):
        pend = getattr(self, "_barrier", {}).pop(eng, None)
        waits = (list(pend) if pend else []) + list(waits)
        incs = list(incs)
        if eng in self.chain and chain:
            ch = self.chain[eng]
            waits.append((ch, ch.val))
            ch.val += 1
            incs.append((ch, 1))
        self.ops[eng].append((waits, fn, incs))
        self.nops += 1

    def dma(self, eng, out, in_, sem, waits=()):
        sem.val += 16
        self.emit(eng, lambda e, o=out, i=in_: e.dma_start(out=o, in_=i),
                  waits=waits, incs=[(sem, 16)], chain=False)
        return sem.val

    def start_phase(self):
        self.barrier()

    def replay(self, block):
        for eng in ENGS:
            ops = self.ops[eng]
            if not ops:
                continue

            def make(ops):
                def body(e):
                    for waits, fn, incs in ops:
                        for s, v in waits[:-1]:
                            e.wait_ge(s.h, v)
                        inst = fn(e)
                        if inst is None:
                            assert not waits and not incs
                            continue
                        if waits:
                            s, v = waits[-1]
                            if v > 0:
                                inst._wait_ge(s.h, v)
                        for s, v in incs:
                            inst.then_inc(s.h, v)
                return body
            getattr(block, eng)(make(ops))


class ArenaBase:

    def __init__(self, handle, width):
        self.h = handle
        self.width = width
        self.off = 0
        self.marks = []
        self.peak = 0

    def alloc_cols(self, name, bfcols, align=32):
        off = (self.off + align - 1) // align * align
        assert off + bfcols <= self.width, (
            f"arena overflow: {name} needs {bfcols} at {off}, width {self.width}")
        self.off = off + bfcols
        self.peak = max(self.peak, self.off)
        return off

    def mark(self):
        self.marks.append(self.off)

    def release(self):
        self.off = self.marks.pop()


class Arena:
    def __init__(self, base, dt):
        self.base = base
        self.dt = dt

    def alloc(self, name, cols, align=32):
        n = cols * (2 if self.dt == F32 else 1)
        off = self.base.alloc_cols(name, n, align)
        v = self.base.h[:, off:off + n]
        return v.bitcast(F32) if self.dt == F32 else v

    def mark(self):
        self.base.mark()

    def release(self):
        self.base.release()


def dram_in(nc, name, shape, dt=F32):
    return nc.dram_tensor(name, list(shape), dt, kind="ExternalInput").ap()


def dram_out(nc, name, shape, dt=F32):
    return nc.dram_tensor(name, list(shape), dt, kind="ExternalOutput").ap()




def ln_tile(P, src, xnb_out, st, scr, waits=(), extra_norm_waits=()):
    vch, sch = P.chain["vector"], P.chain["scalar"]
    P.emit("vector", lambda e, a=src: e.reduce_sum(st[:, 0:1], a, axis=AX.X), waits=list(waits))
    P.emit("vector", lambda e, a=src: e.scalar_tensor_tensor(
        scr, a, 1.0, a, ALU.mult, ALU.mult, accum_out=st[:, 1:2]))
    P.emit("vector", lambda e: e.tensor_scalar(st[:, 2:3], st[:, 0:1], 1.0 / D, None, ALU.mult))
    P.emit("vector", lambda e: e.tensor_tensor(st[:, 3:4], st[:, 2:3], st[:, 2:3], ALU.mult))
    P.emit("vector", lambda e: e.scalar_tensor_tensor(
        st[:, 4:5], st[:, 1:2], 1.0 / D, st[:, 3:4], ALU.mult, ALU.subtract))
    P.emit("scalar", lambda e: e.activation(st[:, 5:6], st[:, 4:5], AF.Ln, bias=P.epscol),
           waits=[(vch, vch.val)])
    P.emit("scalar", lambda e: e.activation(st[:, 6:7], st[:, 5:6], AF.Exp, scale=-0.5))
    P.emit("vector", lambda e, a=src, o=xnb_out: e.tensor_scalar(
        o, a, st[:, 2:3], st[:, 6:7], ALU.subtract, ALU.mult),
        waits=[(sch, sch.val)] + list(extra_norm_waits))
    return vch.val


def transpose8(P, xnb, dst_slices, ident, pT, tev, norm_wait, extra_first=(), gb=None):
    pe = P.sem_pe
    ev = []
    for et in range(8):
        bank = pT[et % 2]
        tw = [norm_wait] + (list(extra_first) if et == 0 else [])
        if tev[et % 2] is not None:
            tw.append(tev[et % 2])
        P.emit("tensor", lambda e, b=bank, s=xnb[:, et * 128:(et + 1) * 128], ii=ident:
               e.transpose(b[:, 0:128], s, ii), waits=tw, incs=[(P.sem_pe, 1)], chain=False)
        pe.val += 1
        if gb is None:
            eng = "vector" if et % 2 == 0 else "scalar"
            P.emit(eng, lambda e, b=bank, d=dst_slices[et]:
                   (e.tensor_copy if hasattr(e, "tensor_copy") else e.copy)(d, b[:, 0:128]),
                   waits=[(pe, pe.val)])
        else:
            eng = "vector"
            g, bb = gb
            P.emit(eng, lambda e, b=bank, d=dst_slices[et],
                   gc=g[:, et:et + 1], bc=bb[:, et:et + 1]:
                   e.tensor_scalar(d, b[:, 0:128], gc, bc, ALU.mult, ALU.add),
                   waits=[(pe, pe.val)])
        tev[et % 2] = (P.chain[eng], P.chain[eng].val)
        ev.append(tev[et % 2])
    return ev


def phase_l0_kv(P, x_ap, B, bf, f32, psum):
    vch, sch = P.chain["vector"], P.chain["scalar"]
    KT, Vb, ident = B["KT"], B["Vb"], B["ident"]
    P.barrier()
    bf.mark(); f32.mark()
    wk = bf.alloc("WkT", 8 * 128)
    wv = bf.alloc("WvT", 8 * 128)
    pek = f32.alloc("pek", 8)

    wsem = P.new_sem("l0w")
    P.dma("gpsimd", wk.rearrange("p (a d) -> p a d", a=8),
          x_ap["WkT"].rearrange("(a p) d -> p a d", p=128), wsem)
    P.dma("gpsimd", wv.rearrange("p (a d) -> p a d", a=8),
          x_ap["WvT"].rearrange("(a p) d -> p a d", p=128), wsem)
    P.dma("sync", ident, x_ap["ident"], wsem)
    P.dma("sync", pek[:, 0:1], x_ap["pek_c"], wsem)
    w_ready = wsem.val

    xt = [f32.alloc("xt0", 1024), f32.alloc("xt1", 1024)]
    scr = f32.alloc("scr", 1024)
    st = f32.alloc("stats", 8)
    xnb = [bf.alloc("xnb0", 1024), bf.alloc("xnb1", 1024)]
    xnt_t = [bf.alloc("xnt_t0", 1024), bf.alloc("xnt_t1", 1024)]
    xsem = [P.new_sem(f"l0x{k}") for k in range(2)]
    pT = [psum["bt0"], psum["bt1"]]
    pK, pV = psum["b2"], psum["b3"]
    pe = P.sem_pe

    xt_free = [0, 0]
    xnb_free = [0, 0]
    tev = [None, None]
    kev = vev = None
    for i in range(NT):
        k = i % 2
        P.dma("sync", xt[k], x_ap["x"][i * 128:(i + 1) * 128, :], xsem[k],
              waits=[(vch, xt_free[k])])
        nw = ln_tile(P, xt[k], xnb[k], st, scr,
                     waits=[(xsem[k], xsem[k].val)],
                     extra_norm_waits=[(pe, xnb_free[k])])
        xt_free[k] = vch.val
        dsts = [xnt_t[k][:, et * 128:(et + 1) * 128] for et in range(8)]
        ev = transpose8(P, xnb[k], dsts, ident, pT, tev, (vch, nw),
                        extra_first=([(wsem, w_ready)] if i == 0 else []))
        xnb_free[k] = pe.val
        for et in range(8):
            kw = list(ev) if et == 0 else []
            if et == 0 and kev is not None:
                kw.append(kev)
            P.emit("tensor", lambda e, w=wk[:, et * 128:(et + 1) * 128], r=dsts[et], o=et:
                   e.matmul(pK[:, 0:128], w, r, start=(o == 0), stop=(o == 7)),
                   waits=kw, incs=([(pe, 1)] if et == 7 else []), chain=False)
        pe.val += 1
        P.emit("scalar", lambda e, d=KT[:, i * 128:(i + 1) * 128]:
               e.add(d, pK[:, 0:128], pek[:, 0:1]),
               waits=[(pe, pe.val)])
        kev = (sch, sch.val)
        for et in range(8):
            vw = [vev] if (et == 0 and vev is not None) else []
            P.emit("tensor", lambda e, s=dsts[et], w=wv[:, et * 128:(et + 1) * 128], o=et:
                   e.matmul(pV[:, 0:128], s, w, start=(o == 0), stop=(o == 7)),
                   waits=vw, incs=([(pe, 1)] if et == 7 else []), chain=False)
        pe.val += 1
        P.emit("vector", lambda e, d=Vb[:, i * 128:(i + 1) * 128]: e.tensor_copy(d, pV[:, 0:128]),
               waits=[(pe, pe.val)])
        vev = (vch, vch.val)
    bf.release(); f32.release()
    return dict()


def phase_l0_own(P, x_ap, B, bf, f32, psum, compute_q=True, xkey="x_own", tkey="xnT"):
    vch, sch = P.chain["vector"], P.chain["scalar"]
    ident, xnT = B["ident"], B[tkey]
    Qb = B.get("Qb")
    P.barrier()
    bf.mark(); f32.mark()
    wsem = P.new_sem("l0qw")
    if compute_q:
        wq = bf.alloc("WqT", 8 * 1024)
        peq = f32.alloc("peqT", TOK)
        P.dma("gpsimd", wq.rearrange("p (a d) -> p a d", a=8),
              x_ap["WqT"].rearrange("(a p) d -> p a d", p=128), wsem)
        P.dma("sync", peq, x_ap["peqT"], wsem)
    w_ready = wsem.val

    xt = [f32.alloc("xt0", 1024), f32.alloc("xt1", 1024)]
    scr = f32.alloc("scr", 1024)
    st = f32.alloc("stats", 8)
    xnb = [bf.alloc("xnb0", 1024), bf.alloc("xnb1", 1024)]
    xsem = [P.new_sem(f"l0q{k}") for k in range(2)]
    pT = [psum["bt0"], psum["bt1"]]
    pQ = [psum["b4"], psum["b5"]]
    pe = P.sem_pe

    xt_free = [0, 0]
    xnb_free = [0, 0]
    tev = [None, None]
    qev = [None, None]
    for j in range(TPC):
        k = j % 2
        P.dma("sync", xt[k], x_ap[xkey][j * 128:(j + 1) * 128, :], xsem[k],
              waits=[(vch, xt_free[k])])
        nw = ln_tile(P, xt[k], xnb[k], st, scr,
                     waits=[(xsem[k], xsem[k].val)],
                     extra_norm_waits=[(pe, xnb_free[k])])
        xt_free[k] = vch.val
        dsts = [xnT[:, et * 2048 + j * 128: et * 2048 + j * 128 + 128] for et in range(8)]
        ev = transpose8(P, xnb[k], dsts, ident, pT, tev, (vch, nw),
                        extra_first=([(wsem, w_ready)] if j == 0 else []))
        xnb_free[k] = pe.val
        for h in (range(NH) if compute_q else ()):
            bank = pQ[h % 2]
            for et in range(8):
                qw = list(ev) if et == 0 else []
                if et == 0 and qev[h % 2] is not None:
                    qw.append(qev[h % 2])
                P.emit("tensor", lambda e, w=wq[:, et * 1024 + h * 128: et * 1024 + h * 128 + 128],
                       r=dsts[et], o=et, b=bank:
                       e.matmul(b[:, 0:128], w, r, start=(o == 0), stop=(o == 7)),
                       waits=qw, incs=([(pe, 1)] if et == 7 else []), chain=False)
            pe.val += 1
            P.emit("vector", lambda e, b=bank,
                   d=Qb[:, j * 1024 + h * 128: j * 1024 + h * 128 + 128],
                   p=peq[:, j * 128:(j + 1) * 128]:
                   e.tensor_tensor(d, b[:, 0:128], p, ALU.add),
                   waits=[(pe, pe.val)])
            qev[h % 2] = (P.chain["vector"], P.chain["vector"].val)
    bf.release(); f32.release()
    return dict()



ARENA_COLS = 106368


def make_ctx(stack, nc):
    P = Prog(nc, stack)
    P.sem_pe = P.new_sem("pe")
    bfh = stack.enter_context(nc.sbuf_tensor("arena", [128, ARENA_COLS], BF16))
    base = ArenaBase(bfh, ARENA_COLS)
    f32 = Arena(base, F32)
    bf = Arena(base, BF16)
    consts = f32.alloc("consts", 8)
    P.epscol = consts[:, 0:1]
    P.onecol = consts[:, 1:2]
    P.emit("vector", lambda e: e.memset(P.epscol, LNEPS))
    P.emit("vector", lambda e: e.memset(P.onecol, 1.0))
    psum = {}
    for di, nm in enumerate(("d0", "d1", "d2", "d3")):
        psum[nm] = stack.enter_context(nc.psum_tensor(nm, [128, 1024], F32))
    for bi in range(8):
        psum[f"b{bi}"] = psum[f"d{bi // 2}"][:, (bi % 2) * 512:(bi % 2) * 512 + 512]
    psum["bt0"] = psum["b0"].bitcast(BF16)
    psum["bt1"] = psum["b1"].bitcast(BF16)
    return P, bf, f32, psum


def declare_inputs(nc, names_shapes):
    io = {}
    for name, arr in names_shapes.items():
        io[name] = dram_in(nc, name, arr.shape, F32 if arr.dtype == np.float32 else BF16)
    return io


def finish_debug_outputs(P, nc, outs):
    ssem = P.new_sem("dbgst")
    vch, sch = P.chain["vector"], P.chain["scalar"]
    for name, (ap, dt) in outs.items():
        o = dram_out(nc, name, [128, ap.shape[-1]] if len(ap.shape) == 2 else ap.shape, dt)
        P.dma("gpsimd", o, ap, ssem,
              waits=[(vch, vch.val), (sch, sch.val), (P.sem_pe, P.sem_pe.val)])
    P.emit("gpsimd", lambda e: e.wait_ge(ssem.h, ssem.val), chain=False)



def phase_attn(P, x_ap, B, bf, f32, psum):
    vch, sch = P.chain["vector"], P.chain["scalar"]
    pe = P.sem_pe
    KT, Vb, Qb = B["KT"], B["Vb"], B["Qb"]
    P.barrier()
    bf.mark(); f32.mark()

    attn = B["attn"]
    mask8 = f32.alloc("mask8", 8 * 512)
    stash = f32.alloc("stash", 8 * 512)
    probs = [[bf.alloc("pr00", 512), bf.alloc("pr01", 512)],
             [bf.alloc("pr10", 512), bf.alloc("pr11", 512)]]
    mcol = [f32.alloc("mcol0", 8), f32.alloc("mcol1", 8)]
    onescol = bf.alloc("onescol", 8)
    onerow = f32.alloc("onerow", 128)
    msc = f32.alloc("msc", 8)
    identf = f32.alloc("identf", 128)
    pacc = [f32.alloc("pacc0", 512), f32.alloc("pacc1", 512)]
    rb = stash[:, 2048:3072]
    pbf = [probs[0][0], probs[1][0]]

    wsem = P.new_sem("atw")
    P.dma("sync", mask8, x_ap["mask8"], wsem)
    P.dma("sync", identf, x_ap["identf"], wsem)
    mask_ready = wsem.val
    P.emit("vector", lambda e: e.memset(onescol, 1.0))
    P.emit("vector", lambda e: e.memset(onerow[0:1, :], 1.0))
    ones_ready = vch.val

    sc = [[psum["b0"], psum["b1"]], [psum["b2"], psum["b3"]]]
    pa = [psum["b4"], psum["b5"]]
    pl = [psum["b6"], psum["b7"]]
    pm = psum["b0"]

    sc_free = [[None, None], [None, None]]
    pa_free = [None, None]
    pl_free = [None, None]
    pm_free = None
    probs_free = [[0, 0], [0, 0]]
    probs_accv = [[0, 0], [0, 0]]

    for j in range(TPC):
        nmain = 8 * j
        for hg in (0, 1):
            qsl = Qb[:, j * 1024 + hg * 512: j * 1024 + (hg + 1) * 512]
            for p in range(8):
                bank = sc[hg][p % 2]
                w = []
                if j == 0 and hg == 0 and p < 2:
                    w.append((wsem, mask_ready))
                if sc_free[hg][p % 2] is not None:
                    w.append(sc_free[hg][p % 2])
                P.emit("tensor", lambda e, kt=KT[:, (8 * j + p) * 128:(8 * j + p + 1) * 128],
                       q=qsl, b=bank: e.matmul(b[:, 0:512], kt, q, start=True, stop=True),
                       waits=w, incs=[(pe, 1)], chain=False)
                pe.val += 1
                P.emit("vector", lambda e, b=bank, o=stash[:, p * 512:(p + 1) * 512],
                       mk=mask8[:, p * 512:(p + 1) * 512]:
                       e.tensor_tensor(o, b[:, 0:512], mk, ALU.add),
                       waits=[(pe, pe.val)])
                sc_free[hg][p % 2] = (vch, vch.val)
            P.emit("vector", lambda e: e.reduce_max(msc[:, 0:1], stash, axis=AX.X))
            mx_v = vch.val
            tw = [(vch, mx_v)]
            if j == 0 and hg == 0:
                tw.append((wsem, mask_ready))
            if pm_free is not None:
                tw.append(pm_free)
            if sc_free[0][0] is not None:
                tw.append(sc_free[0][0])
            P.emit("tensor", lambda e: e.transpose(pm[0:1, 0:128], msc[:, 0:1], identf),
                   waits=tw, incs=[(pe, 1)], chain=False)
            pe.val += 1
            P.emit("vector", lambda e: e.reduce_max(msc[0:1, 1:2], pm[0:1, 0:128], axis=AX.X),
                   waits=[(pe, pe.val)])
            P.emit("vector", lambda e: e.tensor_scalar(
                msc[0:1, 3:4], msc[0:1, 1:2], -SCALE, None, ALU.mult))
            mneg_v = vch.val
            P.emit("tensor", lambda e: e.matmul(pm[:, 33:34], onerow[0:1, :],
                   msc[0:1, 3:4], start=True, stop=True),
                   waits=[(vch, mneg_v), (vch, ones_ready)], incs=[(pe, 1)], chain=False)
            pe.val += 1
            P.emit("vector", lambda e, d=mcol[hg][:, 0:1]: e.tensor_copy(d, pm[:, 33:34]),
                   waits=[(pe, pe.val)])
            pm_free = (vch, vch.val)
            sc_free[0][0] = (vch, vch.val)
            mc_ready = vch.val
            for p in range(8):
                pb = probs[hg][p % 2]
                P.emit("scalar", lambda e, o=pb, i=stash[:, p * 512:(p + 1) * 512],
                       m=mcol[hg][:, 0:1]:
                       e.activation(o, i, AF.Exp, bias=m, scale=SCALE),
                       waits=[(vch, mc_ready), (pe, probs_free[hg][p % 2]),
                              (vch, probs_accv[hg][p % 2])])
                first = (p == 0)
                last = (j == 0 and p == 7)
                if first:
                    P.emit("vector", lambda e, a=pacc[hg], pbuf=pb: e.tensor_copy(a, pbuf),
                           waits=[(sch, sch.val)])
                else:
                    P.emit("vector", lambda e, a=pacc[hg], pbuf=pb:
                           e.tensor_tensor(a, a, pbuf, ALU.add),
                           waits=[(sch, sch.val)])
                acc_v = vch.val
                pw = []
                if first and pa_free[hg] is not None:
                    pw.append(pa_free[hg])
                pw.append((sch, sch.val))
                P.emit("tensor", lambda e, v=Vb[:, (8 * j + p) * 128:(8 * j + p + 1) * 128],
                       pbuf=pb, b=pa[hg], f=first, l=last:
                       e.matmul(b[:, 0:512], v, pbuf, start=f, stop=l),
                       waits=pw, incs=[(pe, 1)], chain=False)
                pe.val += 1
                probs_free[hg][p % 2] = pe.val
                probs_accv[hg][p % 2] = acc_v
        exp_done = {}
        for kb in range(nmain + 1):
            if kb < nmain:
                for hg in (0, 1):
                    bank = sc[hg][kb % 2]
                    qsl = Qb[:, j * 1024 + hg * 512: j * 1024 + (hg + 1) * 512]
                    w = [sc_free[hg][kb % 2]] if sc_free[hg][kb % 2] is not None else []
                    P.emit("tensor", lambda e, kt=KT[:, kb * 128:(kb + 1) * 128], q=qsl, b=bank:
                           e.matmul(b[:, 0:512], kt, q, start=True, stop=True),
                           waits=w, incs=[(pe, 1)], chain=False)
                    pe.val += 1
                    P.emit("scalar", lambda e, b=bank, o=probs[hg][kb % 2], m=mcol[hg][:, 0:1]:
                           e.activation(o, b[:, 0:512], AF.Exp, bias=m, scale=SCALE),
                           waits=[(pe, pe.val), (pe, probs_free[hg][kb % 2]),
                                  (vch, probs_accv[hg][kb % 2])])
                    sc_free[hg][kb % 2] = (sch, sch.val)
                    exp_done[(kb, hg)] = (sch, sch.val)
            if kb > 0:
                kp = kb - 1
                last = (kp == nmain - 1)
                for hg in (0, 1):
                    pb = probs[hg][kp % 2]
                    P.emit("vector", lambda e, a=pacc[hg], pbuf=pb:
                           e.tensor_tensor(a, a, pbuf, ALU.add),
                           waits=[exp_done[(kp, hg)]])
                    probs_accv[hg][kp % 2] = vch.val
                    P.emit("tensor", lambda e, v=Vb[:, kp * 128:(kp + 1) * 128],
                           pbuf=pb, b=pa[hg], l=last:
                           e.matmul(b[:, 0:512], v, pbuf, start=False, stop=l),
                           waits=[exp_done[(kp, hg)]], incs=[(pe, 1)], chain=False)
                    pe.val += 1
                    probs_free[hg][kp % 2] = pe.val
        for hg in (0, 1):
            P.emit("vector", lambda e, d=pbf[hg], a=pacc[hg]: e.tensor_copy(d, a),
                   waits=[(pe, probs_free[hg][0])])
            cast_v = vch.val
            lw = [(vch, cast_v), (vch, ones_ready)]
            if pl_free[hg] is not None:
                lw.append(pl_free[hg])
            P.emit("tensor", lambda e, o=pl[hg][0:1, 0:512], pbuf=pbf[hg]:
                   e.matmul(o, onescol[:, 0:1], pbuf, start=True, stop=True),
                   waits=lw, incs=[(pe, 1)], chain=False)
            pe.val += 1
            probs_free[hg][0] = pe.val
        l_done = pe.val
        P.emit("vector", lambda e: e.reciprocal(stash[0:1, 0:512], pl[0][0:1, 0:512]),
               waits=[(pe, l_done)])
        pl_free[0] = (vch, vch.val)
        P.emit("vector", lambda e: e.reciprocal(stash[0:1, 512:1024], pl[1][0:1, 0:512]))
        pl_free[1] = (vch, vch.val)
        rec_v = vch.val
        for hg in (0, 1):
            bw = [(vch, rec_v)]
            if pm_free is not None:
                bw.append(pm_free)
            if sc_free[0][0] is not None:
                bw.append(sc_free[0][0])
            P.emit("tensor", lambda e, r=stash[0:1, hg * 512:(hg + 1) * 512]:
                   e.matmul(pm[:, 0:512], onerow[0:1, :], r, start=True, stop=True),
                   waits=bw, incs=[(pe, 1)], chain=False)
            pe.val += 1
            P.emit("scalar", lambda e, d=rb[:, hg * 512:(hg + 1) * 512]:
                   e.copy(d, pm[:, 0:512]), waits=[(pe, pe.val)])
            pm_free = (sch, sch.val)
            sc_free[0][0] = (sch, sch.val)
            P.emit("vector", lambda e, b=pa[hg], r=rb[:, hg * 512:(hg + 1) * 512],
                   d=attn[:, j * 1024 + hg * 512: j * 1024 + (hg + 1) * 512]:
                   e.tensor_tensor(d, b[:, 0:512], r, ALU.mult),
                   waits=[(sch, sch.val)])
            pa_free[hg] = (vch, vch.val)
    bf.release(); f32.release()
    return dict(attn=attn)


def phase_wo(P, x_ap, B, bf, f32, psum, s_out):
    vch, sch = P.chain["vector"], P.chain["scalar"]
    pe = P.sem_pe
    attn = B["attn"]
    P.barrier()
    bf.mark(); f32.mark()
    wo = bf.alloc("WoT", 8 * 1024)
    wsem = P.new_sem("wow")
    P.dma("gpsimd", wo.rearrange("p (a d) -> p a d", a=8),
          x_ap["WoT"].rearrange("(a p) d -> p a d", p=128), wsem)
    w_ready = wsem.val
    s_tile = [f32.alloc("s_t0", 1024), f32.alloc("s_t1", 1024)]
    ssem = P.new_sem("s_st")
    ps = [psum["b2"], psum["b3"]]
    ps_free = [None, None]
    s_free = [0, 0]
    for j in range(TPC):
        k = j % 2
        for n in (0, 1):
            bank = ps[n]
            for h in range(8):
                w = []
                if j == 0 and h == 0:
                    w.append((wsem, w_ready))
                if h == 0 and ps_free[n] is not None:
                    w.append(ps_free[n])
                P.emit("tensor", lambda e, a=attn[:, j * 1024 + h * 128: j * 1024 + h * 128 + 128],
                       ww=wo[:, h * 1024 + n * 512: h * 1024 + n * 512 + 512], b=bank, hh=h:
                       e.matmul(b[:, 0:512], a, ww, start=(hh == 0), stop=(hh == 7)),
                       waits=w, incs=([(pe, 1)] if h == 7 else []), chain=False)
            pe.val += 1
            P.emit("vector", lambda e, b=bank, d=s_tile[k][:, n * 512:(n + 1) * 512]:
                   e.tensor_copy(d, b[:, 0:512]),
                   waits=[(pe, pe.val), (ssem, s_free[k])])
            ps_free[n] = (vch, vch.val)
        P.dma("sync", s_out[j * 128:(j + 1) * 128, :], s_tile[k], ssem,
              waits=[(vch, vch.val)])
        s_free[k] = ssem.val
    P.emit("sync", lambda e: e.wait_ge(ssem.h, ssem.val), chain=False)
    bf.release(); f32.release()
    return dict()



def phase_l0_gate(P, x_ap, B, bf, f32, psum):
    vch, sch = P.chain["vector"], P.chain["scalar"]
    pe = P.sem_pe
    xnT = B["xnT"]
    g_buf = B["g_buf"]
    P.barrier()
    bf.mark(); f32.mark()
    gw = bf.alloc("gwT", 8 * 16)
    ebias = f32.alloc("ebias", 16)
    ebc = f32.alloc("ebc", 16)
    onerow = f32.alloc("onerow", 128)
    L = f32.alloc("L", 16)
    Bi = f32.alloc("Bi", 16)
    sel = f32.alloc("sel", 16)
    tmp = f32.alloc("tmp", 16)
    cols = f32.alloc("cols", 8)
    wsem = P.new_sem("gw")
    P.dma("gpsimd", gw.rearrange("p (a d) -> p a d", a=8),
          x_ap["gwT"].rearrange("(a p) d -> p a d", p=128), wsem)
    P.dma("sync", ebias[0:1, :], x_ap["ebias"], wsem)
    w_ready = wsem.val
    P.emit("vector", lambda e: e.memset(onerow[0:1, :], 1.0))
    ones_v = vch.val
    P.emit("tensor", lambda e: e.matmul(psum["b2"][:, 0:16], onerow[0:1, :], ebias[0:1, :],
           start=True, stop=True),
           waits=[(wsem, w_ready), (vch, ones_v)], incs=[(pe, 1)], chain=False)
    pe.val += 1
    P.emit("vector", lambda e: e.tensor_copy(ebc, psum["b2"][:, 0:16]), waits=[(pe, pe.val)])
    ebc_v = vch.val
    lg_free = None
    for j in range(TPC):
        for et in range(8):
            w = [(vch, ebc_v)] if et == 0 else []
            if et == 0 and lg_free is not None:
                w.append(lg_free)
            P.emit("tensor", lambda e, s=xnT[:, et * 2048 + j * 128: et * 2048 + j * 128 + 128],
                   g=gw[:, et * 16:(et + 1) * 16], o=et:
                   e.matmul(psum["b2"][:, 32:48], s, g, start=(o == 0), stop=(o == 7)),
                   waits=w, incs=([(pe, 1)] if et == 7 else []), chain=False)
        pe.val += 1
        P.emit("vector", lambda e: e.tensor_copy(L, psum["b2"][:, 32:48]), waits=[(pe, pe.val)])
        lg_free = (vch, vch.val)
        P.emit("vector", lambda e: e.tensor_tensor(Bi, L, ebc, ALU.add))
        P.emit("vector", lambda e: e.memset(sel, 0.0))
        for r in range(K0):
            P.emit("vector", lambda e: e.reduce_max(cols[:, 0:1], Bi, axis=AX.X))
            P.emit("vector", lambda e: e.tensor_scalar(tmp, Bi, cols[:, 0:1], None, ALU.is_ge))
            P.emit("vector", lambda e: e.tensor_tensor(sel, sel, tmp, ALU.add))
            P.emit("vector", lambda e: e.scalar_tensor_tensor(Bi, tmp, -1e30, Bi, ALU.mult, ALU.add))
        P.emit("vector", lambda e: e.tensor_scalar(tmp, sel, 1e9, -1e9, ALU.mult, ALU.add))
        P.emit("vector", lambda e: e.tensor_tensor(tmp, L, tmp, ALU.add))
        P.emit("vector", lambda e: e.reduce_max(cols[:, 1:2], tmp, axis=AX.X))
        P.emit("vector", lambda e: e.tensor_scalar(cols[:, 2:3], cols[:, 1:2], -1.0, None, ALU.mult))
        P.emit("scalar", lambda e: e.activation(Bi, tmp, AF.Exp, bias=cols[:, 2:3],
               accum_out=cols[:, 3:4]), waits=[(vch, vch.val)])
        P.emit("vector", lambda e: e.reciprocal(cols[:, 4:5], cols[:, 3:4]),
               waits=[(sch, sch.val)])
        P.emit("vector", lambda e, d=g_buf[:, j * 16:(j + 1) * 16]:
               e.tensor_scalar(d, Bi, cols[:, 4:5], None, ALU.mult))
    bf.release(); f32.release()
    return dict()



def phase_l0_ffn(P, x_ap, B, bf, f32, psum):
    vch, sch = P.chain["vector"], P.chain["scalar"]
    pe = P.sem_pe
    xnT, g_buf, y_acc = B["xnT"], B["g_buf"], B["y_acc"]
    P.barrier()
    bf.mark(); f32.mark()
    w1 = bf.alloc("w1T", 8 * 1024)
    w3 = bf.alloc("w3T", 8 * 1024)
    w2 = bf.alloc("w2T", 8 * 1024)
    hh = bf.alloc("hh", 8 * 2048)
    sil = [f32.alloc("sil0", 512), f32.alloc("sil1", 512)]
    P.emit("vector", lambda e: e.memset(y_acc[:, 0:8192], 0.0))
    P.emit("vector", lambda e: e.memset(y_acc[:, 8192:16384], 0.0))
    y_zeroed = vch.val
    wsem = P.new_sem("f0w")
    pu = [(psum["b2"], psum["b3"]), (psum["b4"], psum["b5"])]
    pd = [psum["b6"], psum["b7"]]
    pu_free = [None, None]
    pd_free = [None, None]
    hh_free = 0
    w_free = 0
    sil_free = [0, 0]
    for e0 in range(E0 + 1):
        shared = (e0 == E0)
        pref = "sw" if shared else "w"
        keys = (("sw1T", "sw2T", "sw3T") if shared else ("w1T", "w2T", "w3T"))
        src1 = x_ap[keys[0]] if shared else x_ap[keys[0]][e0]
        src2 = x_ap[keys[1]] if shared else x_ap[keys[1]][e0]
        src3 = x_ap[keys[2]] if shared else x_ap[keys[2]][e0]
        dw = [(P.sem_pe, w_free)] if w_free else []
        P.dma("gpsimd", w1.rearrange("p (a d) -> p a d", a=8),
              src1.rearrange("(a p) d -> p a d", p=128), wsem, waits=dw)
        P.dma("gpsimd", w3.rearrange("p (a d) -> p a d", a=8),
              src3.rearrange("(a p) d -> p a d", p=128), wsem)
        P.dma("gpsimd", w2.rearrange("p (a d) -> p a d", a=8),
              src2.rearrange("(a p) d -> p a d", p=128), wsem)
        w_up = wsem.val - 16
        w_dn = wsem.val
        ctr = 0
        for isub in range(8):
            for tc in range(4):
                b1, b3 = pu[ctr % 2]
                w = [(wsem, w_up)] if (isub == 0 and tc == 0) else []
                if pu_free[ctr % 2] is not None:
                    w.append(pu_free[ctr % 2])
                if isub == 0 and tc == 0 and hh_free:
                    w.append((pe, hh_free))
                for et in range(8):
                    P.emit("tensor", lambda e, ww=w1[:, et * 1024 + isub * 128: et * 1024 + isub * 128 + 128],
                           r=xnT[:, et * 2048 + tc * 512: et * 2048 + tc * 512 + 512],
                           b=b1, o=et:
                           e.matmul(b[:, 0:512], ww, r, start=(o == 0), stop=(o == 7)),
                           waits=(w if et == 0 else []), chain=False)
                for et in range(8):
                    P.emit("tensor", lambda e, ww=w3[:, et * 1024 + isub * 128: et * 1024 + isub * 128 + 128],
                           r=xnT[:, et * 2048 + tc * 512: et * 2048 + tc * 512 + 512],
                           b=b3, o=et:
                           e.matmul(b[:, 0:512], ww, r, start=(o == 0), stop=(o == 7)),
                           incs=([(pe, 1)] if et == 7 else []), chain=False)
                pe.val += 1
                P.emit("scalar", lambda e, b=b1, o=sil[ctr % 2]:
                       e.activation(o, b[:, 0:512], AF.Silu),
                       waits=[(pe, pe.val), (vch, sil_free[ctr % 2])])
                P.emit("vector", lambda e, b=b3, s=sil[ctr % 2],
                       d=hh[:, isub * 2048 + tc * 512: isub * 2048 + tc * 512 + 512]:
                       e.tensor_tensor(d, s, b[:, 0:512], ALU.mult),
                       waits=[(sch, sch.val)])
                pu_free[ctr % 2] = (vch, vch.val)
                sil_free[ctr % 2] = vch.val
                ctr += 1
        hh_v = vch.val
        for j in range(TPC):
            for n in (0, 1):
                bank = pd[n]
                for isub in range(8):
                    w = []
                    if isub == 0:
                        w = [(vch, hh_v), (wsem, w_dn)]
                        if pd_free[n] is not None:
                            w.append(pd_free[n])
                    P.emit("tensor", lambda e, s=hh[:, isub * 2048 + j * 128: isub * 2048 + j * 128 + 128],
                           ww=w2[:, isub * 1024 + n * 512: isub * 1024 + n * 512 + 512],
                           b=bank, o=isub:
                           e.matmul(b[:, 0:512], s, ww, start=(o == 0), stop=(o == 7)),
                           waits=w, incs=([(pe, 1)] if isub == 7 else []), chain=False)
                pe.val += 1
                yw = [(pe, pe.val)]
                if e0 == 0:
                    yw.append((vch, y_zeroed))
                ysl = y_acc[:, j * 1024 + n * 512: j * 1024 + n * 512 + 512]
                if shared:
                    P.emit("vector", lambda e, b=bank, d=ysl:
                           e.tensor_tensor(d, b[:, 0:512], d, ALU.add), waits=yw)
                else:
                    P.emit("vector", lambda e, b=bank, d=ysl, gc=g_buf[:, j * 16 + e0: j * 16 + e0 + 1]:
                           e.scalar_tensor_tensor(d, b[:, 0:512], gc, d, ALU.mult, ALU.add),
                           waits=yw)
                pd_free[n] = (vch, vch.val)
        hh_free = pe.val
        w_free = pe.val
    bf.release(); f32.release()
    return dict()



def phase_combine(P, x_ap, B, bf, f32, psum, s_in, x1_out, xkey="x_own"):
    vch, sch = P.chain["vector"], P.chain["scalar"]
    y_acc = B["y_acc"]
    P.barrier()
    bf.mark(); f32.mark()
    xt = [f32.alloc("cx0", 1024), f32.alloc("cx1", 1024)]
    stl = [f32.alloc("cs0", 1024), f32.alloc("cs1", 1024)]
    gt = f32.alloc("cg", 1024)
    o1 = [f32.alloc("co0", 1024), f32.alloc("co1", 1024)]
    xsem = [P.new_sem("c0x0"), P.new_sem("c0x1")]
    osem = P.new_sem("c0o")
    buf_free = [0, 0]
    out_free = [0, 0]
    for j in range(TPC):
        k = j % 2
        P.dma("sync", xt[k], x_ap[xkey][j * 128:(j + 1) * 128, :], xsem[k],
              waits=[(vch, buf_free[k])])
        P.dma("sync", stl[k], s_in[j * 128:(j + 1) * 128, :], xsem[k])
        P.emit("scalar", lambda e, y=y_acc[:, j * 1024:(j + 1) * 1024]:
               e.activation(gt, y, AF.Silu), waits=[(vch, vch.val)])
        P.emit("vector", lambda e, s=stl[k]: e.tensor_tensor(gt, s, gt, ALU.mult),
               waits=[(sch, sch.val), (xsem[k], xsem[k].val)])
        P.emit("vector", lambda e, x=xt[k], d=o1[k]: e.tensor_tensor(d, x, gt, ALU.add),
               waits=[(osem, out_free[k])])
        buf_free[k] = vch.val
        P.dma("sync", x1_out[j * 128:(j + 1) * 128, :], o1[k], osem,
              waits=[(vch, vch.val)])
        out_free[k] = osem.val
    P.emit("sync", lambda e: e.wait_ge(osem.h, osem.val), chain=False)
    bf.release(); f32.release()
    return dict()



def phase_l1_router(P, x_ap, B, bf, f32, psum):
    vch, sch = P.chain["vector"], P.chain["scalar"]
    pe = P.sem_pe
    xnT, cw_buf = B["xn1T"], B["cw_buf"]
    P.barrier()
    bf.mark(); f32.mark()
    rw = bf.alloc("rwT", 8 * 8)
    L = f32.alloc("L", 8)
    Bi = f32.alloc("Bi", 8)
    sel = f32.alloc("sel", 8)
    tmp = f32.alloc("tmp", 8)
    cols = f32.alloc("cols", 8)
    wsem = P.new_sem("rw")
    P.dma("gpsimd", rw.rearrange("p (a d) -> p a d", a=8),
          x_ap["rwT"].rearrange("(a p) d -> p a d", p=128), wsem)
    w_ready = wsem.val
    lg_free = None
    for j in range(TPC):
        for et in range(8):
            w = [(wsem, w_ready)] if (j == 0 and et == 0) else []
            if et == 0 and lg_free is not None:
                w.append(lg_free)
            P.emit("tensor", lambda e, s=xnT[:, et * 2048 + j * 128: et * 2048 + j * 128 + 128],
                   g=rw[:, et * 8:(et + 1) * 8], o=et:
                   e.matmul(psum["b2"][:, 0:8], s, g, start=(o == 0), stop=(o == 7)),
                   waits=w, incs=([(pe, 1)] if et == 7 else []), chain=False)
        pe.val += 1
        P.emit("vector", lambda e: e.tensor_copy(L, psum["b2"][:, 0:8]), waits=[(pe, pe.val)])
        lg_free = (vch, vch.val)
        P.emit("vector", lambda e: e.tensor_copy(Bi, L))
        P.emit("vector", lambda e: e.memset(sel, 0.0))
        for r in range(K1):
            P.emit("vector", lambda e: e.reduce_max(cols[:, 0:1], Bi, axis=AX.X))
            P.emit("vector", lambda e: e.tensor_scalar(tmp, Bi, cols[:, 0:1], None, ALU.is_ge))
            P.emit("vector", lambda e: e.tensor_tensor(sel, sel, tmp, ALU.add))
            P.emit("vector", lambda e: e.scalar_tensor_tensor(Bi, tmp, -1e30, Bi, ALU.mult, ALU.add))
        P.emit("vector", lambda e: e.tensor_scalar(tmp, sel, 1e9, -1e9, ALU.mult, ALU.add))
        P.emit("vector", lambda e: e.tensor_tensor(tmp, L, tmp, ALU.add))
        P.emit("vector", lambda e: e.reduce_max(cols[:, 1:2], tmp, axis=AX.X))
        P.emit("vector", lambda e: e.tensor_scalar(cols[:, 2:3], cols[:, 1:2], -1.0, None, ALU.mult))
        P.emit("scalar", lambda e: e.activation(Bi, tmp, AF.Exp, bias=cols[:, 2:3],
               accum_out=cols[:, 3:4]), waits=[(vch, vch.val)])
        P.emit("vector", lambda e: e.reciprocal(cols[:, 4:5], cols[:, 3:4]),
               waits=[(sch, sch.val)])
        P.emit("vector", lambda e, d=cw_buf[:, j * 8:(j + 1) * 8]:
               e.tensor_scalar(d, Bi, cols[:, 4:5], None, ALU.mult))
    bf.release(); f32.release()
    return dict()



def phase_lru_scan(P, x_ap, B, bf, f32, psum, cc_in, cc_out):
    vch, sch = P.chain["vector"], P.chain["scalar"]
    pe = P.sem_pe
    ident = B["ident"]
    y_loc, Cown, Ptab, oml = B["y_loc"], B["Cown"], B["Ptab"], B["oml"]
    P.barrier()
    bf.mark(); f32.mark()
    win = bf.alloc("winT", 8 * 1024)
    lamc = f32.alloc("lamc", 8)
    pw = f32.alloc("pw", 8)
    onest = f32.alloc("onest", 128)
    lamrep = f32.alloc("lamrep", 8 * 128)
    Lrep = f32.alloc("Lrep", 8 * 128)
    g1c = f32.alloc("g1c", 8)
    b1c = f32.alloc("b1c", 8)
    selC = bf.alloc("selC", 16)
    Fown = f32.alloc("Fown", 128)
    Fg = f32.alloc("Fg", 128)
    Fscan = f32.alloc("Fscan", 128)
    Cinc = f32.alloc("Cinc", 8 * 128)
    Cb = bf.alloc("Cb", 128)
    CT = bf.alloc("CT", 128)
    xt = [f32.alloc("lx0", 1024), f32.alloc("lx1", 1024)]
    xn1t = f32.alloc("xn1t", 1024)
    scr = f32.alloc("scr", 1024)
    st2 = f32.alloc("st2", 8)
    xnb = [bf.alloc("lxb0", 1024), bf.alloc("lxb1", 1024)]
    uT = bf.alloc("uT", 1024)
    wsem = P.new_sem("lruw")
    P.dma("gpsimd", win.rearrange("p (a d) -> p a d", a=8),
          x_ap["winT"].rearrange("(a p) d -> p a d", p=128), wsem)
    P.dma("sync", lamc, x_ap["loglam_c"], wsem)
    P.dma("sync", g1c, x_ap["ln1g_c"], wsem)
    P.dma("sync", b1c, x_ap["ln1b_c"], wsem)
    P.dma("sync", selC, x_ap["selC"], wsem)
    w_ready = wsem.val
    P.emit("scalar", lambda e: e.activation(lamc, lamc, AF.Sigmoid), waits=[(wsem, w_ready)])
    P.emit("vector", lambda e: e.memset(onest, 1.0), waits=[(sch, sch.val)])
    P.emit("vector", lambda e: e.tensor_scalar(oml, lamc, -1.0, 1.0, ALU.mult, ALU.add))
    P.emit("vector", lambda e: e.tensor_copy(pw, lamc))
    for s in range(8):
        P.emit("vector", lambda e, d=lamrep[:, s * 128:(s + 1) * 128], c=lamc[:, s:s + 1]:
               e.tensor_scalar(d, onest, c, None, ALU.mult))
        P.emit("vector", lambda e, d=Ptab[:, s * 128: s * 128 + 1], c=lamc[:, s:s + 1]:
               e.tensor_scalar(d, onest[:, 0:1], c, None, ALU.mult))
    for i in range(7):
        n = 1 << i
        for s in range(8):
            P.emit("vector", lambda e, d=Ptab[:, s * 128 + n: s * 128 + 2 * n],
                   src=Ptab[:, s * 128: s * 128 + n], c=pw[:, s:s + 1]:
                   e.tensor_scalar(d, src, c, None, ALU.mult))
        P.emit("vector", lambda e: e.tensor_tensor(pw, pw, pw, ALU.mult))
    for s in range(8):
        P.emit("vector", lambda e, d=Lrep[:, s * 128:(s + 1) * 128], c=pw[:, s:s + 1]:
               e.tensor_scalar(d, onest, c, None, ALU.mult))
    tbl_v = vch.val
    xsem = [P.new_sem("lrx0"), P.new_sem("lrx1")]
    pT = [psum["bt0"], psum["bt1"]]
    pU = [psum["b2"], psum["b3"]]
    xt_free = [0, 0]
    xnb_free = [0, 0]
    tev = [None, None]
    uev = [None, None]
    for j in range(TPC):
        k = j % 2
        P.dma("sync", xt[k], x_ap["x1s"][j * 128:(j + 1) * 128, :], xsem[k],
              waits=[(vch, xt_free[k])])
        ln_tile(P, xt[k], xn1t, st2, scr, waits=[(xsem[k], xsem[k].val)])
        nw = ln_tile(P, xn1t, xnb[k], st2, scr,
                     extra_norm_waits=[(pe, xnb_free[k])])
        xt_free[k] = vch.val
        dsts = [uT[:, et * 128:(et + 1) * 128] for et in range(8)]
        ev = transpose8(P, xnb[k], dsts, ident, pT, tev, (vch, nw), gb=(g1c, b1c))
        xnb_free[k] = pe.val
        for s in range(8):
            bank = pU[s % 2]
            for dt in range(8):
                w = list(ev) + [(wsem, w_ready)] if (dt == 0 and s == 0) else []
                if dt == 0 and s > 0:
                    w = list(ev)
                if dt == 0 and uev[s % 2] is not None:
                    w.append(uev[s % 2])
                P.emit("tensor", lambda e, ww=win[:, dt * 1024 + s * 128: dt * 1024 + s * 128 + 128],
                       r=dsts[dt], b=bank, o=dt:
                       e.matmul(b[:, 0:128], ww, r, start=(o == 0), stop=(o == 7)),
                       waits=w, incs=([(pe, 1)] if dt == 7 else []), chain=False)
            pe.val += 1
            P.emit("vector", lambda e, b=bank, d=y_loc[:, s * 2048 + j * 128: s * 2048 + j * 128 + 128],
                   lr=lamrep[:, s * 128:(s + 1) * 128]:
                   e.tensor_tensor_scan(d, lr, b[:, 0:128], 0.0, ALU.mult, ALU.add),
                   waits=[(pe, pe.val), (vch, tbl_v if j == 0 else 0)])
            uev[s % 2] = (vch, vch.val)
    for s in range(8):
        P.emit("vector", lambda e, d=Fown[:, s * 16:(s + 1) * 16],
               i=y_loc[:, s * 2048 + 127: (s + 1) * 2048: 128]:
               e.tensor_copy(d, i))
    csem = P.new_sem("cc")
    P.dma("gpsimd", cc_in, Fown, csem, waits=[(vch, vch.val)])
    P.emit("gpsimd", lambda e: e.collective_compute(
        "AllGather", ALU.bypass, replica_groups=[list(range(NCORES))],
        ins=[cc_in.opt()], outs=[cc_out.opt()]),
        waits=[(csem, csem.val)], incs=[(csem, 1)], chain=False)
    csem.val += 1
    cc_done = csem.val
    fsem = P.new_sem("fb")
    fg_free = 0
    cb_free = 0
    for s in range(8):
        P.dma("sync", Fg.rearrange("p (r j) -> p r j", r=8),
              cc_out[:, s * 16:(s + 1) * 16].rearrange("(r p) j -> p r j", p=128),
              fsem, waits=[(csem, cc_done), (vch, fg_free)])
        P.emit("vector", lambda e, o=Fscan.rearrange("p (j r) -> p j r", r=8),
               i=Fg.rearrange("p (r j) -> p j r", j=16):
               e.tensor_copy(o, i), waits=[(fsem, fsem.val)])
        fg_free = vch.val
        P.emit("vector", lambda e, d=Cinc[:, s * 128:(s + 1) * 128],
               lr=Lrep[:, s * 128:(s + 1) * 128]:
               e.tensor_tensor_scan(d, lr, Fscan, 0.0, ALU.mult, ALU.add))
        P.emit("vector", lambda e, i=Cinc[:, s * 128:(s + 1) * 128]: e.tensor_copy(Cb, i),
               waits=[(pe, cb_free)])
        cb_v = vch.val
        P.emit("tensor", lambda e: e.transpose(pT[0][:, 0:128], Cb, ident),
               waits=[(vch, cb_v)], incs=[(pe, 1)], chain=False)
        pe.val += 1
        cb_free = pe.val
        P.emit("vector", lambda e: e.tensor_copy(CT, pT[0][:, 0:128]), waits=[(pe, pe.val)])
        ct_v = vch.val
        P.emit("tensor", lambda e: e.matmul(psum["b4"][:, 0:16], CT, selC, start=True, stop=True),
               waits=[(vch, ct_v)], incs=[(pe, 1)], chain=False)
        pe.val += 1
        P.emit("vector", lambda e, d=Cown[:, s * 16:(s + 1) * 16]:
               e.tensor_copy(d, psum["b4"][:, 0:16]), waits=[(pe, pe.val)])
    bf.release(); f32.release()
    return dict()



def phase_lru_out(P, x_ap, B, bf, f32, psum, x2_out):
    vch, sch = P.chain["vector"], P.chain["scalar"]
    pe = P.sem_pe
    y_loc, Cown, Ptab, oml = B["y_loc"], B["Cown"], B["Ptab"], B["oml"]
    P.barrier()
    bf.mark(); f32.mark()
    wout = bf.alloc("woutT", 8 * 1024)
    rows = f32.alloc("rows", 4 * 1024)
    gD = f32.alloc("gD", 1024)
    bD = f32.alloc("bD", 1024)
    onerow = f32.alloc("onerow", 128)
    yf = bf.alloc("yf", 1024)
    xt = [f32.alloc("ox0", 1024), f32.alloc("ox1", 1024)]
    xn1t = f32.alloc("oxn", 1024)
    nt = f32.alloc("ont", 1024)
    t3 = f32.alloc("ot3", 1024)
    x2t = [f32.alloc("ox20", 1024), f32.alloc("ox21", 1024)]
    scr = f32.alloc("scr", 1024)
    st2 = f32.alloc("st2", 8)
    wsem = P.new_sem("low")
    P.dma("gpsimd", wout.rearrange("p (a d) -> p a d", a=8),
          x_ap["woutT"].rearrange("(a p) d -> p a d", p=128), wsem)
    P.dma("sync", rows[0:1, 0:1024], x_ap["dskip_r"], wsem)
    P.dma("sync", rows[0:1, 1024:2048], x_ap["ln1g_r"], wsem)
    P.dma("sync", rows[0:1, 2048:3072], x_ap["ln1b_r"], wsem)
    w_ready = wsem.val
    for s in range(8):
        P.emit("vector", lambda e, d=wout[:, s * 1024:(s + 1) * 1024], c=oml[:, s:s + 1]:
               e.tensor_scalar(d, d, c, None, ALU.mult),
               waits=([(wsem, w_ready)] if s == 0 else []))
    wsc_v = vch.val
    P.emit("vector", lambda e: e.memset(onerow[0:1, :], 1.0))
    P.emit("vector", lambda e: e.tensor_tensor(rows[0:1, 3072:4096], rows[0:1, 1024:2048],
           rows[0:1, 0:1024], ALU.mult))
    P.emit("vector", lambda e: e.tensor_tensor(rows[0:1, 0:1024], rows[0:1, 2048:3072],
           rows[0:1, 0:1024], ALU.mult))
    rows_v = vch.val
    bprev = None
    for half, (src, dst) in enumerate([(rows[0:1, 3072:4096], gD), (rows[0:1, 0:1024], bD)]):
        for n in (0, 1):
            bw = [(vch, rows_v)] + ([bprev] if bprev else [])
            P.emit("tensor", lambda e, s=src[:, n * 512:(n + 1) * 512],
                   b=psum["b6"]: e.matmul(b[:, 0:512], onerow[0:1, :], s, start=True, stop=True),
                   waits=bw, incs=[(pe, 1)], chain=False)
            pe.val += 1
            P.emit("vector", lambda e, d=dst[:, n * 512:(n + 1) * 512], b=psum["b6"]:
                   e.tensor_copy(d, b[:, 0:512]), waits=[(pe, pe.val)])
            bprev = (vch, vch.val)
    bc_v = vch.val
    xsem = [P.new_sem("lox0"), P.new_sem("lox1")]
    osem = P.new_sem("x2o")
    pO = [psum["b2"], psum["b3"]]
    xt_free = [0, 0]
    x2_free = [0, 0]
    pO_free = [None, None]
    yf_free = 0
    for j in range(TPC):
        k = j % 2
        P.dma("sync", xt[k], x_ap["x1s"][j * 128:(j + 1) * 128, :], xsem[k],
              waits=[(vch, xt_free[k])])
        for s in range(8):
            w = [(vch, bc_v)] if (j == 0 and s == 0) else []
            if s == 0 and yf_free:
                w.append((pe, yf_free))
            P.emit("vector", lambda e, d=yf[:, s * 128:(s + 1) * 128],
                   p=Ptab[:, s * 128:(s + 1) * 128], c=Cown[:, s * 16 + j: s * 16 + j + 1],
                   z=y_loc[:, s * 2048 + j * 128: s * 2048 + j * 128 + 128]:
                   e.scalar_tensor_tensor(d, p, c, z, ALU.mult, ALU.add), waits=w)
        yf_v = vch.val
        for n in (0, 1):
            bank = pO[n]
            for s in range(8):
                w = []
                if s == 0:
                    w = [(vch, yf_v)] + ([(vch, wsc_v)] if j == 0 else [])
                    if pO_free[n] is not None:
                        w.append(pO_free[n])
                P.emit("tensor", lambda e, yy=yf[:, s * 128:(s + 1) * 128],
                       ww=wout[:, s * 1024 + n * 512: s * 1024 + n * 512 + 512], b=bank, o=s:
                       e.matmul(b[:, 0:512], yy, ww, start=(o == 0), stop=(o == 7)),
                       waits=w, incs=([(pe, 1)] if s == 7 else []), chain=False)
            pe.val += 1
        ln_tile(P, xt[k], xn1t, st2, scr, waits=[(xsem[k], xsem[k].val)])
        ln_tile(P, xn1t, nt, st2, scr)
        P.emit("vector", lambda e: e.tensor_tensor(t3, nt, gD, ALU.mult))
        P.emit("vector", lambda e: e.tensor_tensor(t3, t3, bD, ALU.add))
        P.emit("vector", lambda e: e.tensor_tensor(t3, t3, xn1t, ALU.add),
               waits=[(osem, x2_free[k])])
        for n in (0, 1):
            P.emit("vector", lambda e, d=x2t[k][:, n * 512:(n + 1) * 512],
                   s=t3[:, n * 512:(n + 1) * 512], b=pO[n]:
                   e.tensor_tensor(d, s, b[:, 0:512], ALU.add),
                   waits=[(pe, pe.val)])
            pO_free[n] = (vch, vch.val)
        xt_free[k] = vch.val
        yf_free = pe.val
        P.dma("sync", x2_out[j * 128:(j + 1) * 128, :], x2t[k], osem,
              waits=[(vch, vch.val)])
        x2_free[k] = osem.val
    P.emit("sync", lambda e: e.wait_ge(osem.h, osem.val), chain=False)
    bf.release(); f32.release()
    return dict()



def phase_mlp(P, x_ap, B, bf, f32, psum, x2_in, s1_out):
    vch, sch = P.chain["vector"], P.chain["scalar"]
    pe = P.sem_pe
    ident = B["ident"]
    P.barrier()
    bf.mark(); f32.mark()
    m1 = bf.alloc("m1T", 8 * 2048)
    m2 = bf.alloc("m2T", 16 * 1024)
    g2c = f32.alloc("g2c", 8)
    b2c_ = f32.alloc("b2c_", 8)
    b1c = f32.alloc("b1c", 16)
    b2row = f32.alloc("b2row", 1024)
    b2bc = f32.alloc("b2bc", 1024)
    onerow = f32.alloc("onerow", 128)
    hT = bf.alloc("hT", 16 * 512)
    x2nT = bf.alloc("x2nT", 8 * 512)
    xt = [f32.alloc("mx0", 4 * 1024)]
    x2pb = [f32.alloc("mpb0", 4 * 1024)]
    xnb = bf.alloc("mxb", 1024)
    s1t = [f32.alloc("ms0", 4 * 1024)]
    scr = f32.alloc("scr", 1024)
    st2 = f32.alloc("st2", 8)
    wsem = P.new_sem("mw")
    P.dma("gpsimd", m1.rearrange("p (a d) -> p a d", a=8),
          x_ap["m1T"].rearrange("(a p) d -> p a d", p=128), wsem)
    P.dma("gpsimd", m2.rearrange("p (a d) -> p a d", a=16),
          x_ap["m2T"].rearrange("(a p) d -> p a d", p=128), wsem)
    P.dma("sync", g2c, x_ap["ln2g_c"], wsem)
    P.dma("sync", b2c_, x_ap["ln2b_c"], wsem)
    P.dma("sync", b1c, x_ap["b1_c"], wsem)
    P.dma("sync", b2row[0:1, :], x_ap["b2_r"], wsem)
    w_ready = wsem.val
    P.emit("vector", lambda e: e.memset(onerow[0:1, :], 1.0))
    ones_v = vch.val
    bprev = None
    for n in (0, 1):
        bw = [(vch, ones_v), (wsem, w_ready)] + ([bprev] if bprev else [])
        P.emit("tensor", lambda e, s=b2row[0:1, n * 512:(n + 1) * 512]:
               e.matmul(psum["b6"][:, 0:512], onerow[0:1, :], s, start=True, stop=True),
               waits=bw, incs=[(pe, 1)], chain=False)
        pe.val += 1
        P.emit("vector", lambda e, d=b2bc[:, n * 512:(n + 1) * 512]:
               e.tensor_copy(d, psum["b6"][:, 0:512]), waits=[(pe, pe.val)])
        bprev = (vch, vch.val)
    bc_v = vch.val
    xsem = [P.new_sem("mx0s")]
    osem = P.new_sem("ms1o")
    pT = [psum["bt0"], psum["bt1"]]
    pH = [psum["b2"], psum["b3"]]
    pS = [psum["b4"], psum["b5"]]
    xt_free = [0]
    s1_free = [0]
    tev = [None, None]
    pH_free = [None, None]
    pS_free = [None, None]
    NCH = TPC // 4
    for ch in range(NCH):
        k = 0
        P.dma("sync", xt[k], x2_in[ch * 512:(ch + 1) * 512, :], xsem[k],
              waits=[(vch, xt_free[k])])
        for t4 in range(4):
            xtile = xt[k][:, t4 * 1024:(t4 + 1) * 1024]
            nw = ln_tile(P, xtile, xnb, st2, scr,
                         waits=([(xsem[k], xsem[k].val)] if t4 == 0 else []))
            P.emit("vector", lambda e, d=x2pb[k][:, t4 * 1024:(t4 + 1) * 1024], i=xtile:
                   e.tensor_tensor(d, i, b2bc, ALU.add),
                   waits=[(vch, bc_v)])
            dsts = [x2nT[:, et * 512 + t4 * 128: et * 512 + t4 * 128 + 128] for et in range(8)]
            transpose8(P, xnb, dsts, ident, pT, tev, (vch, nw), gb=(g2c, b2c_))
        ev_v = vch.val
        for ft in range(16):
            bank = pH[ft % 2]
            for dt in range(8):
                w = []
                if dt == 0:
                    w = [(vch, ev_v)]
                    if pH_free[ft % 2] is not None:
                        w.append(pH_free[ft % 2])
                P.emit("tensor", lambda e, ww=m1[:, dt * 2048 + ft * 128: dt * 2048 + ft * 128 + 128],
                       r=x2nT[:, dt * 512:(dt + 1) * 512], b=bank, o=dt:
                       e.matmul(b[:, 0:512], ww, r, start=(o == 0), stop=(o == 7)),
                       waits=w, incs=([(pe, 1)] if dt == 7 else []), chain=False)
            pe.val += 1
            P.emit("scalar", lambda e, b=bank, d=hT[:, ft * 512:(ft + 1) * 512],
                   bc=b1c[:, ft:ft + 1]:
                   e.activation(d, b[:, 0:512], AF.Relu, bias=bc),
                   waits=[(pe, pe.val)])
            pH_free[ft % 2] = (sch, sch.val)
        h_v = sch.val
        for t4 in range(4):
            for n in (0, 1):
                bank = pS[n]
                for ft in range(16):
                    w = []
                    if ft == 0:
                        w = [(sch, h_v)]
                        if pS_free[n] is not None:
                            w.append(pS_free[n])
                    P.emit("tensor", lambda e, s=hT[:, ft * 512 + t4 * 128: ft * 512 + t4 * 128 + 128],
                           ww=m2[:, ft * 1024 + n * 512: ft * 1024 + n * 512 + 512], b=bank, o=ft:
                           e.matmul(b[:, 0:512], s, ww, start=(o == 0), stop=(o == 15)),
                           waits=w, incs=([(pe, 1)] if ft == 15 else []), chain=False)
                pe.val += 1
                P.emit("vector", lambda e, b=bank,
                       d=s1t[k][:, t4 * 1024 + n * 512: t4 * 1024 + n * 512 + 512],
                       p=x2pb[k][:, t4 * 1024 + n * 512: t4 * 1024 + n * 512 + 512]:
                       e.tensor_tensor(d, b[:, 0:512], p, ALU.add),
                       waits=[(pe, pe.val), (osem, s1_free[k])])
                pS_free[n] = (vch, vch.val)
        xt_free[k] = vch.val
        P.dma("sync", s1_out[ch * 512:(ch + 1) * 512, :], s1t[k], osem,
              waits=[(vch, vch.val)])
        s1_free[k] = osem.val
    P.emit("sync", lambda e: e.wait_ge(osem.h, osem.val), chain=False)
    bf.release(); f32.release()
    return dict()



def phase_l1_ffn(P, x_ap, B, bf, f32, psum):
    vch, sch = P.chain["vector"], P.chain["scalar"]
    pe = P.sem_pe
    xnT, cw_buf, y_acc = B["xn1T"], B["cw_buf"], B["y_acc"]
    P.barrier()
    bf.mark(); f32.mark()
    wg = bf.alloc("wg", 8 * 1024)
    wu = bf.alloc("wu", 8 * 1024)
    wd = bf.alloc("wd", 8 * 1024)
    hh = bf.alloc("hh1", 8 * 2048)
    sil = [f32.alloc("sil0", 512), f32.alloc("sil1", 512)]
    P.emit("vector", lambda e: e.memset(y_acc[:, 0:8192], 0.0))
    P.emit("vector", lambda e: e.memset(y_acc[:, 8192:16384], 0.0))
    y_zeroed = vch.val
    wsem = P.new_sem("f1w")
    pu = [(psum["b2"], psum["b3"]), (psum["b4"], psum["b5"])]
    pd = [psum["b6"], psum["b7"]]
    pu_free = [None, None]
    pd_free = [None, None]
    hh_free = 0
    w_free = 0
    sil_free = [0, 0]
    first = True
    for e0 in range(E1):
        for sl in range(4):
            dw = [(P.sem_pe, w_free)] if w_free else []
            P.dma("gpsimd", wg.rearrange("p (a d) -> p a d", a=8),
                  x_ap["wg"][e0][:, sl * 1024:(sl + 1) * 1024].rearrange("(a p) d -> p a d", p=128),
                  wsem, waits=dw)
            P.dma("gpsimd", wu.rearrange("p (a d) -> p a d", a=8),
                  x_ap["wu"][e0][:, sl * 1024:(sl + 1) * 1024].rearrange("(a p) d -> p a d", p=128),
                  wsem)
            P.dma("gpsimd", wd.rearrange("p (a d) -> p a d", a=8),
                  x_ap["wd"][e0][sl * 1024:(sl + 1) * 1024, :].rearrange("(a p) d -> p a d", p=128),
                  wsem)
            w_up = wsem.val - 16
            w_dn = wsem.val
            ctr = 0
            for isub in range(8):
                for tc in range(4):
                    b1, b3 = pu[ctr % 2]
                    w = [(wsem, w_up)] if (isub == 0 and tc == 0) else []
                    if pu_free[ctr % 2] is not None:
                        w.append(pu_free[ctr % 2])
                    if isub == 0 and tc == 0 and hh_free:
                        w.append((pe, hh_free))
                    for et in range(8):
                        P.emit("tensor", lambda e, ww=wg[:, et * 1024 + isub * 128: et * 1024 + isub * 128 + 128],
                               r=xnT[:, et * 2048 + tc * 512: et * 2048 + tc * 512 + 512],
                               b=b1, o=et:
                               e.matmul(b[:, 0:512], ww, r, start=(o == 0), stop=(o == 7)),
                               waits=(w if et == 0 else []), chain=False)
                    for et in range(8):
                        P.emit("tensor", lambda e, ww=wu[:, et * 1024 + isub * 128: et * 1024 + isub * 128 + 128],
                               r=xnT[:, et * 2048 + tc * 512: et * 2048 + tc * 512 + 512],
                               b=b3, o=et:
                               e.matmul(b[:, 0:512], ww, r, start=(o == 0), stop=(o == 7)),
                               incs=([(pe, 1)] if et == 7 else []), chain=False)
                    pe.val += 1
                    P.emit("scalar", lambda e, b=b1, o=sil[ctr % 2]:
                           e.activation(o, b[:, 0:512], AF.Silu),
                           waits=[(pe, pe.val), (vch, sil_free[ctr % 2])])
                    P.emit("vector", lambda e, b=b3, s=sil[ctr % 2],
                           d=hh[:, isub * 2048 + tc * 512: isub * 2048 + tc * 512 + 512]:
                           e.tensor_tensor(d, s, b[:, 0:512], ALU.mult),
                           waits=[(sch, sch.val)])
                    pu_free[ctr % 2] = (vch, vch.val)
                    sil_free[ctr % 2] = vch.val
                    ctr += 1
            hh_v = vch.val
            for j in range(TPC):
                for n in (0, 1):
                    bank = pd[n]
                    for isub in range(8):
                        w = []
                        if isub == 0:
                            w = [(vch, hh_v), (wsem, w_dn)]
                            if pd_free[n] is not None:
                                w.append(pd_free[n])
                        P.emit("tensor", lambda e, s=hh[:, isub * 2048 + j * 128: isub * 2048 + j * 128 + 128],
                               ww=wd[:, isub * 1024 + n * 512: isub * 1024 + n * 512 + 512],
                               b=bank, o=isub:
                               e.matmul(b[:, 0:512], s, ww, start=(o == 0), stop=(o == 7)),
                               waits=w, incs=([(pe, 1)] if isub == 7 else []), chain=False)
                    pe.val += 1
                    yw = [(pe, pe.val)]
                    if first:
                        yw.append((vch, y_zeroed))
                    ysl = y_acc[:, j * 1024 + n * 512: j * 1024 + n * 512 + 512]
                    P.emit("vector", lambda e, b=bank, d=ysl,
                           gc=cw_buf[:, j * 8 + e0: j * 8 + e0 + 1]:
                           e.scalar_tensor_tensor(d, b[:, 0:512], gc, d, ALU.mult, ALU.add),
                           waits=yw)
                    pd_free[n] = (vch, vch.val)
            hh_free = pe.val
            w_free = pe.val
            first = False
    bf.release(); f32.release()
    return dict()



def _build_A():
    nc = bass.Bass("TRN2", target_bir_lowering=False)
    io = {}
    for name, shape in (("x", (T, D)), ("WkT", (D, HD)), ("WvT", (D, HD)),
                        ("WqT", (D, D)), ("WoT", (D, D)), ("identf", (128, 128)),
                        ("pek_c", (HD, 1)), ("peqT", (128, TOK)),
                        ("x_own", (TOK, D)), ("mask8", (128, 8 * 512))):
        io[name] = dram_in(nc, name, shape, F32)
    io["ident"] = dram_in(nc, "ident", (128, 128), BF16)
    s_out = dram_out(nc, "s_out", (TOK, D), F32)
    with ExitStack() as stack:
        P, bf, f32, psum = make_ctx(stack, nc)
        B = dict(ident=bf.alloc("ident", 128), KT=bf.alloc("KT", NT * 128),
                 Vb=bf.alloc("Vb", NT * 128), xnT=bf.alloc("xnT", 8 * TOK),
                 Qb=bf.alloc("Qb", TPC * 1024), attn=bf.alloc("attn", TPC * 1024))
        phase_l0_kv(P, io, B, bf, f32, psum)
        phase_l0_own(P, io, B, bf, f32, psum)
        phase_attn(P, io, B, bf, f32, psum)
        phase_wo(P, io, B, bf, f32, psum, s_out)
        with nc.Block() as block:
            P.replay(block)
    return nc


def _build_B():
    nc = bass.Bass("TRN2", target_bir_lowering=False)
    io = {}
    for name, shape in (("x_own", (TOK, D)), ("s_in", (TOK, D)),
                        ("gwT", (D, E0)), ("ebias", (1, E0)),
                        ("sw1T", (D, D)), ("sw2T", (D, D)), ("sw3T", (D, D))):
        io[name] = dram_in(nc, name, shape, F32)
    for name in ("w1T", "w2T", "w3T"):
        io[name] = dram_in(nc, name, (E0, D, D), F32)
    io["ident"] = dram_in(nc, "ident", (128, 128), BF16)
    x1_out = dram_out(nc, "x1", (TOK, D), F32)
    with ExitStack() as stack:
        P, bf, f32, psum = make_ctx(stack, nc)
        B = dict(ident=bf.alloc("ident", 128), xnT=bf.alloc("xnT", 8 * TOK),
                 g_buf=f32.alloc("g_buf", TPC * 16),
                 y_acc=f32.alloc("y_acc", TPC * 1024))
        P.dma("sync", B["ident"], io["ident"], P.new_sem("idl"))
        phase_l0_own(P, io, B, bf, f32, psum, compute_q=False)
        phase_l0_gate(P, io, B, bf, f32, psum)
        phase_l0_ffn(P, io, B, bf, f32, psum)
        phase_combine(P, io, B, bf, f32, psum, io["s_in"], x1_out, xkey="x_own")
        with nc.Block() as block:
            P.replay(block)
    return nc


def _build_C():
    nc = bass.Bass("TRN2", target_bir_lowering=False)
    io = {}
    for name, shape in (("x1s", (TOK, D)), ("rwT", (D, E1)), ("winT", (D, D)),
                        ("woutT", (D, D)), ("m1T", (D, MLPF)), ("m2T", (MLPF, D)),
                        ("loglam_c", (128, 8)), ("ln1g_c", (128, 8)), ("ln1b_c", (128, 8)),
                        ("ln2g_c", (128, 8)), ("ln2b_c", (128, 8)),
                        ("dskip_r", (1, D)), ("ln1g_r", (1, D)), ("ln1b_r", (1, D)),
                        ("b1_c", (128, 16)), ("b2_r", (1, D))):
        io[name] = dram_in(nc, name, shape, F32)
    for name in ("wg", "wu"):
        io[name] = dram_in(nc, name, (E1, D, FF1), F32)
    io["wd"] = dram_in(nc, "wd", (E1, FF1, D), F32)
    io["ident"] = dram_in(nc, "ident", (128, 128), BF16)
    io["selC"] = dram_in(nc, "selC", (128, TPC), BF16)
    out_ap = dram_out(nc, "out", (TOK, D), F32)
    x2s = nc.dram_tensor("x2s", [TOK, D], F32).ap()
    s1s = nc.dram_tensor("s1s", [TOK, D], F32).ap()
    cc_in = nc.dram_tensor("cc_in", [128, 128], F32).ap()
    cc_out = nc.dram_tensor("cc_out", [1024, 128], F32).ap()
    with ExitStack() as stack:
        P, bf, f32, psum = make_ctx(stack, nc)
        B = dict(ident=bf.alloc("ident", 128), xn1T=bf.alloc("xn1T", 8 * TOK),
                 cw_buf=f32.alloc("cw_buf", TPC * 8),
                 Cown=f32.alloc("Cown", 8 * TPC),
                 Ptab=f32.alloc("Ptab", 8 * 128),
                 oml=f32.alloc("oml", 8))
        P.dma("sync", B["ident"], io["ident"], P.new_sem("idl"))
        phase_l0_own(P, io, B, bf, f32, psum, compute_q=False, xkey="x1s", tkey="xn1T")
        phase_l1_router(P, io, B, bf, f32, psum)
        bf.mark()
        B["y_loc"] = bf.alloc("y_loc", 8 * TOK)
        phase_lru_scan(P, io, B, bf, f32, psum, cc_in, cc_out)
        phase_lru_out(P, io, B, bf, f32, psum, x2s)
        bf.release()
        phase_mlp(P, io, B, bf, f32, psum, x2s, s1s)
        B["y_acc"] = f32.alloc("y_acc", TPC * 1024)
        phase_l1_ffn(P, io, B, bf, f32, psum)
        phase_combine(P, io, B, bf, f32, psum, s1s, out_ap, xkey="x1s")
        with nc.Block() as block:
            P.replay(block)
    return nc


def kernel_3launch(**inputs):
    sh, per = host_prepare(inputs)
    own_rows = [np.concatenate([np.arange(t * 128, (t + 1) * 128) for t in own_tiles(c)])
                for c in range(NCORES)]
    cores = list(range(NCORES))
    A = lambda v: np.ascontiguousarray(v)

    nc = _build_A()
    maps = []
    for c in cores:
        maps.append({"x": sh["x"], "WkT": sh["WkT"], "WvT": sh["WvT"],
                     "WqT": sh["WqT"], "WoT": sh["WoT"], "ident": sh["ident"],
                     "identf": sh["identf"], "pek_c": sh["pek_c"],
                     "peqT": per[c]["peqT"], "x_own": per[c]["x_own"],
                     "mask8": per[c]["mask8"]})
    ra = run_bass_kernel_spmd(nc, maps, core_ids=cores)
    s_parts = [r["s_out"] for r in ra.results]

    nc = _build_B()
    maps = []
    for c in cores:
        maps.append({"x_own": per[c]["x_own"], "s_in": A(s_parts[c]),
                     "gwT": sh["gwT"], "ebias": sh["ebias"],
                     "w1T": sh["w1T"], "w2T": sh["w2T"], "w3T": sh["w3T"],
                     "sw1T": sh["sw1T"], "sw2T": sh["sw2T"], "sw3T": sh["sw3T"],
                     "ident": sh["ident"]})
    rb = run_bass_kernel_spmd(nc, maps, core_ids=cores)
    x1_parts = [r["x1"] for r in rb.results]

    nc = _build_C()
    maps = []
    for c in cores:
        maps.append({"x1s": A(x1_parts[c]), "rwT": sh["rwT"], "winT": sh["winT"],
                     "woutT": sh["woutT"], "m1T": sh["m1T"], "m2T": sh["m2T"],
                     "loglam_c": sh["loglam_c"], "ln1g_c": sh["ln1g_c"],
                     "ln1b_c": sh["ln1b_c"], "ln2g_c": sh["ln2g_c"],
                     "ln2b_c": sh["ln2b_c"], "dskip_r": sh["dskip_r"],
                     "ln1g_r": sh["ln1g_r"], "ln1b_r": sh["ln1b_r"],
                     "b1_c": sh["b1_c"], "b2_r": sh["b2_r"],
                     "wg": sh["wg"], "wu": sh["wu"], "wd": sh["wd"],
                     "ident": sh["ident"], "selC": per[c]["selC"]})
    rc = run_bass_kernel_spmd(nc, maps, core_ids=cores)

    out = np.zeros((T, D), dtype=np.float32)
    for c in cores:
        out[own_rows[c]] = rc.results[c]["out"]
    return out.reshape(1, T, D)



def _build_fused():
    nc = bass.Bass("TRN2", target_bir_lowering=False)
    io = {}
    for name, shape in (("x", (T, D)), ("WkT", (D, HD)), ("WvT", (D, HD)),
                        ("WqT", (D, D)), ("WoT", (D, D)), ("identf", (128, 128)),
                        ("pek_c", (HD, 1)), ("peqT", (128, TOK)),
                        ("x_own", (TOK, D)), ("mask8", (128, 8 * 512)),
                        ("gwT", (D, E0)), ("ebias", (1, E0)),
                        ("sw1T", (D, D)), ("sw2T", (D, D)), ("sw3T", (D, D)),
                        ("rwT", (D, E1)), ("winT", (D, D)), ("woutT", (D, D)),
                        ("m1T", (D, MLPF)), ("m2T", (MLPF, D)),
                        ("loglam_c", (128, 8)), ("ln1g_c", (128, 8)), ("ln1b_c", (128, 8)),
                        ("ln2g_c", (128, 8)), ("ln2b_c", (128, 8)),
                        ("dskip_r", (1, D)), ("ln1g_r", (1, D)), ("ln1b_r", (1, D)),
                        ("b1_c", (128, 16)), ("b2_r", (1, D))):
        io[name] = dram_in(nc, name, shape, F32)
    for name in ("w1T", "w2T", "w3T"):
        io[name] = dram_in(nc, name, (E0, D, D), F32)
    for name in ("wg", "wu"):
        io[name] = dram_in(nc, name, (E1, D, FF1), F32)
    io["wd"] = dram_in(nc, "wd", (E1, FF1, D), F32)
    io["ident"] = dram_in(nc, "ident", (128, 128), BF16)
    io["selC"] = dram_in(nc, "selC", (128, TPC), BF16)
    out_ap = dram_out(nc, "out", (TOK, D), F32)
    s_sc = nc.dram_tensor("s_sc", [TOK, D], F32).ap()
    x1_sc = nc.dram_tensor("x1_sc", [TOK, D], F32).ap()
    x2_sc = nc.dram_tensor("x2_sc", [TOK, D], F32).ap()
    s1_sc = nc.dram_tensor("s1_sc", [TOK, D], F32).ap()
    cc_in = nc.dram_tensor("cc_in", [128, 128], F32).ap()
    cc_out = nc.dram_tensor("cc_out", [1024, 128], F32).ap()
    io["x1s"] = x1_sc
    with ExitStack() as stack:
        P, bf, f32, psum = make_ctx(stack, nc)
        B = dict(ident=bf.alloc("ident", 128), xnT=bf.alloc("xnT", 8 * TOK))
        bf.mark()
        B["KT"] = bf.alloc("KT", NT * 128)
        B["Vb"] = bf.alloc("Vb", NT * 128)
        B["attn"] = bf.alloc("attn", TPC * 1024)
        phase_l0_front(P, io, B, bf, f32, psum)
        phase_wo(P, io, B, bf, f32, psum, s_sc)
        bf.release()
        bf.mark()
        B["g_buf"] = f32.alloc("g_buf", TPC * 16)
        B["y_acc"] = f32.alloc("y_acc", TPC * 1024)
        phase_l0_gate(P, io, B, bf, f32, psum)
        phase_moe2(P, io, B, bf, f32, psum, moe_specs_l0(io), "xnT")
        phase_combine(P, io, B, bf, f32, psum, s_sc, x1_sc, xkey="x_own")
        bf.release()
        B["xn1T"] = B["xnT"]
        phase_l0_own(P, io, B, bf, f32, psum, compute_q=False, xkey="x1s", tkey="xn1T")
        B["cw_buf"] = f32.alloc("cw_buf", TPC * 8)
        B["Cown"] = f32.alloc("Cown", 8 * TPC)
        B["Ptab"] = f32.alloc("Ptab", 8 * 128)
        B["oml"] = f32.alloc("oml", 8)
        phase_l1_router(P, io, B, bf, f32, psum)
        bf.mark()
        B["y_loc"] = bf.alloc("y_loc", 8 * TOK)
        phase_lru_scan(P, io, B, bf, f32, psum, cc_in, cc_out)
        phase_lru_out(P, io, B, bf, f32, psum, x2_sc)
        bf.release()
        phase_mlp(P, io, B, bf, f32, psum, x2_sc, s1_sc)
        bf.mark()
        B["y_acc"] = f32.alloc("y_acc", TPC * 1024)
        phase_moe2(P, io, B, bf, f32, psum, moe_specs_l1(io), "xn1T")
        phase_combine(P, io, B, bf, f32, psum, s1_sc, out_ap, xkey="x1s")
        bf.release()
        with nc.Block() as block:
            P.replay(block)
    return nc


def kernel(**inputs):
    sh, per = host_prepare(inputs)
    own_rows = [np.concatenate([np.arange(t * 128, (t + 1) * 128) for t in own_tiles(c)])
                for c in range(NCORES)]
    cores = list(range(NCORES))
    nc = _build_fused()
    shared_keys = ("x", "WkT", "WvT", "WqT", "WoT", "ident", "identf", "pek_c",
                   "gwT", "ebias", "w1T", "w2T", "w3T", "sw1T", "sw2T", "sw3T",
                   "rwT", "winT", "woutT", "m1T", "m2T", "loglam_c",
                   "ln1g_c", "ln1b_c", "ln2g_c", "ln2b_c", "dskip_r",
                   "ln1g_r", "ln1b_r", "b1_c", "b2_r", "wg", "wu", "wd")
    maps = []
    for c in cores:
        m = {k: sh[k] for k in shared_keys}
        m["peqT"] = per[c]["peqT"]
        m["x_own"] = per[c]["x_own"]
        m["mask8"] = per[c]["mask8"]
        m["selC"] = per[c]["selC"]
        maps.append(m)
    res = run_bass_kernel_spmd(nc, maps, core_ids=cores)
    out = np.zeros((T, D), dtype=np.float32)
    for c in cores:
        out[own_rows[c]] = res.results[c]["out"]
    return out.reshape(1, T, D)


def kernel_unfused(**inputs):
    sh, per = host_prepare(inputs)
    own_rows = [np.concatenate([np.arange(t * 128, (t + 1) * 128) for t in own_tiles(c)])
                for c in range(NCORES)]
    raise NotImplementedError



def phase_moe2(P, x_ap, B, bf, f32, psum, specs, xnT_key):
    vch, sch, gch = P.chain["vector"], P.chain["scalar"], P.chain["gpsimd"]
    pe = P.sem_pe
    xnT, y_acc = B[xnT_key], B["y_acc"]
    P.barrier()
    bf.mark(); f32.mark()
    wgb = [bf.alloc("wg0", 8 * 512), bf.alloc("wg1", 8 * 512)]
    wub = [bf.alloc("wu0", 8 * 512), bf.alloc("wu1", 8 * 512)]
    wdb = [bf.alloc("wd0", 4 * 1024), bf.alloc("wd1", 4 * 1024)]
    hhb = [bf.alloc("hh0", 4 * 2048), bf.alloc("hh1", 4 * 2048)]
    sil = [f32.alloc("sil0", 512), f32.alloc("sil1", 512)]
    P.emit("vector", lambda e: e.memset(y_acc[:, 0:8192], 0.0))
    P.emit("vector", lambda e: e.memset(y_acc[:, 8192:16384], 0.0))
    y_zeroed = vch.val
    wsem = [P.new_sem("mw0"), P.new_sem("mw1")]
    pu = [(psum["b2"], psum["b3"]), (psum["b4"], psum["b5"])]
    pd = [psum["b6"], psum["b7"]]
    pu_free = [None, None]
    pd_free = [None, None]
    dn_done = [0, 0]
    up_done = [0, 0]
    sil_free = [0, 0]
    nsl = len(specs)
    w_up_v = {}
    w_dn_v = {}

    def issue_gu(s):
        k = s % 2
        gcol, sg, su, sd = specs[s]
        dw = [(pe, up_done[k])] if up_done[k] else []
        P.dma("gpsimd", wgb[k].rearrange("p (a d) -> p a d", a=8),
              sg.rearrange("(a p) d -> p a d", p=128), wsem[k], waits=dw)
        P.dma("gpsimd", wub[k].rearrange("p (a d) -> p a d", a=8),
              su.rearrange("(a p) d -> p a d", p=128), wsem[k])
        w_up_v[s] = wsem[k].val

    def issue_d(s):
        k = s % 2
        gcol, sg, su, sd = specs[s]
        dw = [(pe, dn_done[k])] if dn_done[k] else []
        P.dma("gpsimd", wdb[k].rearrange("p (a d) -> p a d", a=4),
              sd.rearrange("(a p) d -> p a d", p=128), wsem[k], waits=dw)
        w_dn_v[s] = wsem[k].val

    issue_gu(0); issue_d(0)
    issue_gu(1); issue_d(1)
    for s in range(nsl):
        k = s % 2
        gcol = specs[s][0]
        w_up = w_up_v[s]
        w_dn = w_dn_v[s]
        ctr = 0
        for isub in range(4):
            for tc in range(4):
                b1, b3 = pu[ctr % 2]
                w = [(wsem[k], w_up)] if (isub == 0 and tc == 0) else []
                if pu_free[ctr % 2] is not None:
                    w.append(pu_free[ctr % 2])
                for et in range(8):
                    P.emit("tensor", lambda e, ww=wgb[k][:, et * 512 + isub * 128: et * 512 + isub * 128 + 128],
                           r=xnT[:, et * 2048 + tc * 512: et * 2048 + tc * 512 + 512],
                           b=b1, o=et:
                           e.matmul(b[:, 0:512], ww, r, start=(o == 0), stop=(o == 7)),
                           waits=(w if et == 0 else []), chain=False)
                for et in range(8):
                    P.emit("tensor", lambda e, ww=wub[k][:, et * 512 + isub * 128: et * 512 + isub * 128 + 128],
                           r=xnT[:, et * 2048 + tc * 512: et * 2048 + tc * 512 + 512],
                           b=b3, o=et:
                           e.matmul(b[:, 0:512], ww, r, start=(o == 0), stop=(o == 7)),
                           incs=([(pe, 1)] if et == 7 else []), chain=False)
                pe.val += 1
                P.emit("scalar", lambda e, b=b1, o=sil[ctr % 2]:
                       e.activation(o, b[:, 0:512], AF.Silu),
                       waits=[(pe, pe.val), (vch, sil_free[ctr % 2])])
                P.emit("vector", lambda e, b=b3, si=sil[ctr % 2],
                       d=hhb[k][:, isub * 2048 + tc * 512: isub * 2048 + tc * 512 + 512]:
                       e.tensor_tensor(d, si, b[:, 0:512], ALU.mult),
                       waits=[(sch, sch.val)])
                pu_free[ctr % 2] = (vch, vch.val)
                sil_free[ctr % 2] = vch.val
                ctr += 1
        hh_v = vch.val
        up_done[k] = pe.val
        if s + 2 < nsl:
            issue_gu(s + 2)
        for j in range(TPC):
            for n in (0, 1):
                bank = pd[n]
                for isub in range(4):
                    w = []
                    if isub == 0:
                        w = [(vch, hh_v)]
                        if j == 0 and n == 0:
                            w.append((wsem[k], w_dn))
                        if pd_free[n] is not None:
                            w.append(pd_free[n])
                    P.emit("tensor", lambda e, st=hhb[k][:, isub * 2048 + j * 128: isub * 2048 + j * 128 + 128],
                           ww=wdb[k][:, isub * 1024 + n * 512: isub * 1024 + n * 512 + 512],
                           b=bank, o=isub:
                           e.matmul(b[:, 0:512], st, ww, start=(o == 0), stop=(o == 3)),
                           waits=w, incs=([(pe, 1)] if isub == 3 else []), chain=False)
                pe.val += 1
                yw = [(pe, pe.val)]
                if s == 0:
                    yw.append((vch, y_zeroed))
                ysl = y_acc[:, j * 1024 + n * 512: j * 1024 + n * 512 + 512]
                if gcol is None:
                    P.emit("vector", lambda e, b=bank, d=ysl:
                           e.tensor_tensor(d, b[:, 0:512], d, ALU.add), waits=yw)
                else:
                    col = j * gcol[1] + gcol[2]
                    gc = B[gcol[0]][:, col:col + 1]
                    P.emit("vector", lambda e, b=bank, d=ysl, g=gc:
                           e.scalar_tensor_tensor(d, b[:, 0:512], g, d, ALU.mult, ALU.add),
                           waits=yw)
                pd_free[n] = (vch, vch.val)
        dn_done[k] = pe.val
        if s + 2 < nsl:
            issue_d(s + 2)
    bf.release(); f32.release()
    return dict()


def moe_specs_l0(x_ap, j_of=None):
    specs = []
    for e0 in range(E0):
        for sl in range(2):
            specs.append((("g_buf", 16, e0),
                          x_ap["w1T"][e0][:, sl * 512:(sl + 1) * 512],
                          x_ap["w3T"][e0][:, sl * 512:(sl + 1) * 512],
                          x_ap["w2T"][e0][sl * 512:(sl + 1) * 512, :]))
    for sl in range(2):
        specs.append((None,
                      x_ap["sw1T"][:, sl * 512:(sl + 1) * 512],
                      x_ap["sw3T"][:, sl * 512:(sl + 1) * 512],
                      x_ap["sw2T"][sl * 512:(sl + 1) * 512, :]))
    return specs


def moe_specs_l1(x_ap):
    specs = []
    for e0 in range(E1):
        for sl in range(8):
            specs.append((("cw_buf", 8, e0),
                          x_ap["wg"][e0][:, sl * 512:(sl + 1) * 512],
                          x_ap["wu"][e0][:, sl * 512:(sl + 1) * 512],
                          x_ap["wd"][e0][sl * 512:(sl + 1) * 512, :]))
    return specs



def phase_l0_front(P, x_ap, B, bf, f32, psum):
    vch, sch = P.chain["vector"], P.chain["scalar"]
    pe = P.sem_pe
    ident = B["ident"]
    KT, Vb, xnT, attn = B["KT"], B["Vb"], B["xnT"], B["attn"]
    P.barrier()
    bf.mark(); f32.mark()
    Qb = bf.alloc("Qb2", 2 * 1024)
    qslot_free = [0, 0]
    wk = bf.alloc("WkT", 8 * 128)
    wv = bf.alloc("WvT", 8 * 128)
    wq = bf.alloc("WqT", 8 * 1024)
    pek = f32.alloc("pek", 8)
    peq = bf.alloc("peqT", TOK)
    xt = [f32.alloc("xt0", 1024), f32.alloc("xt1", 1024)]
    scr = f32.alloc("scr", 1024)
    st = f32.alloc("stats", 8)
    xnb = [bf.alloc("xnb0", 1024), bf.alloc("xnb1", 1024)]
    xnt_t = [bf.alloc("xnt_t0", 1024)]
    wsem = P.new_sem("fw")
    P.dma("gpsimd", wk.rearrange("p (a d) -> p a d", a=8),
          x_ap["WkT"].rearrange("(a p) d -> p a d", p=128), wsem)
    P.dma("gpsimd", wv.rearrange("p (a d) -> p a d", a=8),
          x_ap["WvT"].rearrange("(a p) d -> p a d", p=128), wsem)
    P.dma("gpsimd", wq.rearrange("p (a d) -> p a d", a=8),
          x_ap["WqT"].rearrange("(a p) d -> p a d", p=128), wsem)
    P.dma("sync", pek[:, 0:1], x_ap["pek_c"], wsem)
    P.dma("gpsimd", peq, x_ap["peqT"], wsem)
    P.dma("sync", ident, x_ap["ident"], wsem)
    w_ready = wsem.val
    xsem = [P.new_sem("fx0"), P.new_sem("fx1")]
    osem = P.new_sem("fox")
    pT = [psum["bt0"], psum["bt1"]]
    pK, pV = psum["b2"], psum["b3"]
    pQ = [psum["b2"], psum["b3"]]
    xt_free = [0, 0]
    xnb_free = [0, 0]
    mask8 = bf.alloc("mask8", 8 * 512)
    stash = f32.alloc("stash", 8 * 512)
    probsW = [bf.alloc("prW0", 1024), bf.alloc("prW1", 1024)]
    probs = [[probsW[0][:, 0:512], probsW[0][:, 512:1024]],
             [probsW[1][:, 0:512], probsW[1][:, 512:1024]]]
    mcol = [f32.alloc("mcol0", 8), f32.alloc("mcol1", 8)]
    onescol = bf.alloc("onescol", 8)
    onerow = f32.alloc("onerow", 128)
    msc = f32.alloc("msc", 8)
    identf = f32.alloc("identf", 128)
    pacc = [f32.alloc("pacc0", 512), f32.alloc("pacc1", 512)]
    rb = stash[:, 2048:3072]
    pbf = [probs[0][0], probs[1][0]]
    P.dma("gpsimd", mask8, x_ap["mask8"], wsem)
    P.dma("sync", identf, x_ap["identf"], wsem)
    am_ready = wsem.val
    P.emit("vector", lambda e: e.memset(onescol, 1.0))
    P.emit("vector", lambda e: e.memset(onerow[0:1, :], 1.0))
    ones_ready = vch.val
    scb = [psum["b0"], psum["b1"], psum["b2"], psum["b3"]]
    scW = [psum["d0"], psum["d1"]]
    pa = [psum["b4"], psum["b5"]]
    pl = [psum["b6"], psum["b7"]]
    bfree = [None, None, None, None]
    plfree = [None, None]
    pa_free = [None, None]
    probs_free = [[0, 0], [0, 0]]
    probs_accv = [[0, 0], [0, 0]]

    for j in range(TPC):
        for p8 in range(8):
            i = 8 * j + p8
            k = i % 2
            P.dma("sync", xt[k], x_ap["x"][i * 128:(i + 1) * 128, :], xsem[k],
                  waits=[(vch, xt_free[k])])
            nw = ln_tile(P, xt[k], xnb[k], st, scr,
                         waits=[(xsem[k], xsem[k].val)],
                         extra_norm_waits=[(pe, xnb_free[k])])
            xt_free[k] = vch.val
            dsts = [xnt_t[0][:, et * 128:(et + 1) * 128] for et in range(8)]
            tev = [bfree[0], bfree[1]]
            ev = transpose8(P, xnb[k], dsts, ident, pT, tev, (vch, nw),
                            extra_first=([(wsem, w_ready)] if i == 0 else []))
            bfree[0], bfree[1] = tev
            xnb_free[k] = pe.val
            for et in range(8):
                kw = list(ev) if et == 0 else []
                if et == 0 and bfree[2] is not None:
                    kw.append(bfree[2])
                P.emit("tensor", lambda e, w=wk[:, et * 128:(et + 1) * 128], r=dsts[et], o=et:
                       e.matmul(pK[:, 0:128], w, r, start=(o == 0), stop=(o == 7)),
                       waits=kw, incs=([(pe, 1)] if et == 7 else []), chain=False)
            pe.val += 1
            P.emit("scalar", lambda e, d=KT[:, i * 128:(i + 1) * 128]:
                   e.add(d, pK[:, 0:128], pek[:, 0:1]),
                   waits=[(pe, pe.val)])
            bfree[2] = (sch, sch.val)
            for et in range(8):
                vw = [bfree[3]] if (et == 0 and bfree[3] is not None) else []
                P.emit("tensor", lambda e, s=dsts[et], w=wv[:, et * 128:(et + 1) * 128], o=et:
                       e.matmul(pV[:, 0:128], s, w, start=(o == 0), stop=(o == 7)),
                       waits=vw, incs=([(pe, 1)] if et == 7 else []), chain=False)
            pe.val += 1
            P.emit("vector", lambda e, d=Vb[:, i * 128:(i + 1) * 128]: e.tensor_copy(d, pV[:, 0:128]),
                   waits=[(pe, pe.val)])
            bfree[3] = (vch, vch.val)
        P.dma("sync", xt[0], x_ap["x_own"][j * 128:(j + 1) * 128, :], osem,
              waits=[(vch, xt_free[0])])
        nw = ln_tile(P, xt[0], xnb[0], st, scr,
                     waits=[(osem, osem.val)],
                     extra_norm_waits=[(pe, xnb_free[0])])
        xt_free[0] = vch.val
        odsts = [xnT[:, et * 2048 + j * 128: et * 2048 + j * 128 + 128] for et in range(8)]
        tev = [bfree[0], bfree[1]]
        oev = transpose8(P, xnb[0], odsts, ident, pT, tev, (vch, nw))
        bfree[0], bfree[1] = tev
        xnb_free[0] = pe.val
        for h in range(NH):
            bank = pQ[h % 2]
            for et in range(8):
                qw = list(oev) if et == 0 else []
                if et == 0 and bfree[2 + h % 2] is not None:
                    qw.append(bfree[2 + h % 2])
                P.emit("tensor", lambda e, w=wq[:, et * 1024 + h * 128: et * 1024 + h * 128 + 128],
                       r=odsts[et], o=et, b=bank:
                       e.matmul(b[:, 0:128], w, r, start=(o == 0), stop=(o == 7)),
                       waits=qw, incs=([(pe, 1)] if et == 7 else []), chain=False)
            pe.val += 1
            P.emit("vector", lambda e, b=bank,
                   d=Qb[:, (j % 2) * 1024 + h * 128: (j % 2) * 1024 + h * 128 + 128],
                   p=peq[:, j * 128:(j + 1) * 128]:
                   e.tensor_tensor(d, b[:, 0:128], p, ALU.add),
                   waits=[(pe, pe.val), (pe, qslot_free[j % 2])])
            bfree[2 + h % 2] = (vch, vch.val)
        q_ready = vch.val
        kv_s = sch.val
        kv_v = vch.val
        nmain = 8 * j
        for hg in (0, 1):
            qsl = Qb[:, (j % 2) * 1024 + hg * 512: (j % 2) * 1024 + (hg + 1) * 512]
            for p in range(8):
                bank = scb[p % 4]
                w = [(sch, kv_s), (vch, q_ready)]
                if j == 0 and hg == 0 and p < 4:
                    w.append((wsem, am_ready))
                if bfree[p % 4] is not None:
                    w.append(bfree[p % 4])
                P.emit("tensor", lambda e, kt=KT[:, (8 * j + p) * 128:(8 * j + p + 1) * 128],
                       q=qsl, b=bank: e.matmul(b[:, 0:512], kt, q, start=True, stop=True),
                       waits=w, incs=[(pe, 1)], chain=False)
                pe.val += 1
                P.emit("vector", lambda e, b=bank, o=stash[:, p * 512:(p + 1) * 512],
                       mk=mask8[:, p * 512:(p + 1) * 512]:
                       e.tensor_tensor(o, b[:, 0:512], mk, ALU.add),
                       waits=[(pe, pe.val)])
                bfree[p % 4] = (vch, vch.val)
            P.emit("vector", lambda e: e.reduce_max(msc[:, 0:1], stash, axis=AX.X))
            mx_v = vch.val
            tw = [(vch, mx_v)]
            if j == 0 and hg == 0:
                tw.append((wsem, am_ready))
            if plfree[0] is not None:
                tw.append(plfree[0])
            P.emit("tensor", lambda e: e.transpose(pl[0][0:1, 0:128], msc[:, 0:1], identf),
                   waits=tw, incs=[(pe, 1)], chain=False)
            pe.val += 1
            P.emit("vector", lambda e: e.reduce_max(msc[0:1, 1:2], pl[0][0:1, 0:128], axis=AX.X),
                   waits=[(pe, pe.val)])
            P.emit("vector", lambda e: e.tensor_scalar(
                msc[0:1, 3:4], msc[0:1, 1:2], -SCALE, None, ALU.mult))
            mneg_v = vch.val
            P.emit("tensor", lambda e: e.matmul(pl[0][:, 33:34], onerow[0:1, :],
                   msc[0:1, 3:4], start=True, stop=True),
                   waits=[(vch, mneg_v), (vch, ones_ready)], incs=[(pe, 1)], chain=False)
            pe.val += 1
            P.emit("vector", lambda e, d=mcol[hg][:, 0:1]: e.tensor_copy(d, pl[0][:, 33:34]),
                   waits=[(pe, pe.val)])
            plfree[0] = (vch, vch.val)
            mc_ready = vch.val
            for p in range(8):
                pb = probs[hg][p % 2]
                if p % 2 == 0:
                    P.emit("scalar", lambda e, o=probsW[hg], i=stash[:, p * 512:(p + 2) * 512],
                           m=mcol[hg][:, 0:1]:
                           e.activation(o, i, AF.Exp, bias=m, scale=SCALE),
                           waits=[(vch, mc_ready), (pe, probs_free[hg][0]),
                                  (pe, probs_free[hg][1]),
                                  (vch, probs_accv[hg][0]), (vch, probs_accv[hg][1])])
                first = (p == 0)
                last = (j == 0 and p == 7)
                if hg == 0:
                    if first:
                        P.emit("vector", lambda e, a=pacc[0], pbuf=pb: e.tensor_copy(a, pbuf),
                               waits=[(sch, sch.val)])
                    else:
                        P.emit("vector", lambda e, a=pacc[0], pbuf=pb:
                               e.tensor_tensor(a, a, pbuf, ALU.add),
                               waits=[(sch, sch.val)])
                    acc_v = vch.val
                else:
                    lw = [(sch, sch.val)]
                    if first and plfree[1] is not None:
                        lw.append(plfree[1])
                    P.emit("tensor", lambda e, o=pl[1][0:1, 0:512], pbuf=pb, f=first, l=last:
                           e.matmul(o, onescol[:, 0:1], pbuf, start=f, stop=l),
                           waits=lw, chain=False)
                    acc_v = 0
                pw = [(sch, sch.val), (vch, kv_v)]
                if first and pa_free[hg] is not None:
                    pw.append(pa_free[hg])
                P.emit("tensor", lambda e, v=Vb[:, (8 * j + p) * 128:(8 * j + p + 1) * 128],
                       pbuf=pb, b=pa[hg], f=first, l=last:
                       e.matmul(b[:, 0:512], v, pbuf, start=f, stop=l),
                       waits=pw, incs=[(pe, 1)], chain=False)
                pe.val += 1
                probs_free[hg][p % 2] = pe.val
                probs_accv[hg][p % 2] = acc_v
        exp_done = {}
        for kq in range(0, nmain + 2, 2):
            if kq < nmain:
                for hg in (0, 1):
                    qsl = Qb[:, (j % 2) * 1024 + hg * 512: (j % 2) * 1024 + (hg + 1) * 512]
                    for t in (0, 1):
                        kb = kq + t
                        bi = 2 * hg + t
                        bank = scb[bi]
                        w = [bfree[bi]] if bfree[bi] is not None else []
                        P.emit("tensor", lambda e, kt=KT[:, kb * 128:(kb + 1) * 128], q=qsl, b=bank:
                               e.matmul(b[:, 0:512], kt, q, start=True, stop=True),
                               waits=w, incs=([(pe, 1)] if t == 1 else []), chain=False)
                    pe.val += 1
                    P.emit("scalar", lambda e, b=scW[hg], o=probsW[hg], m=mcol[hg][:, 0:1]:
                           e.activation(o, b[:, 0:1024], AF.Exp, bias=m, scale=SCALE),
                           waits=[(pe, pe.val), (pe, probs_free[hg][0]),
                                  (pe, probs_free[hg][1]),
                                  (vch, probs_accv[hg][0]), (vch, probs_accv[hg][1])])
                    bfree[2 * hg] = (sch, sch.val)
                    bfree[2 * hg + 1] = (sch, sch.val)
                    exp_done[(kq, hg)] = (sch, sch.val)
            if kq > 0:
                kp0 = kq - 2
                for hg in (0, 1):
                    if hg == 0:
                        P.emit("vector", lambda e, a=pacc[0], pbuf=probs[0][0]:
                               e.tensor_tensor(a, a, pbuf, ALU.add),
                               waits=[exp_done[(kp0, 0)]])
                        P.emit("vector", lambda e, a=pacc[0], pbuf=probs[0][1]:
                               e.tensor_tensor(a, a, pbuf, ALU.add))
                        probs_accv[0][0] = vch.val
                        probs_accv[0][1] = vch.val
                    else:
                        for t in (0, 1):
                            P.emit("tensor", lambda e, o=pl[1][0:1, 0:512], pbuf=probs[1][t],
                                   l=(kp0 + t == nmain - 1):
                                   e.matmul(o, onescol[:, 0:1], pbuf, start=False, stop=l),
                                   waits=([exp_done[(kp0, 1)]] if t == 0 else []), chain=False)
                    for t in (0, 1):
                        kp = kp0 + t
                        last = (kp == nmain - 1)
                        P.emit("tensor", lambda e, v=Vb[:, kp * 128:(kp + 1) * 128],
                               pbuf=probs[hg][t], b=pa[hg], l=last:
                               e.matmul(b[:, 0:512], v, pbuf, start=False, stop=l),
                               waits=([exp_done[(kp0, hg)]] if t == 0 else []),
                               incs=([(pe, 1)] if t == 1 else []), chain=False)
                    pe.val += 1
                    probs_free[hg][0] = pe.val
                    probs_free[hg][1] = pe.val
        P.emit("vector", lambda e, d=pbf[0], a=pacc[0]: e.tensor_copy(d, a),
               waits=[(pe, probs_free[0][0])])
        cast_v = vch.val
        lw = [(vch, cast_v), (vch, ones_ready)]
        if plfree[0] is not None:
            lw.append(plfree[0])
        P.emit("tensor", lambda e, o=pl[0][0:1, 0:512], pbuf=pbf[0]:
               e.matmul(o, onescol[:, 0:1], pbuf, start=True, stop=True),
               waits=lw, incs=[(pe, 1)], chain=False)
        pe.val += 1
        probs_free[0][0] = pe.val
        P.emit("tensor", lambda e: e.nop(), incs=[(pe, 1)], chain=False)
        pe.val += 1
        l_done = pe.val
        P.emit("vector", lambda e: e.reciprocal(stash[0:1, 0:512], pl[0][0:1, 0:512]),
               waits=[(pe, l_done)])
        P.emit("vector", lambda e: e.reciprocal(stash[0:1, 512:1024], pl[1][0:1, 0:512]))
        rec_v = vch.val
        plfree[0] = (vch, rec_v)
        plfree[1] = (vch, rec_v)
        for hg in (0, 1):
            bw = [(vch, rec_v), plfree[hg]]
            P.emit("tensor", lambda e, r=stash[0:1, hg * 512:(hg + 1) * 512], b=pl[hg]:
                   e.matmul(b[:, 0:512], onerow[0:1, :], r, start=True, stop=True),
                   waits=bw, incs=[(pe, 1)], chain=False)
            pe.val += 1
            P.emit("scalar", lambda e, d=rb[:, hg * 512:(hg + 1) * 512], b=pl[hg]:
                   e.copy(d, b[:, 0:512]), waits=[(pe, pe.val)])
            plfree[hg] = (sch, sch.val)
            P.emit("vector", lambda e, b=pa[hg], r=rb[:, hg * 512:(hg + 1) * 512],
                   d=attn[:, j * 1024 + hg * 512: j * 1024 + (hg + 1) * 512]:
                   e.tensor_tensor(d, b[:, 0:512], r, ALU.mult),
                   waits=[(sch, sch.val)])
            pa_free[hg] = (vch, vch.val)
        qslot_free[j % 2] = pe.val
    bf.release(); f32.release()
    return dict()


# version 24 (final)
# speedup vs baseline: 1.0761x; 1.0761x over previous
import math
from contextlib import ExitStack

import numpy as np
import ml_dtypes
import concourse.bass as bass
import concourse.mybir as mybir
from concourse.bass_utils import run_bass_kernel_spmd

F32 = mybir.dt.float32
BF16 = mybir.dt.bfloat16
AF = mybir.ActivationFunctionType
ALU = mybir.AluOpType
AX = mybir.AxisListType

T = 16384
D = 1024
NT = T // 128
NCORES = 8
TPC = NT // NCORES
TOK = TPC * 128
NH = 8
HD = 128
SCALE = 1.0 / math.sqrt(HD)
E0 = 16
K0 = 4
E1 = 8
K1 = 2
FF1 = 4096
MLPF = 2048
LNEPS = 1e-5
MASKNEG = -30000.0

DK = D // 128


def own_tiles(c):
    return [c + NCORES * j for j in range(TPC)]



def _pe_table(length, dim):
    pos = np.arange(length, dtype=np.float32)[:, None]
    inv = np.exp(-np.log(np.float32(10000.0)) * (np.arange(0, dim, 2, dtype=np.float32) / dim))
    ang = pos * inv[None, :]
    pe = np.zeros((length, dim), dtype=np.float32)
    pe[:, 0::2] = np.sin(ang)
    pe[:, 1::2] = np.cos(ang)
    return pe


def host_prepare(inp):
    tt = lambda a: np.ascontiguousarray(np.swapaxes(a, -1, -2))
    sh = {}
    sh["x"] = np.ascontiguousarray(inp["L0_attn_x"].reshape(T, D))
    sh["WqT"] = tt(inp["L0_attn_Wq"])
    sh["WkT"] = tt(inp["L0_attn_Wk"])
    sh["WvT"] = tt(inp["L0_attn_Wv"])
    sh["WoT"] = tt(inp["L0_attn_Wo"])
    sh["gwT"] = tt(inp["L0_ffn_gate_w"])
    sh["ebias"] = inp["L0_ffn_expert_bias"].reshape(1, E0)
    sh["w1T"] = tt(inp["L0_ffn_w1"])
    sh["w2T"] = tt(inp["L0_ffn_w2"])
    sh["w3T"] = tt(inp["L0_ffn_w3"])
    sh["sw1T"] = tt(inp["L0_ffn_sw1"])
    sh["sw2T"] = tt(inp["L0_ffn_sw2"])
    sh["sw3T"] = tt(inp["L0_ffn_sw3"])
    sh["winT"] = tt(inp["L1_attn_w_in"])
    sh["woutT"] = tt(inp["L1_attn_w_out"])
    sh["m1T"] = tt(inp["L1_attn_w1"])
    sh["m2T"] = tt(inp["L1_attn_w2"])
    col = lambda v: np.ascontiguousarray(v.reshape(-1, 128).T)
    sh["loglam_c"] = col(inp["L1_attn_log_lambda"])
    sh["dskip_r"] = inp["L1_attn_D_skip"].reshape(1, D)
    sh["ln1g_c"] = col(inp["L1_attn_ln1_g"])
    sh["ln1b_c"] = col(inp["L1_attn_ln1_b"])
    sh["ln1g_r"] = inp["L1_attn_ln1_g"].reshape(1, D)
    sh["ln1b_r"] = inp["L1_attn_ln1_b"].reshape(1, D)
    sh["ln2g_c"] = col(inp["L1_attn_ln2_g"])
    sh["ln2b_c"] = col(inp["L1_attn_ln2_b"])
    sh["b1_c"] = col(inp["L1_attn_b1"])
    sh["b2_r"] = inp["L1_attn_b2"].reshape(1, D)
    sh["rwT"] = tt(inp["L1_ffn_router_w"])
    sh["wg"] = inp["L1_ffn_w_gate"]
    sh["wu"] = inp["L1_ffn_w_up"]
    sh["wd"] = inp["L1_ffn_w_down"]
    sh["ident"] = np.eye(128, dtype=ml_dtypes.bfloat16)
    sh["pek_c"] = _pe_table(1, HD).reshape(HD, 1)
    m = np.where(np.arange(128)[:, None] > np.arange(128)[None, :], MASKNEG, 0.0).astype(np.float32)
    sh["maskdiag"] = np.ascontiguousarray(np.tile(m, (1, 4)))
    sh["identf"] = np.eye(128, dtype=np.float32)
    per = []
    pe = _pe_table(T, HD)
    diag = np.tile(m, (1, 4))
    for c in range(NCORES):
        d = {}
        rows = np.concatenate([np.arange(t * 128, (t + 1) * 128) for t in own_tiles(c)])
        d["peqT"] = np.ascontiguousarray(pe[rows].T)
        d["x_own"] = np.ascontiguousarray(sh["x"][rows])
        mk = np.zeros((128, 8 * 512), dtype=np.float32)
        for p in range(8):
            if p == c:
                mk[:, p * 512:(p + 1) * 512] = diag
            elif p > c:
                mk[:, p * 512:(p + 1) * 512] = MASKNEG
        d["mask8"] = mk
        selc = np.zeros((128, TPC), dtype=ml_dtypes.bfloat16)
        for j in range(TPC):
            g = c + 8 * j - 1
            if g >= 0:
                selc[g, j] = 1.0
        d["selC"] = selc
        per.append(d)
    return sh, per



ENGS = ("sync", "tensor", "vector", "scalar", "gpsimd")


class Sem:
    __slots__ = ("h", "val")

    def __init__(self, h):
        self.h = h
        self.val = 0


class Prog:

    def __init__(self, nc, stack):
        self.nc = nc
        self.stack = stack
        self.ops = {e: [] for e in ENGS}
        self._semid = 0
        self.chain = {e: self.new_sem(f"chain_{e}") for e in ("vector", "scalar", "gpsimd")}
        self.nops = 0

    def new_sem(self, name=None):
        self._semid += 1
        nm = f"{name or 's'}_{self._semid}"
        h = self.stack.enter_context(self.nc.semaphore(nm))
        return Sem(h)

    def barrier(self):
        snap = [(self.chain["vector"], self.chain["vector"].val),
                (self.chain["scalar"], self.chain["scalar"].val),
                (self.chain["gpsimd"], self.chain["gpsimd"].val),
                (self.sem_pe, self.sem_pe.val)]
        self._barrier = {e: snap for e in ENGS}

    def emit(self, eng, fn, waits=(), incs=(), chain=True):
        pend = getattr(self, "_barrier", {}).pop(eng, None)
        waits = (list(pend) if pend else []) + list(waits)
        incs = list(incs)
        if eng in self.chain and chain:
            ch = self.chain[eng]
            waits.append((ch, ch.val))
            ch.val += 1
            incs.append((ch, 1))
        self.ops[eng].append((waits, fn, incs))
        self.nops += 1

    def dma(self, eng, out, in_, sem, waits=()):
        sem.val += 16
        self.emit(eng, lambda e, o=out, i=in_: e.dma_start(out=o, in_=i),
                  waits=waits, incs=[(sem, 16)], chain=False)
        return sem.val

    def start_phase(self):
        self.barrier()

    def replay(self, block):
        for eng in ENGS:
            ops = self.ops[eng]
            if not ops:
                continue

            def make(ops):
                def body(e):
                    for waits, fn, incs in ops:
                        for s, v in waits[:-1]:
                            e.wait_ge(s.h, v)
                        inst = fn(e)
                        if inst is None:
                            assert not waits and not incs
                            continue
                        if waits:
                            s, v = waits[-1]
                            if v > 0:
                                inst._wait_ge(s.h, v)
                        for s, v in incs:
                            inst.then_inc(s.h, v)
                return body
            getattr(block, eng)(make(ops))


class ArenaBase:

    def __init__(self, handle, width):
        self.h = handle
        self.width = width
        self.off = 0
        self.marks = []
        self.peak = 0

    def alloc_cols(self, name, bfcols, align=32):
        off = (self.off + align - 1) // align * align
        assert off + bfcols <= self.width, (
            f"arena overflow: {name} needs {bfcols} at {off}, width {self.width}")
        self.off = off + bfcols
        self.peak = max(self.peak, self.off)
        return off

    def mark(self):
        self.marks.append(self.off)

    def release(self):
        self.off = self.marks.pop()


class Arena:
    def __init__(self, base, dt):
        self.base = base
        self.dt = dt

    def alloc(self, name, cols, align=32):
        n = cols * (2 if self.dt == F32 else 1)
        off = self.base.alloc_cols(name, n, align)
        v = self.base.h[:, off:off + n]
        return v.bitcast(F32) if self.dt == F32 else v

    def mark(self):
        self.base.mark()

    def release(self):
        self.base.release()


def dram_in(nc, name, shape, dt=F32):
    return nc.dram_tensor(name, list(shape), dt, kind="ExternalInput").ap()


def dram_out(nc, name, shape, dt=F32):
    return nc.dram_tensor(name, list(shape), dt, kind="ExternalOutput").ap()




def ln_tile(P, src, xnb_out, st, scr, waits=(), extra_norm_waits=()):
    vch, sch = P.chain["vector"], P.chain["scalar"]
    P.emit("vector", lambda e, a=src: e.reduce_sum(st[:, 0:1], a, axis=AX.X), waits=list(waits))
    P.emit("vector", lambda e, a=src: e.scalar_tensor_tensor(
        scr, a, 1.0, a, ALU.mult, ALU.mult, accum_out=st[:, 1:2]))
    P.emit("vector", lambda e: e.tensor_scalar(st[:, 2:3], st[:, 0:1], 1.0 / D, None, ALU.mult))
    P.emit("vector", lambda e: e.tensor_tensor(st[:, 3:4], st[:, 2:3], st[:, 2:3], ALU.mult))
    P.emit("vector", lambda e: e.scalar_tensor_tensor(
        st[:, 4:5], st[:, 1:2], 1.0 / D, st[:, 3:4], ALU.mult, ALU.subtract))
    P.emit("scalar", lambda e: e.activation(st[:, 5:6], st[:, 4:5], AF.Ln, bias=P.epscol),
           waits=[(vch, vch.val)])
    P.emit("scalar", lambda e: e.activation(st[:, 6:7], st[:, 5:6], AF.Exp, scale=-0.5))
    P.emit("vector", lambda e, a=src, o=xnb_out: e.tensor_scalar(
        o, a, st[:, 2:3], st[:, 6:7], ALU.subtract, ALU.mult),
        waits=[(sch, sch.val)] + list(extra_norm_waits))
    return vch.val


def transpose8(P, xnb, dst_slices, ident, pT, tev, norm_wait, extra_first=(), gb=None):
    pe = P.sem_pe
    ev = []
    for et in range(8):
        bank = pT[et % 2]
        tw = [norm_wait] + (list(extra_first) if et == 0 else [])
        if tev[et % 2] is not None:
            tw.append(tev[et % 2])
        P.emit("tensor", lambda e, b=bank, s=xnb[:, et * 128:(et + 1) * 128], ii=ident:
               e.transpose(b[:, 0:128], s, ii), waits=tw, incs=[(P.sem_pe, 1)], chain=False)
        pe.val += 1
        if gb is None:
            eng = "vector" if et % 2 == 0 else "scalar"
            P.emit(eng, lambda e, b=bank, d=dst_slices[et]:
                   (e.tensor_copy if hasattr(e, "tensor_copy") else e.copy)(d, b[:, 0:128]),
                   waits=[(pe, pe.val)])
        else:
            eng = "vector"
            g, bb = gb
            P.emit(eng, lambda e, b=bank, d=dst_slices[et],
                   gc=g[:, et:et + 1], bc=bb[:, et:et + 1]:
                   e.tensor_scalar(d, b[:, 0:128], gc, bc, ALU.mult, ALU.add),
                   waits=[(pe, pe.val)])
        tev[et % 2] = (P.chain[eng], P.chain[eng].val)
        ev.append(tev[et % 2])
    return ev


def phase_l0_kv(P, x_ap, B, bf, f32, psum):
    vch, sch = P.chain["vector"], P.chain["scalar"]
    KT, Vb, ident = B["KT"], B["Vb"], B["ident"]
    P.barrier()
    bf.mark(); f32.mark()
    wk = bf.alloc("WkT", 8 * 128)
    wv = bf.alloc("WvT", 8 * 128)
    pek = f32.alloc("pek", 8)

    wsem = P.new_sem("l0w")
    P.dma("gpsimd", wk.rearrange("p (a d) -> p a d", a=8),
          x_ap["WkT"].rearrange("(a p) d -> p a d", p=128), wsem)
    P.dma("gpsimd", wv.rearrange("p (a d) -> p a d", a=8),
          x_ap["WvT"].rearrange("(a p) d -> p a d", p=128), wsem)
    P.dma("sync", ident, x_ap["ident"], wsem)
    P.dma("sync", pek[:, 0:1], x_ap["pek_c"], wsem)
    w_ready = wsem.val

    xt = [f32.alloc("xt0", 1024), f32.alloc("xt1", 1024)]
    scr = f32.alloc("scr", 1024)
    st = f32.alloc("stats", 8)
    xnb = [bf.alloc("xnb0", 1024), bf.alloc("xnb1", 1024)]
    xnt_t = [bf.alloc("xnt_t0", 1024), bf.alloc("xnt_t1", 1024)]
    xsem = [P.new_sem(f"l0x{k}") for k in range(2)]
    pT = [psum["bt0"], psum["bt1"]]
    pK, pV = psum["b2"], psum["b3"]
    pe = P.sem_pe

    xt_free = [0, 0]
    xnb_free = [0, 0]
    tev = [None, None]
    kev = vev = None
    for i in range(NT):
        k = i % 2
        P.dma("sync", xt[k], x_ap["x"][i * 128:(i + 1) * 128, :], xsem[k],
              waits=[(vch, xt_free[k])])
        nw = ln_tile(P, xt[k], xnb[k], st, scr,
                     waits=[(xsem[k], xsem[k].val)],
                     extra_norm_waits=[(pe, xnb_free[k])])
        xt_free[k] = vch.val
        dsts = [xnt_t[k][:, et * 128:(et + 1) * 128] for et in range(8)]
        ev = transpose8(P, xnb[k], dsts, ident, pT, tev, (vch, nw),
                        extra_first=([(wsem, w_ready)] if i == 0 else []))
        xnb_free[k] = pe.val
        for et in range(8):
            kw = list(ev) if et == 0 else []
            if et == 0 and kev is not None:
                kw.append(kev)
            P.emit("tensor", lambda e, w=wk[:, et * 128:(et + 1) * 128], r=dsts[et], o=et:
                   e.matmul(pK[:, 0:128], w, r, start=(o == 0), stop=(o == 7)),
                   waits=kw, incs=([(pe, 1)] if et == 7 else []), chain=False)
        pe.val += 1
        P.emit("scalar", lambda e, d=KT[:, i * 128:(i + 1) * 128]:
               e.add(d, pK[:, 0:128], pek[:, 0:1]),
               waits=[(pe, pe.val)])
        kev = (sch, sch.val)
        for et in range(8):
            vw = [vev] if (et == 0 and vev is not None) else []
            P.emit("tensor", lambda e, s=dsts[et], w=wv[:, et * 128:(et + 1) * 128], o=et:
                   e.matmul(pV[:, 0:128], s, w, start=(o == 0), stop=(o == 7)),
                   waits=vw, incs=([(pe, 1)] if et == 7 else []), chain=False)
        pe.val += 1
        P.emit("vector", lambda e, d=Vb[:, i * 128:(i + 1) * 128]: e.tensor_copy(d, pV[:, 0:128]),
               waits=[(pe, pe.val)])
        vev = (vch, vch.val)
    bf.release(); f32.release()
    return dict()


def phase_l0_own(P, x_ap, B, bf, f32, psum, compute_q=True, xkey="x_own", tkey="xnT"):
    vch, sch = P.chain["vector"], P.chain["scalar"]
    ident, xnT = B["ident"], B[tkey]
    Qb = B.get("Qb")
    P.barrier()
    bf.mark(); f32.mark()
    wsem = P.new_sem("l0qw")
    if compute_q:
        wq = bf.alloc("WqT", 8 * 1024)
        peq = f32.alloc("peqT", TOK)
        P.dma("gpsimd", wq.rearrange("p (a d) -> p a d", a=8),
              x_ap["WqT"].rearrange("(a p) d -> p a d", p=128), wsem)
        P.dma("sync", peq, x_ap["peqT"], wsem)
    w_ready = wsem.val

    xt = [f32.alloc("xt0", 1024), f32.alloc("xt1", 1024)]
    scr = f32.alloc("scr", 1024)
    st = f32.alloc("stats", 8)
    xnb = [bf.alloc("xnb0", 1024), bf.alloc("xnb1", 1024)]
    xsem = [P.new_sem(f"l0q{k}") for k in range(2)]
    pT = [psum["bt0"], psum["bt1"]]
    pQ = [psum["b4"], psum["b5"]]
    pe = P.sem_pe

    xt_free = [0, 0]
    xnb_free = [0, 0]
    tev = [None, None]
    qev = [None, None]
    for j in range(TPC):
        k = j % 2
        P.dma("sync", xt[k], x_ap[xkey][j * 128:(j + 1) * 128, :], xsem[k],
              waits=[(vch, xt_free[k])])
        nw = ln_tile(P, xt[k], xnb[k], st, scr,
                     waits=[(xsem[k], xsem[k].val)],
                     extra_norm_waits=[(pe, xnb_free[k])])
        xt_free[k] = vch.val
        dsts = [xnT[:, et * 2048 + j * 128: et * 2048 + j * 128 + 128] for et in range(8)]
        ev = transpose8(P, xnb[k], dsts, ident, pT, tev, (vch, nw),
                        extra_first=([(wsem, w_ready)] if j == 0 else []))
        xnb_free[k] = pe.val
        for h in (range(NH) if compute_q else ()):
            bank = pQ[h % 2]
            for et in range(8):
                qw = list(ev) if et == 0 else []
                if et == 0 and qev[h % 2] is not None:
                    qw.append(qev[h % 2])
                P.emit("tensor", lambda e, w=wq[:, et * 1024 + h * 128: et * 1024 + h * 128 + 128],
                       r=dsts[et], o=et, b=bank:
                       e.matmul(b[:, 0:128], w, r, start=(o == 0), stop=(o == 7)),
                       waits=qw, incs=([(pe, 1)] if et == 7 else []), chain=False)
            pe.val += 1
            P.emit("vector", lambda e, b=bank,
                   d=Qb[:, j * 1024 + h * 128: j * 1024 + h * 128 + 128],
                   p=peq[:, j * 128:(j + 1) * 128]:
                   e.tensor_tensor(d, b[:, 0:128], p, ALU.add),
                   waits=[(pe, pe.val)])
            qev[h % 2] = (P.chain["vector"], P.chain["vector"].val)
    bf.release(); f32.release()
    return dict()



ARENA_COLS = 106368


def make_ctx(stack, nc):
    P = Prog(nc, stack)
    P.sem_pe = P.new_sem("pe")
    bfh = stack.enter_context(nc.sbuf_tensor("arena", [128, ARENA_COLS], BF16))
    base = ArenaBase(bfh, ARENA_COLS)
    f32 = Arena(base, F32)
    bf = Arena(base, BF16)
    consts = f32.alloc("consts", 8)
    P.epscol = consts[:, 0:1]
    P.onecol = consts[:, 1:2]
    P.emit("vector", lambda e: e.memset(P.epscol, LNEPS))
    P.emit("vector", lambda e: e.memset(P.onecol, 1.0))
    psum = {}
    for nm in ("b0", "b1", "b2", "b3", "b4", "b5", "b6", "b7"):
        psum[nm] = stack.enter_context(nc.psum_tensor(nm, [128, 512], F32))
    psum["bt0"] = psum["b0"][:].bitcast(BF16)
    psum["bt1"] = psum["b1"][:].bitcast(BF16)
    return P, bf, f32, psum


def declare_inputs(nc, names_shapes):
    io = {}
    for name, arr in names_shapes.items():
        io[name] = dram_in(nc, name, arr.shape, F32 if arr.dtype == np.float32 else BF16)
    return io


def finish_debug_outputs(P, nc, outs):
    ssem = P.new_sem("dbgst")
    vch, sch = P.chain["vector"], P.chain["scalar"]
    for name, (ap, dt) in outs.items():
        o = dram_out(nc, name, [128, ap.shape[-1]] if len(ap.shape) == 2 else ap.shape, dt)
        P.dma("gpsimd", o, ap, ssem,
              waits=[(vch, vch.val), (sch, sch.val), (P.sem_pe, P.sem_pe.val)])
    P.emit("gpsimd", lambda e: e.wait_ge(ssem.h, ssem.val), chain=False)



def phase_attn(P, x_ap, B, bf, f32, psum):
    vch, sch = P.chain["vector"], P.chain["scalar"]
    pe = P.sem_pe
    KT, Vb, Qb = B["KT"], B["Vb"], B["Qb"]
    P.barrier()
    bf.mark(); f32.mark()

    attn = B["attn"]
    mask8 = f32.alloc("mask8", 8 * 512)
    stash = f32.alloc("stash", 8 * 512)
    probs = [[bf.alloc("pr00", 512), bf.alloc("pr01", 512)],
             [bf.alloc("pr10", 512), bf.alloc("pr11", 512)]]
    mcol = [f32.alloc("mcol0", 8), f32.alloc("mcol1", 8)]
    onescol = bf.alloc("onescol", 8)
    onerow = f32.alloc("onerow", 128)
    msc = f32.alloc("msc", 8)
    identf = f32.alloc("identf", 128)
    pacc = [f32.alloc("pacc0", 512), f32.alloc("pacc1", 512)]
    rb = stash[:, 2048:3072]
    pbf = [probs[0][0], probs[1][0]]

    wsem = P.new_sem("atw")
    P.dma("sync", mask8, x_ap["mask8"], wsem)
    P.dma("sync", identf, x_ap["identf"], wsem)
    mask_ready = wsem.val
    P.emit("vector", lambda e: e.memset(onescol, 1.0))
    P.emit("vector", lambda e: e.memset(onerow[0:1, :], 1.0))
    ones_ready = vch.val

    sc = [[psum["b0"], psum["b1"]], [psum["b2"], psum["b3"]]]
    pa = [psum["b4"], psum["b5"]]
    pl = [psum["b6"], psum["b7"]]
    pm = psum["b0"]

    sc_free = [[None, None], [None, None]]
    pa_free = [None, None]
    pl_free = [None, None]
    pm_free = None
    probs_free = [[0, 0], [0, 0]]
    probs_accv = [[0, 0], [0, 0]]

    for j in range(TPC):
        nmain = 8 * j
        for hg in (0, 1):
            qsl = Qb[:, j * 1024 + hg * 512: j * 1024 + (hg + 1) * 512]
            for p in range(8):
                bank = sc[hg][p % 2]
                w = []
                if j == 0 and hg == 0 and p < 2:
                    w.append((wsem, mask_ready))
                if sc_free[hg][p % 2] is not None:
                    w.append(sc_free[hg][p % 2])
                P.emit("tensor", lambda e, kt=KT[:, (8 * j + p) * 128:(8 * j + p + 1) * 128],
                       q=qsl, b=bank: e.matmul(b[:, 0:512], kt, q, start=True, stop=True),
                       waits=w, incs=[(pe, 1)], chain=False)
                pe.val += 1
                P.emit("vector", lambda e, b=bank, o=stash[:, p * 512:(p + 1) * 512],
                       mk=mask8[:, p * 512:(p + 1) * 512]:
                       e.tensor_tensor(o, b[:, 0:512], mk, ALU.add),
                       waits=[(pe, pe.val)])
                sc_free[hg][p % 2] = (vch, vch.val)
            P.emit("vector", lambda e: e.reduce_max(msc[:, 0:1], stash, axis=AX.X))
            mx_v = vch.val
            tw = [(vch, mx_v)]
            if j == 0 and hg == 0:
                tw.append((wsem, mask_ready))
            if pm_free is not None:
                tw.append(pm_free)
            if sc_free[0][0] is not None:
                tw.append(sc_free[0][0])
            P.emit("tensor", lambda e: e.transpose(pm[0:1, 0:128], msc[:, 0:1], identf),
                   waits=tw, incs=[(pe, 1)], chain=False)
            pe.val += 1
            P.emit("vector", lambda e: e.reduce_max(msc[0:1, 1:2], pm[0:1, 0:128], axis=AX.X),
                   waits=[(pe, pe.val)])
            P.emit("vector", lambda e: e.tensor_scalar(
                msc[0:1, 3:4], msc[0:1, 1:2], -SCALE, None, ALU.mult))
            mneg_v = vch.val
            P.emit("tensor", lambda e: e.matmul(pm[:, 33:34], onerow[0:1, :],
                   msc[0:1, 3:4], start=True, stop=True),
                   waits=[(vch, mneg_v), (vch, ones_ready)], incs=[(pe, 1)], chain=False)
            pe.val += 1
            P.emit("vector", lambda e, d=mcol[hg][:, 0:1]: e.tensor_copy(d, pm[:, 33:34]),
                   waits=[(pe, pe.val)])
            pm_free = (vch, vch.val)
            sc_free[0][0] = (vch, vch.val)
            mc_ready = vch.val
            for p in range(8):
                pb = probs[hg][p % 2]
                P.emit("scalar", lambda e, o=pb, i=stash[:, p * 512:(p + 1) * 512],
                       m=mcol[hg][:, 0:1]:
                       e.activation(o, i, AF.Exp, bias=m, scale=SCALE),
                       waits=[(vch, mc_ready), (pe, probs_free[hg][p % 2]),
                              (vch, probs_accv[hg][p % 2])])
                first = (p == 0)
                last = (j == 0 and p == 7)
                if first:
                    P.emit("vector", lambda e, a=pacc[hg], pbuf=pb: e.tensor_copy(a, pbuf),
                           waits=[(sch, sch.val)])
                else:
                    P.emit("vector", lambda e, a=pacc[hg], pbuf=pb:
                           e.tensor_tensor(a, a, pbuf, ALU.add),
                           waits=[(sch, sch.val)])
                acc_v = vch.val
                pw = []
                if first and pa_free[hg] is not None:
                    pw.append(pa_free[hg])
                pw.append((sch, sch.val))
                P.emit("tensor", lambda e, v=Vb[:, (8 * j + p) * 128:(8 * j + p + 1) * 128],
                       pbuf=pb, b=pa[hg], f=first, l=last:
                       e.matmul(b[:, 0:512], v, pbuf, start=f, stop=l),
                       waits=pw, incs=[(pe, 1)], chain=False)
                pe.val += 1
                probs_free[hg][p % 2] = pe.val
                probs_accv[hg][p % 2] = acc_v
        exp_done = {}
        for kb in range(nmain + 1):
            if kb < nmain:
                for hg in (0, 1):
                    bank = sc[hg][kb % 2]
                    qsl = Qb[:, j * 1024 + hg * 512: j * 1024 + (hg + 1) * 512]
                    w = [sc_free[hg][kb % 2]] if sc_free[hg][kb % 2] is not None else []
                    P.emit("tensor", lambda e, kt=KT[:, kb * 128:(kb + 1) * 128], q=qsl, b=bank:
                           e.matmul(b[:, 0:512], kt, q, start=True, stop=True),
                           waits=w, incs=[(pe, 1)], chain=False)
                    pe.val += 1
                    P.emit("scalar", lambda e, b=bank, o=probs[hg][kb % 2], m=mcol[hg][:, 0:1]:
                           e.activation(o, b[:, 0:512], AF.Exp, bias=m, scale=SCALE),
                           waits=[(pe, pe.val), (pe, probs_free[hg][kb % 2]),
                                  (vch, probs_accv[hg][kb % 2])])
                    sc_free[hg][kb % 2] = (sch, sch.val)
                    exp_done[(kb, hg)] = (sch, sch.val)
            if kb > 0:
                kp = kb - 1
                last = (kp == nmain - 1)
                for hg in (0, 1):
                    pb = probs[hg][kp % 2]
                    P.emit("vector", lambda e, a=pacc[hg], pbuf=pb:
                           e.tensor_tensor(a, a, pbuf, ALU.add),
                           waits=[exp_done[(kp, hg)]])
                    probs_accv[hg][kp % 2] = vch.val
                    P.emit("tensor", lambda e, v=Vb[:, kp * 128:(kp + 1) * 128],
                           pbuf=pb, b=pa[hg], l=last:
                           e.matmul(b[:, 0:512], v, pbuf, start=False, stop=l),
                           waits=[exp_done[(kp, hg)]], incs=[(pe, 1)], chain=False)
                    pe.val += 1
                    probs_free[hg][kp % 2] = pe.val
        for hg in (0, 1):
            P.emit("vector", lambda e, d=pbf[hg], a=pacc[hg]: e.tensor_copy(d, a),
                   waits=[(pe, probs_free[hg][0])])
            cast_v = vch.val
            lw = [(vch, cast_v), (vch, ones_ready)]
            if pl_free[hg] is not None:
                lw.append(pl_free[hg])
            P.emit("tensor", lambda e, o=pl[hg][0:1, 0:512], pbuf=pbf[hg]:
                   e.matmul(o, onescol[:, 0:1], pbuf, start=True, stop=True),
                   waits=lw, incs=[(pe, 1)], chain=False)
            pe.val += 1
            probs_free[hg][0] = pe.val
        l_done = pe.val
        P.emit("vector", lambda e: e.reciprocal(stash[0:1, 0:512], pl[0][0:1, 0:512]),
               waits=[(pe, l_done)])
        pl_free[0] = (vch, vch.val)
        P.emit("vector", lambda e: e.reciprocal(stash[0:1, 512:1024], pl[1][0:1, 0:512]))
        pl_free[1] = (vch, vch.val)
        rec_v = vch.val
        for hg in (0, 1):
            bw = [(vch, rec_v)]
            if pm_free is not None:
                bw.append(pm_free)
            if sc_free[0][0] is not None:
                bw.append(sc_free[0][0])
            P.emit("tensor", lambda e, r=stash[0:1, hg * 512:(hg + 1) * 512]:
                   e.matmul(pm[:, 0:512], onerow[0:1, :], r, start=True, stop=True),
                   waits=bw, incs=[(pe, 1)], chain=False)
            pe.val += 1
            P.emit("scalar", lambda e, d=rb[:, hg * 512:(hg + 1) * 512]:
                   e.copy(d, pm[:, 0:512]), waits=[(pe, pe.val)])
            pm_free = (sch, sch.val)
            sc_free[0][0] = (sch, sch.val)
            P.emit("vector", lambda e, b=pa[hg], r=rb[:, hg * 512:(hg + 1) * 512],
                   d=attn[:, j * 1024 + hg * 512: j * 1024 + (hg + 1) * 512]:
                   e.tensor_tensor(d, b[:, 0:512], r, ALU.mult),
                   waits=[(sch, sch.val)])
            pa_free[hg] = (vch, vch.val)
    bf.release(); f32.release()
    return dict(attn=attn)


def phase_wo(P, x_ap, B, bf, f32, psum, s_out):
    vch, sch = P.chain["vector"], P.chain["scalar"]
    pe = P.sem_pe
    attn = B["attn"]
    P.barrier()
    bf.mark(); f32.mark()
    wo = bf.alloc("WoT", 8 * 1024)
    wsem = P.new_sem("wow")
    P.dma("gpsimd", wo.rearrange("p (a d) -> p a d", a=8),
          x_ap["WoT"].rearrange("(a p) d -> p a d", p=128), wsem)
    w_ready = wsem.val
    s_tile = [f32.alloc("s_t0", 1024), f32.alloc("s_t1", 1024)]
    ssem = P.new_sem("s_st")
    ps = [psum["b2"], psum["b3"]]
    ps_free = [None, None]
    s_free = [0, 0]
    for j in range(TPC):
        k = j % 2
        for n in (0, 1):
            bank = ps[n]
            for h in range(8):
                w = []
                if j == 0 and h == 0:
                    w.append((wsem, w_ready))
                if h == 0 and ps_free[n] is not None:
                    w.append(ps_free[n])
                P.emit("tensor", lambda e, a=attn[:, j * 1024 + h * 128: j * 1024 + h * 128 + 128],
                       ww=wo[:, h * 1024 + n * 512: h * 1024 + n * 512 + 512], b=bank, hh=h:
                       e.matmul(b[:, 0:512], a, ww, start=(hh == 0), stop=(hh == 7)),
                       waits=w, incs=([(pe, 1)] if h == 7 else []), chain=False)
            pe.val += 1
            P.emit("vector", lambda e, b=bank, d=s_tile[k][:, n * 512:(n + 1) * 512]:
                   e.tensor_copy(d, b[:, 0:512]),
                   waits=[(pe, pe.val), (ssem, s_free[k])])
            ps_free[n] = (vch, vch.val)
        P.dma("sync", s_out[j * 128:(j + 1) * 128, :], s_tile[k], ssem,
              waits=[(vch, vch.val)])
        s_free[k] = ssem.val
    P.emit("sync", lambda e: e.wait_ge(ssem.h, ssem.val), chain=False)
    bf.release(); f32.release()
    return dict()



def phase_l0_gate(P, x_ap, B, bf, f32, psum):
    vch, sch = P.chain["vector"], P.chain["scalar"]
    pe = P.sem_pe
    xnT = B["xnT"]
    g_buf = B["g_buf"]
    P.barrier()
    bf.mark(); f32.mark()
    gw = bf.alloc("gwT", 8 * 16)
    ebias = f32.alloc("ebias", 16)
    ebc = f32.alloc("ebc", 16)
    onerow = f32.alloc("onerow", 128)
    L = f32.alloc("L", 16)
    Bi = f32.alloc("Bi", 16)
    sel = f32.alloc("sel", 16)
    tmp = f32.alloc("tmp", 16)
    cols = f32.alloc("cols", 8)
    wsem = P.new_sem("gw")
    P.dma("gpsimd", gw.rearrange("p (a d) -> p a d", a=8),
          x_ap["gwT"].rearrange("(a p) d -> p a d", p=128), wsem)
    P.dma("sync", ebias[0:1, :], x_ap["ebias"], wsem)
    w_ready = wsem.val
    P.emit("vector", lambda e: e.memset(onerow[0:1, :], 1.0))
    ones_v = vch.val
    P.emit("tensor", lambda e: e.matmul(psum["b2"][:, 0:16], onerow[0:1, :], ebias[0:1, :],
           start=True, stop=True),
           waits=[(wsem, w_ready), (vch, ones_v)], incs=[(pe, 1)], chain=False)
    pe.val += 1
    P.emit("vector", lambda e: e.tensor_copy(ebc, psum["b2"][:, 0:16]), waits=[(pe, pe.val)])
    ebc_v = vch.val
    lg_free = None
    for j in range(TPC):
        for et in range(8):
            w = [(vch, ebc_v)] if et == 0 else []
            if et == 0 and lg_free is not None:
                w.append(lg_free)
            P.emit("tensor", lambda e, s=xnT[:, et * 2048 + j * 128: et * 2048 + j * 128 + 128],
                   g=gw[:, et * 16:(et + 1) * 16], o=et:
                   e.matmul(psum["b2"][:, 32:48], s, g, start=(o == 0), stop=(o == 7)),
                   waits=w, incs=([(pe, 1)] if et == 7 else []), chain=False)
        pe.val += 1
        P.emit("vector", lambda e: e.tensor_copy(L, psum["b2"][:, 32:48]), waits=[(pe, pe.val)])
        lg_free = (vch, vch.val)
        P.emit("vector", lambda e: e.tensor_tensor(Bi, L, ebc, ALU.add))
        P.emit("vector", lambda e: e.memset(sel, 0.0))
        for r in range(K0):
            P.emit("vector", lambda e: e.reduce_max(cols[:, 0:1], Bi, axis=AX.X))
            P.emit("vector", lambda e: e.tensor_scalar(tmp, Bi, cols[:, 0:1], None, ALU.is_ge))
            P.emit("vector", lambda e: e.tensor_tensor(sel, sel, tmp, ALU.add))
            P.emit("vector", lambda e: e.scalar_tensor_tensor(Bi, tmp, -1e30, Bi, ALU.mult, ALU.add))
        P.emit("vector", lambda e: e.tensor_scalar(tmp, sel, 1e9, -1e9, ALU.mult, ALU.add))
        P.emit("vector", lambda e: e.tensor_tensor(tmp, L, tmp, ALU.add))
        P.emit("vector", lambda e: e.reduce_max(cols[:, 1:2], tmp, axis=AX.X))
        P.emit("vector", lambda e: e.tensor_scalar(cols[:, 2:3], cols[:, 1:2], -1.0, None, ALU.mult))
        P.emit("scalar", lambda e: e.activation(Bi, tmp, AF.Exp, bias=cols[:, 2:3],
               accum_out=cols[:, 3:4]), waits=[(vch, vch.val)])
        P.emit("vector", lambda e: e.reciprocal(cols[:, 4:5], cols[:, 3:4]),
               waits=[(sch, sch.val)])
        P.emit("vector", lambda e, d=g_buf[:, j * 16:(j + 1) * 16]:
               e.tensor_scalar(d, Bi, cols[:, 4:5], None, ALU.mult))
    bf.release(); f32.release()
    return dict()



def phase_l0_ffn(P, x_ap, B, bf, f32, psum):
    vch, sch = P.chain["vector"], P.chain["scalar"]
    pe = P.sem_pe
    xnT, g_buf, y_acc = B["xnT"], B["g_buf"], B["y_acc"]
    P.barrier()
    bf.mark(); f32.mark()
    w1 = bf.alloc("w1T", 8 * 1024)
    w3 = bf.alloc("w3T", 8 * 1024)
    w2 = bf.alloc("w2T", 8 * 1024)
    hh = bf.alloc("hh", 8 * 2048)
    sil = [f32.alloc("sil0", 512), f32.alloc("sil1", 512)]
    P.emit("vector", lambda e: e.memset(y_acc[:, 0:8192], 0.0))
    P.emit("vector", lambda e: e.memset(y_acc[:, 8192:16384], 0.0))
    y_zeroed = vch.val
    wsem = P.new_sem("f0w")
    pu = [(psum["b2"], psum["b3"]), (psum["b4"], psum["b5"])]
    pd = [psum["b6"], psum["b7"]]
    pu_free = [None, None]
    pd_free = [None, None]
    hh_free = 0
    w_free = 0
    sil_free = [0, 0]
    for e0 in range(E0 + 1):
        shared = (e0 == E0)
        pref = "sw" if shared else "w"
        keys = (("sw1T", "sw2T", "sw3T") if shared else ("w1T", "w2T", "w3T"))
        src1 = x_ap[keys[0]] if shared else x_ap[keys[0]][e0]
        src2 = x_ap[keys[1]] if shared else x_ap[keys[1]][e0]
        src3 = x_ap[keys[2]] if shared else x_ap[keys[2]][e0]
        dw = [(P.sem_pe, w_free)] if w_free else []
        P.dma("gpsimd", w1.rearrange("p (a d) -> p a d", a=8),
              src1.rearrange("(a p) d -> p a d", p=128), wsem, waits=dw)
        P.dma("gpsimd", w3.rearrange("p (a d) -> p a d", a=8),
              src3.rearrange("(a p) d -> p a d", p=128), wsem)
        P.dma("gpsimd", w2.rearrange("p (a d) -> p a d", a=8),
              src2.rearrange("(a p) d -> p a d", p=128), wsem)
        w_up = wsem.val - 16
        w_dn = wsem.val
        ctr = 0
        for isub in range(8):
            for tc in range(4):
                b1, b3 = pu[ctr % 2]
                w = [(wsem, w_up)] if (isub == 0 and tc == 0) else []
                if pu_free[ctr % 2] is not None:
                    w.append(pu_free[ctr % 2])
                if isub == 0 and tc == 0 and hh_free:
                    w.append((pe, hh_free))
                for et in range(8):
                    P.emit("tensor", lambda e, ww=w1[:, et * 1024 + isub * 128: et * 1024 + isub * 128 + 128],
                           r=xnT[:, et * 2048 + tc * 512: et * 2048 + tc * 512 + 512],
                           b=b1, o=et:
                           e.matmul(b[:, 0:512], ww, r, start=(o == 0), stop=(o == 7)),
                           waits=(w if et == 0 else []), chain=False)
                for et in range(8):
                    P.emit("tensor", lambda e, ww=w3[:, et * 1024 + isub * 128: et * 1024 + isub * 128 + 128],
                           r=xnT[:, et * 2048 + tc * 512: et * 2048 + tc * 512 + 512],
                           b=b3, o=et:
                           e.matmul(b[:, 0:512], ww, r, start=(o == 0), stop=(o == 7)),
                           incs=([(pe, 1)] if et == 7 else []), chain=False)
                pe.val += 1
                P.emit("scalar", lambda e, b=b1, o=sil[ctr % 2]:
                       e.activation(o, b[:, 0:512], AF.Silu),
                       waits=[(pe, pe.val), (vch, sil_free[ctr % 2])])
                P.emit("vector", lambda e, b=b3, s=sil[ctr % 2],
                       d=hh[:, isub * 2048 + tc * 512: isub * 2048 + tc * 512 + 512]:
                       e.tensor_tensor(d, s, b[:, 0:512], ALU.mult),
                       waits=[(sch, sch.val)])
                pu_free[ctr % 2] = (vch, vch.val)
                sil_free[ctr % 2] = vch.val
                ctr += 1
        hh_v = vch.val
        for j in range(TPC):
            for n in (0, 1):
                bank = pd[n]
                for isub in range(8):
                    w = []
                    if isub == 0:
                        w = [(vch, hh_v), (wsem, w_dn)]
                        if pd_free[n] is not None:
                            w.append(pd_free[n])
                    P.emit("tensor", lambda e, s=hh[:, isub * 2048 + j * 128: isub * 2048 + j * 128 + 128],
                           ww=w2[:, isub * 1024 + n * 512: isub * 1024 + n * 512 + 512],
                           b=bank, o=isub:
                           e.matmul(b[:, 0:512], s, ww, start=(o == 0), stop=(o == 7)),
                           waits=w, incs=([(pe, 1)] if isub == 7 else []), chain=False)
                pe.val += 1
                yw = [(pe, pe.val)]
                if e0 == 0:
                    yw.append((vch, y_zeroed))
                ysl = y_acc[:, j * 1024 + n * 512: j * 1024 + n * 512 + 512]
                if shared:
                    P.emit("vector", lambda e, b=bank, d=ysl:
                           e.tensor_tensor(d, b[:, 0:512], d, ALU.add), waits=yw)
                else:
                    P.emit("vector", lambda e, b=bank, d=ysl, gc=g_buf[:, j * 16 + e0: j * 16 + e0 + 1]:
                           e.scalar_tensor_tensor(d, b[:, 0:512], gc, d, ALU.mult, ALU.add),
                           waits=yw)
                pd_free[n] = (vch, vch.val)
        hh_free = pe.val
        w_free = pe.val
    bf.release(); f32.release()
    return dict()



def phase_combine(P, x_ap, B, bf, f32, psum, s_in, x1_out, xkey="x_own"):
    vch, sch = P.chain["vector"], P.chain["scalar"]
    y_acc = B["y_acc"]
    P.barrier()
    bf.mark(); f32.mark()
    xt = [f32.alloc("cx0", 1024), f32.alloc("cx1", 1024)]
    stl = [f32.alloc("cs0", 1024), f32.alloc("cs1", 1024)]
    gt = f32.alloc("cg", 1024)
    o1 = [f32.alloc("co0", 1024), f32.alloc("co1", 1024)]
    xsem = [P.new_sem("c0x0"), P.new_sem("c0x1")]
    osem = P.new_sem("c0o")
    buf_free = [0, 0]
    out_free = [0, 0]
    for j in range(TPC):
        k = j % 2
        P.dma("sync", xt[k], x_ap[xkey][j * 128:(j + 1) * 128, :], xsem[k],
              waits=[(vch, buf_free[k])])
        P.dma("sync", stl[k], s_in[j * 128:(j + 1) * 128, :], xsem[k])
        P.emit("scalar", lambda e, y=y_acc[:, j * 1024:(j + 1) * 1024]:
               e.activation(gt, y, AF.Silu), waits=[(vch, vch.val)])
        P.emit("vector", lambda e, s=stl[k]: e.tensor_tensor(gt, s, gt, ALU.mult),
               waits=[(sch, sch.val), (xsem[k], xsem[k].val)])
        P.emit("vector", lambda e, x=xt[k], d=o1[k]: e.tensor_tensor(d, x, gt, ALU.add),
               waits=[(osem, out_free[k])])
        buf_free[k] = vch.val
        P.dma("sync", x1_out[j * 128:(j + 1) * 128, :], o1[k], osem,
              waits=[(vch, vch.val)])
        out_free[k] = osem.val
    P.emit("sync", lambda e: e.wait_ge(osem.h, osem.val), chain=False)
    bf.release(); f32.release()
    return dict()



def phase_l1_router(P, x_ap, B, bf, f32, psum):
    vch, sch = P.chain["vector"], P.chain["scalar"]
    pe = P.sem_pe
    xnT, cw_buf = B["xn1T"], B["cw_buf"]
    P.barrier()
    bf.mark(); f32.mark()
    rw = bf.alloc("rwT", 8 * 8)
    L = f32.alloc("L", 8)
    Bi = f32.alloc("Bi", 8)
    sel = f32.alloc("sel", 8)
    tmp = f32.alloc("tmp", 8)
    cols = f32.alloc("cols", 8)
    wsem = P.new_sem("rw")
    P.dma("gpsimd", rw.rearrange("p (a d) -> p a d", a=8),
          x_ap["rwT"].rearrange("(a p) d -> p a d", p=128), wsem)
    w_ready = wsem.val
    lg_free = None
    for j in range(TPC):
        for et in range(8):
            w = [(wsem, w_ready)] if (j == 0 and et == 0) else []
            if et == 0 and lg_free is not None:
                w.append(lg_free)
            P.emit("tensor", lambda e, s=xnT[:, et * 2048 + j * 128: et * 2048 + j * 128 + 128],
                   g=rw[:, et * 8:(et + 1) * 8], o=et:
                   e.matmul(psum["b2"][:, 0:8], s, g, start=(o == 0), stop=(o == 7)),
                   waits=w, incs=([(pe, 1)] if et == 7 else []), chain=False)
        pe.val += 1
        P.emit("vector", lambda e: e.tensor_copy(L, psum["b2"][:, 0:8]), waits=[(pe, pe.val)])
        lg_free = (vch, vch.val)
        P.emit("vector", lambda e: e.tensor_copy(Bi, L))
        P.emit("vector", lambda e: e.memset(sel, 0.0))
        for r in range(K1):
            P.emit("vector", lambda e: e.reduce_max(cols[:, 0:1], Bi, axis=AX.X))
            P.emit("vector", lambda e: e.tensor_scalar(tmp, Bi, cols[:, 0:1], None, ALU.is_ge))
            P.emit("vector", lambda e: e.tensor_tensor(sel, sel, tmp, ALU.add))
            P.emit("vector", lambda e: e.scalar_tensor_tensor(Bi, tmp, -1e30, Bi, ALU.mult, ALU.add))
        P.emit("vector", lambda e: e.tensor_scalar(tmp, sel, 1e9, -1e9, ALU.mult, ALU.add))
        P.emit("vector", lambda e: e.tensor_tensor(tmp, L, tmp, ALU.add))
        P.emit("vector", lambda e: e.reduce_max(cols[:, 1:2], tmp, axis=AX.X))
        P.emit("vector", lambda e: e.tensor_scalar(cols[:, 2:3], cols[:, 1:2], -1.0, None, ALU.mult))
        P.emit("scalar", lambda e: e.activation(Bi, tmp, AF.Exp, bias=cols[:, 2:3],
               accum_out=cols[:, 3:4]), waits=[(vch, vch.val)])
        P.emit("vector", lambda e: e.reciprocal(cols[:, 4:5], cols[:, 3:4]),
               waits=[(sch, sch.val)])
        P.emit("vector", lambda e, d=cw_buf[:, j * 8:(j + 1) * 8]:
               e.tensor_scalar(d, Bi, cols[:, 4:5], None, ALU.mult))
    bf.release(); f32.release()
    return dict()



def phase_lru_scan(P, x_ap, B, bf, f32, psum, cc_in, cc_out):
    vch, sch = P.chain["vector"], P.chain["scalar"]
    pe = P.sem_pe
    ident = B["ident"]
    y_loc, Cown, Ptab, oml = B["y_loc"], B["Cown"], B["Ptab"], B["oml"]
    P.barrier()
    bf.mark(); f32.mark()
    win = bf.alloc("winT", 8 * 1024)
    lamc = f32.alloc("lamc", 8)
    pw = f32.alloc("pw", 8)
    onest = f32.alloc("onest", 128)
    lamrep = f32.alloc("lamrep", 8 * 128)
    Lrep = f32.alloc("Lrep", 8 * 128)
    g1c = f32.alloc("g1c", 8)
    b1c = f32.alloc("b1c", 8)
    selC = bf.alloc("selC", 16)
    Fown = f32.alloc("Fown", 128)
    Fg = f32.alloc("Fg", 128)
    Fscan = f32.alloc("Fscan", 128)
    Cinc = f32.alloc("Cinc", 8 * 128)
    Cb = bf.alloc("Cb", 128)
    CT = bf.alloc("CT", 128)
    xt = [f32.alloc("lx0", 1024), f32.alloc("lx1", 1024)]
    xn1t = f32.alloc("xn1t", 1024)
    scr = f32.alloc("scr", 1024)
    st2 = f32.alloc("st2", 8)
    xnb = [bf.alloc("lxb0", 1024), bf.alloc("lxb1", 1024)]
    uT = bf.alloc("uT", 1024)
    wsem = P.new_sem("lruw")
    P.dma("gpsimd", win.rearrange("p (a d) -> p a d", a=8),
          x_ap["winT"].rearrange("(a p) d -> p a d", p=128), wsem)
    P.dma("sync", lamc, x_ap["loglam_c"], wsem)
    P.dma("sync", g1c, x_ap["ln1g_c"], wsem)
    P.dma("sync", b1c, x_ap["ln1b_c"], wsem)
    P.dma("sync", selC, x_ap["selC"], wsem)
    w_ready = wsem.val
    P.emit("scalar", lambda e: e.activation(lamc, lamc, AF.Sigmoid), waits=[(wsem, w_ready)])
    P.emit("vector", lambda e: e.memset(onest, 1.0), waits=[(sch, sch.val)])
    P.emit("vector", lambda e: e.tensor_scalar(oml, lamc, -1.0, 1.0, ALU.mult, ALU.add))
    P.emit("vector", lambda e: e.tensor_copy(pw, lamc))
    for s in range(8):
        P.emit("vector", lambda e, d=lamrep[:, s * 128:(s + 1) * 128], c=lamc[:, s:s + 1]:
               e.tensor_scalar(d, onest, c, None, ALU.mult))
        P.emit("vector", lambda e, d=Ptab[:, s * 128: s * 128 + 1], c=lamc[:, s:s + 1]:
               e.tensor_scalar(d, onest[:, 0:1], c, None, ALU.mult))
    for i in range(7):
        n = 1 << i
        for s in range(8):
            P.emit("vector", lambda e, d=Ptab[:, s * 128 + n: s * 128 + 2 * n],
                   src=Ptab[:, s * 128: s * 128 + n], c=pw[:, s:s + 1]:
                   e.tensor_scalar(d, src, c, None, ALU.mult))
        P.emit("vector", lambda e: e.tensor_tensor(pw, pw, pw, ALU.mult))
    for s in range(8):
        P.emit("vector", lambda e, d=Lrep[:, s * 128:(s + 1) * 128], c=pw[:, s:s + 1]:
               e.tensor_scalar(d, onest, c, None, ALU.mult))
    tbl_v = vch.val
    xsem = [P.new_sem("lrx0"), P.new_sem("lrx1")]
    pT = [psum["bt0"], psum["bt1"]]
    pU = [psum["b2"], psum["b3"]]
    xt_free = [0, 0]
    xnb_free = [0, 0]
    tev = [None, None]
    uev = [None, None]
    for j in range(TPC):
        k = j % 2
        P.dma("sync", xt[k], x_ap["x1s"][j * 128:(j + 1) * 128, :], xsem[k],
              waits=[(vch, xt_free[k])])
        ln_tile(P, xt[k], xn1t, st2, scr, waits=[(xsem[k], xsem[k].val)])
        nw = ln_tile(P, xn1t, xnb[k], st2, scr,
                     extra_norm_waits=[(pe, xnb_free[k])])
        xt_free[k] = vch.val
        dsts = [uT[:, et * 128:(et + 1) * 128] for et in range(8)]
        ev = transpose8(P, xnb[k], dsts, ident, pT, tev, (vch, nw), gb=(g1c, b1c))
        xnb_free[k] = pe.val
        for s in range(8):
            bank = pU[s % 2]
            for dt in range(8):
                w = list(ev) + [(wsem, w_ready)] if (dt == 0 and s == 0) else []
                if dt == 0 and s > 0:
                    w = list(ev)
                if dt == 0 and uev[s % 2] is not None:
                    w.append(uev[s % 2])
                P.emit("tensor", lambda e, ww=win[:, dt * 1024 + s * 128: dt * 1024 + s * 128 + 128],
                       r=dsts[dt], b=bank, o=dt:
                       e.matmul(b[:, 0:128], ww, r, start=(o == 0), stop=(o == 7)),
                       waits=w, incs=([(pe, 1)] if dt == 7 else []), chain=False)
            pe.val += 1
            P.emit("vector", lambda e, b=bank, d=y_loc[:, s * 2048 + j * 128: s * 2048 + j * 128 + 128],
                   lr=lamrep[:, s * 128:(s + 1) * 128]:
                   e.tensor_tensor_scan(d, lr, b[:, 0:128], 0.0, ALU.mult, ALU.add),
                   waits=[(pe, pe.val), (vch, tbl_v if j == 0 else 0)])
            uev[s % 2] = (vch, vch.val)
    for s in range(8):
        P.emit("vector", lambda e, d=Fown[:, s * 16:(s + 1) * 16],
               i=y_loc[:, s * 2048 + 127: (s + 1) * 2048: 128]:
               e.tensor_copy(d, i))
    csem = P.new_sem("cc")
    P.dma("gpsimd", cc_in, Fown, csem, waits=[(vch, vch.val)])
    P.emit("gpsimd", lambda e: e.collective_compute(
        "AllGather", ALU.bypass, replica_groups=[list(range(NCORES))],
        ins=[cc_in.opt()], outs=[cc_out.opt()]),
        waits=[(csem, csem.val)], incs=[(csem, 1)], chain=False)
    csem.val += 1
    cc_done = csem.val
    fsem = P.new_sem("fb")
    fg_free = 0
    cb_free = 0
    for s in range(8):
        P.dma("sync", Fg.rearrange("p (r j) -> p r j", r=8),
              cc_out[:, s * 16:(s + 1) * 16].rearrange("(r p) j -> p r j", p=128),
              fsem, waits=[(csem, cc_done), (vch, fg_free)])
        P.emit("vector", lambda e, o=Fscan.rearrange("p (j r) -> p j r", r=8),
               i=Fg.rearrange("p (r j) -> p j r", j=16):
               e.tensor_copy(o, i), waits=[(fsem, fsem.val)])
        fg_free = vch.val
        P.emit("vector", lambda e, d=Cinc[:, s * 128:(s + 1) * 128],
               lr=Lrep[:, s * 128:(s + 1) * 128]:
               e.tensor_tensor_scan(d, lr, Fscan, 0.0, ALU.mult, ALU.add))
        P.emit("vector", lambda e, i=Cinc[:, s * 128:(s + 1) * 128]: e.tensor_copy(Cb, i),
               waits=[(pe, cb_free)])
        cb_v = vch.val
        P.emit("tensor", lambda e: e.transpose(pT[0][:, 0:128], Cb, ident),
               waits=[(vch, cb_v)], incs=[(pe, 1)], chain=False)
        pe.val += 1
        cb_free = pe.val
        P.emit("vector", lambda e: e.tensor_copy(CT, pT[0][:, 0:128]), waits=[(pe, pe.val)])
        ct_v = vch.val
        P.emit("tensor", lambda e: e.matmul(psum["b4"][:, 0:16], CT, selC, start=True, stop=True),
               waits=[(vch, ct_v)], incs=[(pe, 1)], chain=False)
        pe.val += 1
        P.emit("vector", lambda e, d=Cown[:, s * 16:(s + 1) * 16]:
               e.tensor_copy(d, psum["b4"][:, 0:16]), waits=[(pe, pe.val)])
    bf.release(); f32.release()
    return dict()



def phase_lru_out(P, x_ap, B, bf, f32, psum, x2_out):
    vch, sch = P.chain["vector"], P.chain["scalar"]
    pe = P.sem_pe
    y_loc, Cown, Ptab, oml = B["y_loc"], B["Cown"], B["Ptab"], B["oml"]
    P.barrier()
    bf.mark(); f32.mark()
    wout = bf.alloc("woutT", 8 * 1024)
    rows = f32.alloc("rows", 4 * 1024)
    gD = f32.alloc("gD", 1024)
    bD = f32.alloc("bD", 1024)
    onerow = f32.alloc("onerow", 128)
    yf = bf.alloc("yf", 1024)
    xt = [f32.alloc("ox0", 1024), f32.alloc("ox1", 1024)]
    xn1t = f32.alloc("oxn", 1024)
    nt = f32.alloc("ont", 1024)
    t3 = f32.alloc("ot3", 1024)
    x2t = [f32.alloc("ox20", 1024), f32.alloc("ox21", 1024)]
    scr = f32.alloc("scr", 1024)
    st2 = f32.alloc("st2", 8)
    wsem = P.new_sem("low")
    P.dma("gpsimd", wout.rearrange("p (a d) -> p a d", a=8),
          x_ap["woutT"].rearrange("(a p) d -> p a d", p=128), wsem)
    P.dma("sync", rows[0:1, 0:1024], x_ap["dskip_r"], wsem)
    P.dma("sync", rows[0:1, 1024:2048], x_ap["ln1g_r"], wsem)
    P.dma("sync", rows[0:1, 2048:3072], x_ap["ln1b_r"], wsem)
    w_ready = wsem.val
    for s in range(8):
        P.emit("vector", lambda e, d=wout[:, s * 1024:(s + 1) * 1024], c=oml[:, s:s + 1]:
               e.tensor_scalar(d, d, c, None, ALU.mult),
               waits=([(wsem, w_ready)] if s == 0 else []))
    wsc_v = vch.val
    P.emit("vector", lambda e: e.memset(onerow[0:1, :], 1.0))
    P.emit("vector", lambda e: e.tensor_tensor(rows[0:1, 3072:4096], rows[0:1, 1024:2048],
           rows[0:1, 0:1024], ALU.mult))
    P.emit("vector", lambda e: e.tensor_tensor(rows[0:1, 0:1024], rows[0:1, 2048:3072],
           rows[0:1, 0:1024], ALU.mult))
    rows_v = vch.val
    bprev = None
    for half, (src, dst) in enumerate([(rows[0:1, 3072:4096], gD), (rows[0:1, 0:1024], bD)]):
        for n in (0, 1):
            bw = [(vch, rows_v)] + ([bprev] if bprev else [])
            P.emit("tensor", lambda e, s=src[:, n * 512:(n + 1) * 512],
                   b=psum["b6"]: e.matmul(b[:, 0:512], onerow[0:1, :], s, start=True, stop=True),
                   waits=bw, incs=[(pe, 1)], chain=False)
            pe.val += 1
            P.emit("vector", lambda e, d=dst[:, n * 512:(n + 1) * 512], b=psum["b6"]:
                   e.tensor_copy(d, b[:, 0:512]), waits=[(pe, pe.val)])
            bprev = (vch, vch.val)
    bc_v = vch.val
    xsem = [P.new_sem("lox0"), P.new_sem("lox1")]
    osem = P.new_sem("x2o")
    pO = [psum["b2"], psum["b3"]]
    xt_free = [0, 0]
    x2_free = [0, 0]
    pO_free = [None, None]
    yf_free = 0
    for j in range(TPC):
        k = j % 2
        P.dma("sync", xt[k], x_ap["x1s"][j * 128:(j + 1) * 128, :], xsem[k],
              waits=[(vch, xt_free[k])])
        for s in range(8):
            w = [(vch, bc_v)] if (j == 0 and s == 0) else []
            if s == 0 and yf_free:
                w.append((pe, yf_free))
            P.emit("vector", lambda e, d=yf[:, s * 128:(s + 1) * 128],
                   p=Ptab[:, s * 128:(s + 1) * 128], c=Cown[:, s * 16 + j: s * 16 + j + 1],
                   z=y_loc[:, s * 2048 + j * 128: s * 2048 + j * 128 + 128]:
                   e.scalar_tensor_tensor(d, p, c, z, ALU.mult, ALU.add), waits=w)
        yf_v = vch.val
        for n in (0, 1):
            bank = pO[n]
            for s in range(8):
                w = []
                if s == 0:
                    w = [(vch, yf_v)] + ([(vch, wsc_v)] if j == 0 else [])
                    if pO_free[n] is not None:
                        w.append(pO_free[n])
                P.emit("tensor", lambda e, yy=yf[:, s * 128:(s + 1) * 128],
                       ww=wout[:, s * 1024 + n * 512: s * 1024 + n * 512 + 512], b=bank, o=s:
                       e.matmul(b[:, 0:512], yy, ww, start=(o == 0), stop=(o == 7)),
                       waits=w, incs=([(pe, 1)] if s == 7 else []), chain=False)
            pe.val += 1
        ln_tile(P, xt[k], xn1t, st2, scr, waits=[(xsem[k], xsem[k].val)])
        ln_tile(P, xn1t, nt, st2, scr)
        P.emit("vector", lambda e: e.tensor_tensor(t3, nt, gD, ALU.mult))
        P.emit("vector", lambda e: e.tensor_tensor(t3, t3, bD, ALU.add))
        P.emit("vector", lambda e: e.tensor_tensor(t3, t3, xn1t, ALU.add),
               waits=[(osem, x2_free[k])])
        for n in (0, 1):
            P.emit("vector", lambda e, d=x2t[k][:, n * 512:(n + 1) * 512],
                   s=t3[:, n * 512:(n + 1) * 512], b=pO[n]:
                   e.tensor_tensor(d, s, b[:, 0:512], ALU.add),
                   waits=[(pe, pe.val)])
            pO_free[n] = (vch, vch.val)
        xt_free[k] = vch.val
        yf_free = pe.val
        P.dma("sync", x2_out[j * 128:(j + 1) * 128, :], x2t[k], osem,
              waits=[(vch, vch.val)])
        x2_free[k] = osem.val
    P.emit("sync", lambda e: e.wait_ge(osem.h, osem.val), chain=False)
    bf.release(); f32.release()
    return dict()



def phase_mlp(P, x_ap, B, bf, f32, psum, x2_in, s1_out):
    vch, sch = P.chain["vector"], P.chain["scalar"]
    pe = P.sem_pe
    ident = B["ident"]
    P.barrier()
    bf.mark(); f32.mark()
    m1 = bf.alloc("m1T", 8 * 2048)
    m2 = bf.alloc("m2T", 16 * 1024)
    g2c = f32.alloc("g2c", 8)
    b2c_ = f32.alloc("b2c_", 8)
    b1c = f32.alloc("b1c", 16)
    b2row = f32.alloc("b2row", 1024)
    b2bc = f32.alloc("b2bc", 1024)
    onerow = f32.alloc("onerow", 128)
    hT = bf.alloc("hT", 16 * 512)
    x2nT = bf.alloc("x2nT", 8 * 512)
    xt = [f32.alloc("mx0", 4 * 1024)]
    x2pb = [f32.alloc("mpb0", 4 * 1024)]
    xnb = bf.alloc("mxb", 1024)
    s1t = [f32.alloc("ms0", 4 * 1024)]
    scr = f32.alloc("scr", 1024)
    st2 = f32.alloc("st2", 8)
    wsem = P.new_sem("mw")
    P.dma("gpsimd", m1.rearrange("p (a d) -> p a d", a=8),
          x_ap["m1T"].rearrange("(a p) d -> p a d", p=128), wsem)
    P.dma("gpsimd", m2.rearrange("p (a d) -> p a d", a=16),
          x_ap["m2T"].rearrange("(a p) d -> p a d", p=128), wsem)
    P.dma("sync", g2c, x_ap["ln2g_c"], wsem)
    P.dma("sync", b2c_, x_ap["ln2b_c"], wsem)
    P.dma("sync", b1c, x_ap["b1_c"], wsem)
    P.dma("sync", b2row[0:1, :], x_ap["b2_r"], wsem)
    w_ready = wsem.val
    P.emit("vector", lambda e: e.memset(onerow[0:1, :], 1.0))
    ones_v = vch.val
    bprev = None
    for n in (0, 1):
        bw = [(vch, ones_v), (wsem, w_ready)] + ([bprev] if bprev else [])
        P.emit("tensor", lambda e, s=b2row[0:1, n * 512:(n + 1) * 512]:
               e.matmul(psum["b6"][:, 0:512], onerow[0:1, :], s, start=True, stop=True),
               waits=bw, incs=[(pe, 1)], chain=False)
        pe.val += 1
        P.emit("vector", lambda e, d=b2bc[:, n * 512:(n + 1) * 512]:
               e.tensor_copy(d, psum["b6"][:, 0:512]), waits=[(pe, pe.val)])
        bprev = (vch, vch.val)
    bc_v = vch.val
    xsem = [P.new_sem("mx0s")]
    osem = P.new_sem("ms1o")
    pT = [psum["bt0"], psum["bt1"]]
    pH = [psum["b2"], psum["b3"]]
    pS = [psum["b4"], psum["b5"]]
    xt_free = [0]
    s1_free = [0]
    tev = [None, None]
    pH_free = [None, None]
    pS_free = [None, None]
    NCH = TPC // 4
    for ch in range(NCH):
        k = 0
        P.dma("sync", xt[k], x2_in[ch * 512:(ch + 1) * 512, :], xsem[k],
              waits=[(vch, xt_free[k])])
        for t4 in range(4):
            xtile = xt[k][:, t4 * 1024:(t4 + 1) * 1024]
            nw = ln_tile(P, xtile, xnb, st2, scr,
                         waits=([(xsem[k], xsem[k].val)] if t4 == 0 else []))
            P.emit("vector", lambda e, d=x2pb[k][:, t4 * 1024:(t4 + 1) * 1024], i=xtile:
                   e.tensor_tensor(d, i, b2bc, ALU.add),
                   waits=[(vch, bc_v)])
            dsts = [x2nT[:, et * 512 + t4 * 128: et * 512 + t4 * 128 + 128] for et in range(8)]
            transpose8(P, xnb, dsts, ident, pT, tev, (vch, nw), gb=(g2c, b2c_))
        ev_v = vch.val
        for ft in range(16):
            bank = pH[ft % 2]
            for dt in range(8):
                w = []
                if dt == 0:
                    w = [(vch, ev_v)]
                    if pH_free[ft % 2] is not None:
                        w.append(pH_free[ft % 2])
                P.emit("tensor", lambda e, ww=m1[:, dt * 2048 + ft * 128: dt * 2048 + ft * 128 + 128],
                       r=x2nT[:, dt * 512:(dt + 1) * 512], b=bank, o=dt:
                       e.matmul(b[:, 0:512], ww, r, start=(o == 0), stop=(o == 7)),
                       waits=w, incs=([(pe, 1)] if dt == 7 else []), chain=False)
            pe.val += 1
            P.emit("scalar", lambda e, b=bank, d=hT[:, ft * 512:(ft + 1) * 512],
                   bc=b1c[:, ft:ft + 1]:
                   e.activation(d, b[:, 0:512], AF.Relu, bias=bc),
                   waits=[(pe, pe.val)])
            pH_free[ft % 2] = (sch, sch.val)
        h_v = sch.val
        for t4 in range(4):
            for n in (0, 1):
                bank = pS[n]
                for ft in range(16):
                    w = []
                    if ft == 0:
                        w = [(sch, h_v)]
                        if pS_free[n] is not None:
                            w.append(pS_free[n])
                    P.emit("tensor", lambda e, s=hT[:, ft * 512 + t4 * 128: ft * 512 + t4 * 128 + 128],
                           ww=m2[:, ft * 1024 + n * 512: ft * 1024 + n * 512 + 512], b=bank, o=ft:
                           e.matmul(b[:, 0:512], s, ww, start=(o == 0), stop=(o == 15)),
                           waits=w, incs=([(pe, 1)] if ft == 15 else []), chain=False)
                pe.val += 1
                P.emit("vector", lambda e, b=bank,
                       d=s1t[k][:, t4 * 1024 + n * 512: t4 * 1024 + n * 512 + 512],
                       p=x2pb[k][:, t4 * 1024 + n * 512: t4 * 1024 + n * 512 + 512]:
                       e.tensor_tensor(d, b[:, 0:512], p, ALU.add),
                       waits=[(pe, pe.val), (osem, s1_free[k])])
                pS_free[n] = (vch, vch.val)
        xt_free[k] = vch.val
        P.dma("sync", s1_out[ch * 512:(ch + 1) * 512, :], s1t[k], osem,
              waits=[(vch, vch.val)])
        s1_free[k] = osem.val
    P.emit("sync", lambda e: e.wait_ge(osem.h, osem.val), chain=False)
    bf.release(); f32.release()
    return dict()



def phase_l1_ffn(P, x_ap, B, bf, f32, psum):
    vch, sch = P.chain["vector"], P.chain["scalar"]
    pe = P.sem_pe
    xnT, cw_buf, y_acc = B["xn1T"], B["cw_buf"], B["y_acc"]
    P.barrier()
    bf.mark(); f32.mark()
    wg = bf.alloc("wg", 8 * 1024)
    wu = bf.alloc("wu", 8 * 1024)
    wd = bf.alloc("wd", 8 * 1024)
    hh = bf.alloc("hh1", 8 * 2048)
    sil = [f32.alloc("sil0", 512), f32.alloc("sil1", 512)]
    P.emit("vector", lambda e: e.memset(y_acc[:, 0:8192], 0.0))
    P.emit("vector", lambda e: e.memset(y_acc[:, 8192:16384], 0.0))
    y_zeroed = vch.val
    wsem = P.new_sem("f1w")
    pu = [(psum["b2"], psum["b3"]), (psum["b4"], psum["b5"])]
    pd = [psum["b6"], psum["b7"]]
    pu_free = [None, None]
    pd_free = [None, None]
    hh_free = 0
    w_free = 0
    sil_free = [0, 0]
    first = True
    for e0 in range(E1):
        for sl in range(4):
            dw = [(P.sem_pe, w_free)] if w_free else []
            P.dma("gpsimd", wg.rearrange("p (a d) -> p a d", a=8),
                  x_ap["wg"][e0][:, sl * 1024:(sl + 1) * 1024].rearrange("(a p) d -> p a d", p=128),
                  wsem, waits=dw)
            P.dma("gpsimd", wu.rearrange("p (a d) -> p a d", a=8),
                  x_ap["wu"][e0][:, sl * 1024:(sl + 1) * 1024].rearrange("(a p) d -> p a d", p=128),
                  wsem)
            P.dma("gpsimd", wd.rearrange("p (a d) -> p a d", a=8),
                  x_ap["wd"][e0][sl * 1024:(sl + 1) * 1024, :].rearrange("(a p) d -> p a d", p=128),
                  wsem)
            w_up = wsem.val - 16
            w_dn = wsem.val
            ctr = 0
            for isub in range(8):
                for tc in range(4):
                    b1, b3 = pu[ctr % 2]
                    w = [(wsem, w_up)] if (isub == 0 and tc == 0) else []
                    if pu_free[ctr % 2] is not None:
                        w.append(pu_free[ctr % 2])
                    if isub == 0 and tc == 0 and hh_free:
                        w.append((pe, hh_free))
                    for et in range(8):
                        P.emit("tensor", lambda e, ww=wg[:, et * 1024 + isub * 128: et * 1024 + isub * 128 + 128],
                               r=xnT[:, et * 2048 + tc * 512: et * 2048 + tc * 512 + 512],
                               b=b1, o=et:
                               e.matmul(b[:, 0:512], ww, r, start=(o == 0), stop=(o == 7)),
                               waits=(w if et == 0 else []), chain=False)
                    for et in range(8):
                        P.emit("tensor", lambda e, ww=wu[:, et * 1024 + isub * 128: et * 1024 + isub * 128 + 128],
                               r=xnT[:, et * 2048 + tc * 512: et * 2048 + tc * 512 + 512],
                               b=b3, o=et:
                               e.matmul(b[:, 0:512], ww, r, start=(o == 0), stop=(o == 7)),
                               incs=([(pe, 1)] if et == 7 else []), chain=False)
                    pe.val += 1
                    P.emit("scalar", lambda e, b=b1, o=sil[ctr % 2]:
                           e.activation(o, b[:, 0:512], AF.Silu),
                           waits=[(pe, pe.val), (vch, sil_free[ctr % 2])])
                    P.emit("vector", lambda e, b=b3, s=sil[ctr % 2],
                           d=hh[:, isub * 2048 + tc * 512: isub * 2048 + tc * 512 + 512]:
                           e.tensor_tensor(d, s, b[:, 0:512], ALU.mult),
                           waits=[(sch, sch.val)])
                    pu_free[ctr % 2] = (vch, vch.val)
                    sil_free[ctr % 2] = vch.val
                    ctr += 1
            hh_v = vch.val
            for j in range(TPC):
                for n in (0, 1):
                    bank = pd[n]
                    for isub in range(8):
                        w = []
                        if isub == 0:
                            w = [(vch, hh_v), (wsem, w_dn)]
                            if pd_free[n] is not None:
                                w.append(pd_free[n])
                        P.emit("tensor", lambda e, s=hh[:, isub * 2048 + j * 128: isub * 2048 + j * 128 + 128],
                               ww=wd[:, isub * 1024 + n * 512: isub * 1024 + n * 512 + 512],
                               b=bank, o=isub:
                               e.matmul(b[:, 0:512], s, ww, start=(o == 0), stop=(o == 7)),
                               waits=w, incs=([(pe, 1)] if isub == 7 else []), chain=False)
                    pe.val += 1
                    yw = [(pe, pe.val)]
                    if first:
                        yw.append((vch, y_zeroed))
                    ysl = y_acc[:, j * 1024 + n * 512: j * 1024 + n * 512 + 512]
                    P.emit("vector", lambda e, b=bank, d=ysl,
                           gc=cw_buf[:, j * 8 + e0: j * 8 + e0 + 1]:
                           e.scalar_tensor_tensor(d, b[:, 0:512], gc, d, ALU.mult, ALU.add),
                           waits=yw)
                    pd_free[n] = (vch, vch.val)
            hh_free = pe.val
            w_free = pe.val
            first = False
    bf.release(); f32.release()
    return dict()



def _build_A():
    nc = bass.Bass("TRN2", target_bir_lowering=False)
    io = {}
    for name, shape in (("x", (T, D)), ("WkT", (D, HD)), ("WvT", (D, HD)),
                        ("WqT", (D, D)), ("WoT", (D, D)), ("identf", (128, 128)),
                        ("pek_c", (HD, 1)), ("peqT", (128, TOK)),
                        ("x_own", (TOK, D)), ("mask8", (128, 8 * 512))):
        io[name] = dram_in(nc, name, shape, F32)
    io["ident"] = dram_in(nc, "ident", (128, 128), BF16)
    s_out = dram_out(nc, "s_out", (TOK, D), F32)
    with ExitStack() as stack:
        P, bf, f32, psum = make_ctx(stack, nc)
        B = dict(ident=bf.alloc("ident", 128), KT=bf.alloc("KT", NT * 128),
                 Vb=bf.alloc("Vb", NT * 128), xnT=bf.alloc("xnT", 8 * TOK),
                 Qb=bf.alloc("Qb", TPC * 1024), attn=bf.alloc("attn", TPC * 1024))
        phase_l0_kv(P, io, B, bf, f32, psum)
        phase_l0_own(P, io, B, bf, f32, psum)
        phase_attn(P, io, B, bf, f32, psum)
        phase_wo(P, io, B, bf, f32, psum, s_out)
        with nc.Block() as block:
            P.replay(block)
    return nc


def _build_B():
    nc = bass.Bass("TRN2", target_bir_lowering=False)
    io = {}
    for name, shape in (("x_own", (TOK, D)), ("s_in", (TOK, D)),
                        ("gwT", (D, E0)), ("ebias", (1, E0)),
                        ("sw1T", (D, D)), ("sw2T", (D, D)), ("sw3T", (D, D))):
        io[name] = dram_in(nc, name, shape, F32)
    for name in ("w1T", "w2T", "w3T"):
        io[name] = dram_in(nc, name, (E0, D, D), F32)
    io["ident"] = dram_in(nc, "ident", (128, 128), BF16)
    x1_out = dram_out(nc, "x1", (TOK, D), F32)
    with ExitStack() as stack:
        P, bf, f32, psum = make_ctx(stack, nc)
        B = dict(ident=bf.alloc("ident", 128), xnT=bf.alloc("xnT", 8 * TOK),
                 g_buf=f32.alloc("g_buf", TPC * 16),
                 y_acc=f32.alloc("y_acc", TPC * 1024))
        P.dma("sync", B["ident"], io["ident"], P.new_sem("idl"))
        phase_l0_own(P, io, B, bf, f32, psum, compute_q=False)
        phase_l0_gate(P, io, B, bf, f32, psum)
        phase_l0_ffn(P, io, B, bf, f32, psum)
        phase_combine(P, io, B, bf, f32, psum, io["s_in"], x1_out, xkey="x_own")
        with nc.Block() as block:
            P.replay(block)
    return nc


def _build_C():
    nc = bass.Bass("TRN2", target_bir_lowering=False)
    io = {}
    for name, shape in (("x1s", (TOK, D)), ("rwT", (D, E1)), ("winT", (D, D)),
                        ("woutT", (D, D)), ("m1T", (D, MLPF)), ("m2T", (MLPF, D)),
                        ("loglam_c", (128, 8)), ("ln1g_c", (128, 8)), ("ln1b_c", (128, 8)),
                        ("ln2g_c", (128, 8)), ("ln2b_c", (128, 8)),
                        ("dskip_r", (1, D)), ("ln1g_r", (1, D)), ("ln1b_r", (1, D)),
                        ("b1_c", (128, 16)), ("b2_r", (1, D))):
        io[name] = dram_in(nc, name, shape, F32)
    for name in ("wg", "wu"):
        io[name] = dram_in(nc, name, (E1, D, FF1), F32)
    io["wd"] = dram_in(nc, "wd", (E1, FF1, D), F32)
    io["ident"] = dram_in(nc, "ident", (128, 128), BF16)
    io["selC"] = dram_in(nc, "selC", (128, TPC), BF16)
    out_ap = dram_out(nc, "out", (TOK, D), F32)
    x2s = nc.dram_tensor("x2s", [TOK, D], F32).ap()
    s1s = nc.dram_tensor("s1s", [TOK, D], F32).ap()
    cc_in = nc.dram_tensor("cc_in", [128, 128], F32).ap()
    cc_out = nc.dram_tensor("cc_out", [1024, 128], F32).ap()
    with ExitStack() as stack:
        P, bf, f32, psum = make_ctx(stack, nc)
        B = dict(ident=bf.alloc("ident", 128), xn1T=bf.alloc("xn1T", 8 * TOK),
                 cw_buf=f32.alloc("cw_buf", TPC * 8),
                 Cown=f32.alloc("Cown", 8 * TPC),
                 Ptab=f32.alloc("Ptab", 8 * 128),
                 oml=f32.alloc("oml", 8))
        P.dma("sync", B["ident"], io["ident"], P.new_sem("idl"))
        phase_l0_own(P, io, B, bf, f32, psum, compute_q=False, xkey="x1s", tkey="xn1T")
        phase_l1_router(P, io, B, bf, f32, psum)
        bf.mark()
        B["y_loc"] = bf.alloc("y_loc", 8 * TOK)
        phase_lru_scan(P, io, B, bf, f32, psum, cc_in, cc_out)
        phase_lru_out(P, io, B, bf, f32, psum, x2s)
        bf.release()
        phase_mlp(P, io, B, bf, f32, psum, x2s, s1s)
        B["y_acc"] = f32.alloc("y_acc", TPC * 1024)
        phase_l1_ffn(P, io, B, bf, f32, psum)
        phase_combine(P, io, B, bf, f32, psum, s1s, out_ap, xkey="x1s")
        with nc.Block() as block:
            P.replay(block)
    return nc


def kernel_3launch(**inputs):
    sh, per = host_prepare(inputs)
    own_rows = [np.concatenate([np.arange(t * 128, (t + 1) * 128) for t in own_tiles(c)])
                for c in range(NCORES)]
    cores = list(range(NCORES))
    A = lambda v: np.ascontiguousarray(v)

    nc = _build_A()
    maps = []
    for c in cores:
        maps.append({"x": sh["x"], "WkT": sh["WkT"], "WvT": sh["WvT"],
                     "WqT": sh["WqT"], "WoT": sh["WoT"], "ident": sh["ident"],
                     "identf": sh["identf"], "pek_c": sh["pek_c"],
                     "peqT": per[c]["peqT"], "x_own": per[c]["x_own"],
                     "mask8": per[c]["mask8"]})
    ra = run_bass_kernel_spmd(nc, maps, core_ids=cores)
    s_parts = [r["s_out"] for r in ra.results]

    nc = _build_B()
    maps = []
    for c in cores:
        maps.append({"x_own": per[c]["x_own"], "s_in": A(s_parts[c]),
                     "gwT": sh["gwT"], "ebias": sh["ebias"],
                     "w1T": sh["w1T"], "w2T": sh["w2T"], "w3T": sh["w3T"],
                     "sw1T": sh["sw1T"], "sw2T": sh["sw2T"], "sw3T": sh["sw3T"],
                     "ident": sh["ident"]})
    rb = run_bass_kernel_spmd(nc, maps, core_ids=cores)
    x1_parts = [r["x1"] for r in rb.results]

    nc = _build_C()
    maps = []
    for c in cores:
        maps.append({"x1s": A(x1_parts[c]), "rwT": sh["rwT"], "winT": sh["winT"],
                     "woutT": sh["woutT"], "m1T": sh["m1T"], "m2T": sh["m2T"],
                     "loglam_c": sh["loglam_c"], "ln1g_c": sh["ln1g_c"],
                     "ln1b_c": sh["ln1b_c"], "ln2g_c": sh["ln2g_c"],
                     "ln2b_c": sh["ln2b_c"], "dskip_r": sh["dskip_r"],
                     "ln1g_r": sh["ln1g_r"], "ln1b_r": sh["ln1b_r"],
                     "b1_c": sh["b1_c"], "b2_r": sh["b2_r"],
                     "wg": sh["wg"], "wu": sh["wu"], "wd": sh["wd"],
                     "ident": sh["ident"], "selC": per[c]["selC"]})
    rc = run_bass_kernel_spmd(nc, maps, core_ids=cores)

    out = np.zeros((T, D), dtype=np.float32)
    for c in cores:
        out[own_rows[c]] = rc.results[c]["out"]
    return out.reshape(1, T, D)



def _build_fused():
    nc = bass.Bass("TRN2", target_bir_lowering=False)
    io = {}
    for name, shape in (("x", (T, D)), ("WkT", (D, HD)), ("WvT", (D, HD)),
                        ("WqT", (D, D)), ("WoT", (D, D)), ("identf", (128, 128)),
                        ("pek_c", (HD, 1)), ("peqT", (128, TOK)),
                        ("x_own", (TOK, D)), ("mask8", (128, 8 * 512)),
                        ("gwT", (D, E0)), ("ebias", (1, E0)),
                        ("sw1T", (D, D)), ("sw2T", (D, D)), ("sw3T", (D, D)),
                        ("rwT", (D, E1)), ("winT", (D, D)), ("woutT", (D, D)),
                        ("m1T", (D, MLPF)), ("m2T", (MLPF, D)),
                        ("loglam_c", (128, 8)), ("ln1g_c", (128, 8)), ("ln1b_c", (128, 8)),
                        ("ln2g_c", (128, 8)), ("ln2b_c", (128, 8)),
                        ("dskip_r", (1, D)), ("ln1g_r", (1, D)), ("ln1b_r", (1, D)),
                        ("b1_c", (128, 16)), ("b2_r", (1, D))):
        io[name] = dram_in(nc, name, shape, F32)
    for name in ("w1T", "w2T", "w3T"):
        io[name] = dram_in(nc, name, (E0, D, D), F32)
    for name in ("wg", "wu"):
        io[name] = dram_in(nc, name, (E1, D, FF1), F32)
    io["wd"] = dram_in(nc, "wd", (E1, FF1, D), F32)
    io["ident"] = dram_in(nc, "ident", (128, 128), BF16)
    io["selC"] = dram_in(nc, "selC", (128, TPC), BF16)
    out_ap = dram_out(nc, "out", (TOK, D), F32)
    s_sc = nc.dram_tensor("s_sc", [TOK, D], F32).ap()
    x1_sc = nc.dram_tensor("x1_sc", [TOK, D], F32).ap()
    x2_sc = nc.dram_tensor("x2_sc", [TOK, D], F32).ap()
    s1_sc = nc.dram_tensor("s1_sc", [TOK, D], F32).ap()
    cc_in = nc.dram_tensor("cc_in", [128, 128], F32).ap()
    cc_out = nc.dram_tensor("cc_out", [1024, 128], F32).ap()
    io["x1s"] = x1_sc
    with ExitStack() as stack:
        P, bf, f32, psum = make_ctx(stack, nc)
        B = dict(ident=bf.alloc("ident", 128), xnT=bf.alloc("xnT", 8 * TOK))
        bf.mark()
        B["KT"] = bf.alloc("KT", NT * 128)
        B["Vb"] = bf.alloc("Vb", NT * 128)
        B["attn"] = bf.alloc("attn", TPC * 1024)
        phase_l0_front(P, io, B, bf, f32, psum)
        phase_wo(P, io, B, bf, f32, psum, s_sc)
        bf.release()
        bf.mark()
        B["g_buf"] = f32.alloc("g_buf", TPC * 16)
        B["y_acc"] = f32.alloc("y_acc", TPC * 1024)
        phase_l0_gate(P, io, B, bf, f32, psum)
        phase_moe2(P, io, B, bf, f32, psum, moe_specs_l0(io), "xnT")
        phase_combine(P, io, B, bf, f32, psum, s_sc, x1_sc, xkey="x_own")
        bf.release()
        B["xn1T"] = B["xnT"]
        phase_l0_own(P, io, B, bf, f32, psum, compute_q=False, xkey="x1s", tkey="xn1T")
        B["cw_buf"] = f32.alloc("cw_buf", TPC * 8)
        B["Cown"] = f32.alloc("Cown", 8 * TPC)
        B["Ptab"] = f32.alloc("Ptab", 8 * 128)
        B["oml"] = f32.alloc("oml", 8)
        phase_l1_router(P, io, B, bf, f32, psum)
        bf.mark()
        B["y_loc"] = bf.alloc("y_loc", 8 * TOK)
        phase_lru_scan(P, io, B, bf, f32, psum, cc_in, cc_out)
        phase_lru_out(P, io, B, bf, f32, psum, x2_sc)
        bf.release()
        phase_mlp(P, io, B, bf, f32, psum, x2_sc, s1_sc)
        bf.mark()
        B["y_acc"] = f32.alloc("y_acc", TPC * 1024)
        phase_moe2(P, io, B, bf, f32, psum, moe_specs_l1(io), "xn1T")
        phase_combine(P, io, B, bf, f32, psum, s1_sc, out_ap, xkey="x1s")
        bf.release()
        with nc.Block() as block:
            P.replay(block)
    return nc


def kernel(**inputs):
    sh, per = host_prepare(inputs)
    own_rows = [np.concatenate([np.arange(t * 128, (t + 1) * 128) for t in own_tiles(c)])
                for c in range(NCORES)]
    cores = list(range(NCORES))
    nc = _build_fused()
    shared_keys = ("x", "WkT", "WvT", "WqT", "WoT", "ident", "identf", "pek_c",
                   "gwT", "ebias", "w1T", "w2T", "w3T", "sw1T", "sw2T", "sw3T",
                   "rwT", "winT", "woutT", "m1T", "m2T", "loglam_c",
                   "ln1g_c", "ln1b_c", "ln2g_c", "ln2b_c", "dskip_r",
                   "ln1g_r", "ln1b_r", "b1_c", "b2_r", "wg", "wu", "wd")
    maps = []
    for c in cores:
        m = {k: sh[k] for k in shared_keys}
        m["peqT"] = per[c]["peqT"]
        m["x_own"] = per[c]["x_own"]
        m["mask8"] = per[c]["mask8"]
        m["selC"] = per[c]["selC"]
        maps.append(m)
    res = run_bass_kernel_spmd(nc, maps, core_ids=cores)
    out = np.zeros((T, D), dtype=np.float32)
    for c in cores:
        out[own_rows[c]] = res.results[c]["out"]
    return out.reshape(1, T, D)


def kernel_unfused(**inputs):
    sh, per = host_prepare(inputs)
    own_rows = [np.concatenate([np.arange(t * 128, (t + 1) * 128) for t in own_tiles(c)])
                for c in range(NCORES)]
    raise NotImplementedError



def phase_moe2(P, x_ap, B, bf, f32, psum, specs, xnT_key):
    vch, sch, gch = P.chain["vector"], P.chain["scalar"], P.chain["gpsimd"]
    pe = P.sem_pe
    xnT, y_acc = B[xnT_key], B["y_acc"]
    P.barrier()
    bf.mark(); f32.mark()
    wgb = [bf.alloc("wg0", 8 * 512), bf.alloc("wg1", 8 * 512)]
    wub = [bf.alloc("wu0", 8 * 512), bf.alloc("wu1", 8 * 512)]
    wdb = [bf.alloc("wd0", 4 * 1024), bf.alloc("wd1", 4 * 1024)]
    hhb = [bf.alloc("hh0", 4 * 2048), bf.alloc("hh1", 4 * 2048)]
    sil = [f32.alloc("sil0", 512), f32.alloc("sil1", 512)]
    P.emit("vector", lambda e: e.memset(y_acc[:, 0:8192], 0.0))
    P.emit("vector", lambda e: e.memset(y_acc[:, 8192:16384], 0.0))
    y_zeroed = vch.val
    wsem = [P.new_sem("mw0"), P.new_sem("mw1")]
    pu = [(psum["b2"], psum["b3"]), (psum["b4"], psum["b5"])]
    pd = [psum["b6"], psum["b7"]]
    pu_free = [None, None]
    pd_free = [None, None]
    dn_done = [0, 0]
    up_done = [0, 0]
    sil_free = [0, 0]
    nsl = len(specs)
    w_up_v = {}
    w_dn_v = {}

    def issue_gu(s):
        k = s % 2
        gcol, sg, su, sd = specs[s]
        dw = [(pe, up_done[k])] if up_done[k] else []
        P.dma("gpsimd", wgb[k].rearrange("p (a d) -> p a d", a=8),
              sg.rearrange("(a p) d -> p a d", p=128), wsem[k], waits=dw)
        P.dma("gpsimd", wub[k].rearrange("p (a d) -> p a d", a=8),
              su.rearrange("(a p) d -> p a d", p=128), wsem[k])
        w_up_v[s] = wsem[k].val

    def issue_d(s):
        k = s % 2
        gcol, sg, su, sd = specs[s]
        dw = [(pe, dn_done[k])] if dn_done[k] else []
        P.dma("gpsimd", wdb[k].rearrange("p (a d) -> p a d", a=4),
              sd.rearrange("(a p) d -> p a d", p=128), wsem[k], waits=dw)
        w_dn_v[s] = wsem[k].val

    issue_gu(0); issue_d(0)
    issue_gu(1); issue_d(1)
    for s in range(nsl):
        k = s % 2
        gcol = specs[s][0]
        w_up = w_up_v[s]
        w_dn = w_dn_v[s]
        ctr = 0
        for isub in range(4):
            for tc in range(4):
                b1, b3 = pu[ctr % 2]
                w = [(wsem[k], w_up)] if (isub == 0 and tc == 0) else []
                if pu_free[ctr % 2] is not None:
                    w.append(pu_free[ctr % 2])
                for et in range(8):
                    P.emit("tensor", lambda e, ww=wgb[k][:, et * 512 + isub * 128: et * 512 + isub * 128 + 128],
                           r=xnT[:, et * 2048 + tc * 512: et * 2048 + tc * 512 + 512],
                           b=b1, o=et:
                           e.matmul(b[:, 0:512], ww, r, start=(o == 0), stop=(o == 7)),
                           waits=(w if et == 0 else []), chain=False)
                for et in range(8):
                    P.emit("tensor", lambda e, ww=wub[k][:, et * 512 + isub * 128: et * 512 + isub * 128 + 128],
                           r=xnT[:, et * 2048 + tc * 512: et * 2048 + tc * 512 + 512],
                           b=b3, o=et:
                           e.matmul(b[:, 0:512], ww, r, start=(o == 0), stop=(o == 7)),
                           incs=([(pe, 1)] if et == 7 else []), chain=False)
                pe.val += 1
                P.emit("scalar", lambda e, b=b1, o=sil[ctr % 2]:
                       e.activation(o, b[:, 0:512], AF.Silu),
                       waits=[(pe, pe.val), (vch, sil_free[ctr % 2])])
                P.emit("vector", lambda e, b=b3, si=sil[ctr % 2],
                       d=hhb[k][:, isub * 2048 + tc * 512: isub * 2048 + tc * 512 + 512]:
                       e.tensor_tensor(d, si, b[:, 0:512], ALU.mult),
                       waits=[(sch, sch.val)])
                pu_free[ctr % 2] = (vch, vch.val)
                sil_free[ctr % 2] = vch.val
                ctr += 1
        hh_v = vch.val
        up_done[k] = pe.val
        if s + 2 < nsl:
            issue_gu(s + 2)
        for j in range(TPC):
            for n in (0, 1):
                bank = pd[n]
                for isub in range(4):
                    w = []
                    if isub == 0:
                        w = [(vch, hh_v)]
                        if j == 0 and n == 0:
                            w.append((wsem[k], w_dn))
                        if pd_free[n] is not None:
                            w.append(pd_free[n])
                    P.emit("tensor", lambda e, st=hhb[k][:, isub * 2048 + j * 128: isub * 2048 + j * 128 + 128],
                           ww=wdb[k][:, isub * 1024 + n * 512: isub * 1024 + n * 512 + 512],
                           b=bank, o=isub:
                           e.matmul(b[:, 0:512], st, ww, start=(o == 0), stop=(o == 3)),
                           waits=w, incs=([(pe, 1)] if isub == 3 else []), chain=False)
                pe.val += 1
                yw = [(pe, pe.val)]
                if s == 0:
                    yw.append((vch, y_zeroed))
                ysl = y_acc[:, j * 1024 + n * 512: j * 1024 + n * 512 + 512]
                if gcol is None:
                    P.emit("vector", lambda e, b=bank, d=ysl:
                           e.tensor_tensor(d, b[:, 0:512], d, ALU.add), waits=yw)
                else:
                    col = j * gcol[1] + gcol[2]
                    gc = B[gcol[0]][:, col:col + 1]
                    P.emit("vector", lambda e, b=bank, d=ysl, g=gc:
                           e.scalar_tensor_tensor(d, b[:, 0:512], g, d, ALU.mult, ALU.add),
                           waits=yw)
                pd_free[n] = (vch, vch.val)
        dn_done[k] = pe.val
        if s + 2 < nsl:
            issue_d(s + 2)
    bf.release(); f32.release()
    return dict()


def moe_specs_l0(x_ap, j_of=None):
    specs = []
    for e0 in range(E0):
        for sl in range(2):
            specs.append((("g_buf", 16, e0),
                          x_ap["w1T"][e0][:, sl * 512:(sl + 1) * 512],
                          x_ap["w3T"][e0][:, sl * 512:(sl + 1) * 512],
                          x_ap["w2T"][e0][sl * 512:(sl + 1) * 512, :]))
    for sl in range(2):
        specs.append((None,
                      x_ap["sw1T"][:, sl * 512:(sl + 1) * 512],
                      x_ap["sw3T"][:, sl * 512:(sl + 1) * 512],
                      x_ap["sw2T"][sl * 512:(sl + 1) * 512, :]))
    return specs


def moe_specs_l1(x_ap):
    specs = []
    for e0 in range(E1):
        for sl in range(8):
            specs.append((("cw_buf", 8, e0),
                          x_ap["wg"][e0][:, sl * 512:(sl + 1) * 512],
                          x_ap["wu"][e0][:, sl * 512:(sl + 1) * 512],
                          x_ap["wd"][e0][sl * 512:(sl + 1) * 512, :]))
    return specs



def phase_l0_front(P, x_ap, B, bf, f32, psum):
    vch, sch = P.chain["vector"], P.chain["scalar"]
    pe = P.sem_pe
    ident = B["ident"]
    KT, Vb, xnT, attn = B["KT"], B["Vb"], B["xnT"], B["attn"]
    P.barrier()
    bf.mark(); f32.mark()
    Qb = bf.alloc("Qb2", 2 * 1024)
    qslot_free = [0, 0]
    wk = bf.alloc("WkT", 8 * 128)
    wv = bf.alloc("WvT", 8 * 128)
    wq = bf.alloc("WqT", 8 * 1024)
    pek = f32.alloc("pek", 8)
    peq = bf.alloc("peqT", TOK)
    xt = [f32.alloc("xt0", 1024), f32.alloc("xt1", 1024)]
    scr = f32.alloc("scr", 1024)
    st = f32.alloc("stats", 8)
    xnb = [bf.alloc("xnb0", 1024), bf.alloc("xnb1", 1024)]
    xnt_t = [bf.alloc("xnt_t0", 1024)]
    wsem = P.new_sem("fw")
    P.dma("gpsimd", wk.rearrange("p (a d) -> p a d", a=8),
          x_ap["WkT"].rearrange("(a p) d -> p a d", p=128), wsem)
    P.dma("gpsimd", wv.rearrange("p (a d) -> p a d", a=8),
          x_ap["WvT"].rearrange("(a p) d -> p a d", p=128), wsem)
    P.dma("gpsimd", wq.rearrange("p (a d) -> p a d", a=8),
          x_ap["WqT"].rearrange("(a p) d -> p a d", p=128), wsem)
    P.dma("sync", pek[:, 0:1], x_ap["pek_c"], wsem)
    P.dma("gpsimd", peq, x_ap["peqT"], wsem)
    P.dma("sync", ident, x_ap["ident"], wsem)
    w_ready = wsem.val
    xsem = [P.new_sem("fx0"), P.new_sem("fx1")]
    osem = P.new_sem("fox")
    pT = [psum["bt0"], psum["bt1"]]
    pK, pV = psum["b2"], psum["b3"]
    pQ = [psum["b2"], psum["b3"]]
    xt_free = [0, 0]
    xnb_free = [0, 0]
    mask8 = bf.alloc("mask8", 8 * 512)
    stash = f32.alloc("stash", 8 * 512)
    probs = [[bf.alloc("pr00", 512), bf.alloc("pr01", 512)],
             [bf.alloc("pr10", 512), bf.alloc("pr11", 512)]]
    mcol = [f32.alloc("mcol0", 8), f32.alloc("mcol1", 8)]
    onescol = bf.alloc("onescol", 8)
    onerow = f32.alloc("onerow", 128)
    msc = f32.alloc("msc", 8)
    identf = f32.alloc("identf", 128)
    pacc = [f32.alloc("pacc0", 512), f32.alloc("pacc1", 512)]
    rb = stash[:, 2048:3072]
    pbf = [probs[0][0], probs[1][0]]
    P.dma("gpsimd", mask8, x_ap["mask8"], wsem)
    P.dma("sync", identf, x_ap["identf"], wsem)
    am_ready = wsem.val
    P.emit("vector", lambda e: e.memset(onescol, 1.0))
    P.emit("vector", lambda e: e.memset(onerow[0:1, :], 1.0))
    ones_ready = vch.val
    scb = [psum["b0"], psum["b1"], psum["b2"], psum["b3"]]
    pa = [psum["b4"], psum["b5"]]
    pl = [psum["b6"], psum["b7"]]
    bfree = [None, None, None, None]
    plfree = [None, None]
    pa_free = [None, None]
    probs_free = [[0, 0], [0, 0]]
    probs_accv = [[0, 0], [0, 0]]

    kvs_l = {}
    kvv_l = {}
    qrdy_l = {}

    def emit_prep(j):
        for p8 in range(8):
            i = 8 * j + p8
            k = i % 2
            P.dma("sync", xt[k], x_ap["x"][i * 128:(i + 1) * 128, :], xsem[k],
                  waits=[(vch, xt_free[k])])
            nw = ln_tile(P, xt[k], xnb[k], st, scr,
                         waits=[(xsem[k], xsem[k].val)],
                         extra_norm_waits=[(pe, xnb_free[k])])
            xt_free[k] = vch.val
            dsts = [xnt_t[0][:, et * 128:(et + 1) * 128] for et in range(8)]
            tev = [bfree[0], bfree[1]]
            ev = transpose8(P, xnb[k], dsts, ident, pT, tev, (vch, nw),
                            extra_first=([(wsem, w_ready)] if i == 0 else []))
            bfree[0], bfree[1] = tev
            xnb_free[k] = pe.val
            for et in range(8):
                kw = list(ev) if et == 0 else []
                if et == 0 and bfree[2] is not None:
                    kw.append(bfree[2])
                P.emit("tensor", lambda e, w=wk[:, et * 128:(et + 1) * 128], r=dsts[et], o=et:
                       e.matmul(pK[:, 0:128], w, r, start=(o == 0), stop=(o == 7)),
                       waits=kw, incs=([(pe, 1)] if et == 7 else []), chain=False)
            pe.val += 1
            P.emit("scalar", lambda e, d=KT[:, i * 128:(i + 1) * 128]:
                   e.add(d, pK[:, 0:128], pek[:, 0:1]),
                   waits=[(pe, pe.val)])
            bfree[2] = (sch, sch.val)
            for et in range(8):
                vw = [bfree[3]] if (et == 0 and bfree[3] is not None) else []
                P.emit("tensor", lambda e, s=dsts[et], w=wv[:, et * 128:(et + 1) * 128], o=et:
                       e.matmul(pV[:, 0:128], s, w, start=(o == 0), stop=(o == 7)),
                       waits=vw, incs=([(pe, 1)] if et == 7 else []), chain=False)
            pe.val += 1
            P.emit("vector", lambda e, d=Vb[:, i * 128:(i + 1) * 128]: e.tensor_copy(d, pV[:, 0:128]),
                   waits=[(pe, pe.val)])
            bfree[3] = (vch, vch.val)
        P.dma("sync", xt[0], x_ap["x_own"][j * 128:(j + 1) * 128, :], osem,
              waits=[(vch, xt_free[0])])
        nw = ln_tile(P, xt[0], xnb[0], st, scr,
                     waits=[(osem, osem.val)],
                     extra_norm_waits=[(pe, xnb_free[0])])
        xt_free[0] = vch.val
        odsts = [xnT[:, et * 2048 + j * 128: et * 2048 + j * 128 + 128] for et in range(8)]
        tev = [bfree[0], bfree[1]]
        oev = transpose8(P, xnb[0], odsts, ident, pT, tev, (vch, nw))
        bfree[0], bfree[1] = tev
        xnb_free[0] = pe.val
        for h in range(NH):
            bank = pQ[h % 2]
            for et in range(8):
                qw = list(oev) if et == 0 else []
                if et == 0 and bfree[2 + h % 2] is not None:
                    qw.append(bfree[2 + h % 2])
                P.emit("tensor", lambda e, w=wq[:, et * 1024 + h * 128: et * 1024 + h * 128 + 128],
                       r=odsts[et], o=et, b=bank:
                       e.matmul(b[:, 0:128], w, r, start=(o == 0), stop=(o == 7)),
                       waits=qw, incs=([(pe, 1)] if et == 7 else []), chain=False)
            pe.val += 1
            P.emit("vector", lambda e, b=bank,
                   d=Qb[:, (j % 2) * 1024 + h * 128: (j % 2) * 1024 + h * 128 + 128],
                   p=peq[:, j * 128:(j + 1) * 128]:
                   e.tensor_tensor(d, b[:, 0:128], p, ALU.add),
                   waits=[(pe, pe.val), (pe, qslot_free[j % 2])])
            bfree[2 + h % 2] = (vch, vch.val)
        q_ready = vch.val
        kvs_l[j] = sch.val
        kvv_l[j] = vch.val
        qrdy_l[j] = vch.val

    def emit_attn(j):
        kv_s = kvs_l[j]
        kv_v = kvv_l[j]
        q_ready = qrdy_l[j]
        nmain = 8 * j
        for hg in (0, 1):
            qsl = Qb[:, (j % 2) * 1024 + hg * 512: (j % 2) * 1024 + (hg + 1) * 512]
            for p in range(8):
                bank = scb[p % 4]
                w = [(sch, kv_s), (vch, q_ready)]
                if j == 0 and hg == 0 and p < 4:
                    w.append((wsem, am_ready))
                if bfree[p % 4] is not None:
                    w.append(bfree[p % 4])
                P.emit("tensor", lambda e, kt=KT[:, (8 * j + p) * 128:(8 * j + p + 1) * 128],
                       q=qsl, b=bank: e.matmul(b[:, 0:512], kt, q, start=True, stop=True),
                       waits=w, incs=[(pe, 1)], chain=False)
                pe.val += 1
                P.emit("vector", lambda e, b=bank, o=stash[:, p * 512:(p + 1) * 512],
                       mk=mask8[:, p * 512:(p + 1) * 512]:
                       e.tensor_tensor(o, b[:, 0:512], mk, ALU.add),
                       waits=[(pe, pe.val)])
                bfree[p % 4] = (vch, vch.val)
            P.emit("vector", lambda e: e.reduce_max(msc[:, 0:1], stash, axis=AX.X))
            mx_v = vch.val
            tw = [(vch, mx_v)]
            if j == 0 and hg == 0:
                tw.append((wsem, am_ready))
            if plfree[0] is not None:
                tw.append(plfree[0])
            P.emit("tensor", lambda e: e.transpose(pl[0][0:1, 0:128], msc[:, 0:1], identf),
                   waits=tw, incs=[(pe, 1)], chain=False)
            pe.val += 1
            P.emit("vector", lambda e: e.reduce_max(msc[0:1, 1:2], pl[0][0:1, 0:128], axis=AX.X),
                   waits=[(pe, pe.val)])
            P.emit("vector", lambda e: e.tensor_scalar(
                msc[0:1, 3:4], msc[0:1, 1:2], -SCALE, None, ALU.mult))
            mneg_v = vch.val
            P.emit("tensor", lambda e: e.matmul(pl[0][:, 33:34], onerow[0:1, :],
                   msc[0:1, 3:4], start=True, stop=True),
                   waits=[(vch, mneg_v), (vch, ones_ready)], incs=[(pe, 1)], chain=False)
            pe.val += 1
            P.emit("vector", lambda e, d=mcol[hg][:, 0:1]: e.tensor_copy(d, pl[0][:, 33:34]),
                   waits=[(pe, pe.val)])
            plfree[0] = (vch, vch.val)
            mc_ready = vch.val
            for p in range(8):
                pb = probs[hg][p % 2]
                P.emit("scalar", lambda e, o=pb, i=stash[:, p * 512:(p + 1) * 512],
                       m=mcol[hg][:, 0:1]:
                       e.activation(o, i, AF.Exp, bias=m, scale=SCALE),
                       waits=[(vch, mc_ready), (pe, probs_free[hg][p % 2]),
                              (vch, probs_accv[hg][p % 2])])
                first = (p == 0)
                last = (j == 0 and p == 7)
                if hg == 0:
                    if first:
                        P.emit("vector", lambda e, a=pacc[0], pbuf=pb: e.tensor_copy(a, pbuf),
                               waits=[(sch, sch.val)])
                    else:
                        P.emit("vector", lambda e, a=pacc[0], pbuf=pb:
                               e.tensor_tensor(a, a, pbuf, ALU.add),
                               waits=[(sch, sch.val)])
                    acc_v = vch.val
                else:
                    lw = [(sch, sch.val)]
                    if first and plfree[1] is not None:
                        lw.append(plfree[1])
                    P.emit("tensor", lambda e, o=pl[1][0:1, 0:512], pbuf=pb, f=first, l=last:
                           e.matmul(o, onescol[:, 0:1], pbuf, start=f, stop=l),
                           waits=lw, chain=False)
                    acc_v = 0
                pw = [(sch, sch.val), (vch, kv_v)]
                if first and pa_free[hg] is not None:
                    pw.append(pa_free[hg])
                P.emit("tensor", lambda e, v=Vb[:, (8 * j + p) * 128:(8 * j + p + 1) * 128],
                       pbuf=pb, b=pa[hg], f=first, l=last:
                       e.matmul(b[:, 0:512], v, pbuf, start=f, stop=l),
                       waits=pw, incs=[(pe, 1)], chain=False)
                pe.val += 1
                probs_free[hg][p % 2] = pe.val
                probs_accv[hg][p % 2] = acc_v
        exp_done = {}
        for kb in range(nmain + 1):
            if kb < nmain:
                for hg in (0, 1):
                    bi = 2 * hg + kb % 2
                    bank = scb[bi]
                    qsl = Qb[:, (j % 2) * 1024 + hg * 512: (j % 2) * 1024 + (hg + 1) * 512]
                    w = [bfree[bi]] if bfree[bi] is not None else []
                    P.emit("tensor", lambda e, kt=KT[:, kb * 128:(kb + 1) * 128], q=qsl, b=bank:
                           e.matmul(b[:, 0:512], kt, q, start=True, stop=True),
                           waits=w, incs=[(pe, 1)], chain=False)
                    pe.val += 1
                    P.emit("scalar", lambda e, b=bank, o=probs[hg][kb % 2], m=mcol[hg][:, 0:1]:
                           e.activation(o, b[:, 0:512], AF.Exp, bias=m, scale=SCALE),
                           waits=[(pe, pe.val), (pe, probs_free[hg][kb % 2]),
                                  (vch, probs_accv[hg][kb % 2])])
                    bfree[bi] = (sch, sch.val)
                    exp_done[(kb, hg)] = (sch, sch.val)
            if kb > 0:
                kp = kb - 1
                last = (kp == nmain - 1)
                for hg in (0, 1):
                    pb = probs[hg][kp % 2]
                    if hg == 0:
                        P.emit("vector", lambda e, a=pacc[0], pbuf=pb:
                               e.tensor_tensor(a, a, pbuf, ALU.add),
                               waits=[exp_done[(kp, hg)]])
                        probs_accv[0][kp % 2] = vch.val
                    else:
                        P.emit("tensor", lambda e, o=pl[1][0:1, 0:512], pbuf=pb,
                               l=(kp == nmain - 1):
                               e.matmul(o, onescol[:, 0:1], pbuf, start=False, stop=l),
                               waits=[exp_done[(kp, hg)]], chain=False)
                        probs_accv[1][kp % 2] = 0
                    P.emit("tensor", lambda e, v=Vb[:, kp * 128:(kp + 1) * 128],
                           pbuf=pb, b=pa[hg], l=last:
                           e.matmul(b[:, 0:512], v, pbuf, start=False, stop=l),
                           waits=[exp_done[(kp, hg)]], incs=[(pe, 1)], chain=False)
                    pe.val += 1
                    probs_free[hg][kp % 2] = pe.val
        P.emit("vector", lambda e, d=pbf[0], a=pacc[0]: e.tensor_copy(d, a),
               waits=[(pe, probs_free[0][0])])
        cast_v = vch.val
        lw = [(vch, cast_v), (vch, ones_ready)]
        if plfree[0] is not None:
            lw.append(plfree[0])
        P.emit("tensor", lambda e, o=pl[0][0:1, 0:512], pbuf=pbf[0]:
               e.matmul(o, onescol[:, 0:1], pbuf, start=True, stop=True),
               waits=lw, incs=[(pe, 1)], chain=False)
        pe.val += 1
        probs_free[0][0] = pe.val
        P.emit("tensor", lambda e: e.nop(), incs=[(pe, 1)], chain=False)
        pe.val += 1
        l_done = pe.val
        P.emit("vector", lambda e: e.reciprocal(stash[0:1, 0:512], pl[0][0:1, 0:512]),
               waits=[(pe, l_done)])
        P.emit("vector", lambda e: e.reciprocal(stash[0:1, 512:1024], pl[1][0:1, 0:512]))
        rec_v = vch.val
        plfree[0] = (vch, rec_v)
        plfree[1] = (vch, rec_v)
        for hg in (0, 1):
            bw = [(vch, rec_v), plfree[hg]]
            P.emit("tensor", lambda e, r=stash[0:1, hg * 512:(hg + 1) * 512], b=pl[hg]:
                   e.matmul(b[:, 0:512], onerow[0:1, :], r, start=True, stop=True),
                   waits=bw, incs=[(pe, 1)], chain=False)
            pe.val += 1
            P.emit("scalar", lambda e, d=rb[:, hg * 512:(hg + 1) * 512], b=pl[hg]:
                   e.copy(d, b[:, 0:512]), waits=[(pe, pe.val)])
            plfree[hg] = (sch, sch.val)
            P.emit("vector", lambda e, b=pa[hg], r=rb[:, hg * 512:(hg + 1) * 512],
                   d=attn[:, j * 1024 + hg * 512: j * 1024 + (hg + 1) * 512]:
                   e.tensor_tensor(d, b[:, 0:512], r, ALU.mult),
                   waits=[(sch, sch.val)])
            pa_free[hg] = (vch, vch.val)
        qslot_free[j % 2] = pe.val

    for jj in range(TPC + 1):
        if jj < TPC:
            emit_prep(jj)
        if jj > 0:
            emit_attn(jj - 1)
    bf.release(); f32.release()
    return dict()


# version 25
# speedup vs baseline: 1.0805x; 1.0042x over previous
import math
from contextlib import ExitStack

import numpy as np
import ml_dtypes
import concourse.bass as bass
import concourse.mybir as mybir
from concourse.bass_utils import run_bass_kernel_spmd

F32 = mybir.dt.float32
BF16 = mybir.dt.bfloat16
AF = mybir.ActivationFunctionType
ALU = mybir.AluOpType
AX = mybir.AxisListType

T = 16384
D = 1024
NT = T // 128
NCORES = 8
TPC = NT // NCORES
TOK = TPC * 128
NH = 8
HD = 128
SCALE = 1.0 / math.sqrt(HD)
E0 = 16
K0 = 4
E1 = 8
K1 = 2
FF1 = 4096
MLPF = 2048
LNEPS = 1e-5
MASKNEG = -30000.0

DK = D // 128


def own_tiles(c):
    return [c + NCORES * j for j in range(TPC)]



def _pe_table(length, dim):
    pos = np.arange(length, dtype=np.float32)[:, None]
    inv = np.exp(-np.log(np.float32(10000.0)) * (np.arange(0, dim, 2, dtype=np.float32) / dim))
    ang = pos * inv[None, :]
    pe = np.zeros((length, dim), dtype=np.float32)
    pe[:, 0::2] = np.sin(ang)
    pe[:, 1::2] = np.cos(ang)
    return pe


def host_prepare(inp):
    tt = lambda a: np.ascontiguousarray(np.swapaxes(a, -1, -2))
    sh = {}
    sh["x"] = np.ascontiguousarray(inp["L0_attn_x"].reshape(T, D))
    sh["WqT"] = tt(inp["L0_attn_Wq"])
    sh["WkT"] = tt(inp["L0_attn_Wk"])
    sh["WvT"] = tt(inp["L0_attn_Wv"])
    sh["WoT"] = tt(inp["L0_attn_Wo"])
    sh["gwT"] = tt(inp["L0_ffn_gate_w"])
    sh["ebias"] = inp["L0_ffn_expert_bias"].reshape(1, E0)
    sh["w1T"] = tt(inp["L0_ffn_w1"])
    sh["w2T"] = tt(inp["L0_ffn_w2"])
    sh["w3T"] = tt(inp["L0_ffn_w3"])
    sh["sw1T"] = tt(inp["L0_ffn_sw1"])
    sh["sw2T"] = tt(inp["L0_ffn_sw2"])
    sh["sw3T"] = tt(inp["L0_ffn_sw3"])
    sh["winT"] = tt(inp["L1_attn_w_in"])
    sh["woutT"] = tt(inp["L1_attn_w_out"])
    sh["m1T"] = tt(inp["L1_attn_w1"])
    sh["m2T"] = tt(inp["L1_attn_w2"])
    col = lambda v: np.ascontiguousarray(v.reshape(-1, 128).T)
    sh["loglam_c"] = col(inp["L1_attn_log_lambda"])
    sh["dskip_r"] = inp["L1_attn_D_skip"].reshape(1, D)
    sh["ln1g_c"] = col(inp["L1_attn_ln1_g"])
    sh["ln1b_c"] = col(inp["L1_attn_ln1_b"])
    sh["ln1g_r"] = inp["L1_attn_ln1_g"].reshape(1, D)
    sh["ln1b_r"] = inp["L1_attn_ln1_b"].reshape(1, D)
    sh["ln2g_c"] = col(inp["L1_attn_ln2_g"])
    sh["ln2b_c"] = col(inp["L1_attn_ln2_b"])
    sh["b1_c"] = col(inp["L1_attn_b1"])
    sh["b2_r"] = inp["L1_attn_b2"].reshape(1, D)
    sh["rwT"] = tt(inp["L1_ffn_router_w"])
    sh["wg"] = inp["L1_ffn_w_gate"]
    sh["wu"] = inp["L1_ffn_w_up"]
    sh["wd"] = inp["L1_ffn_w_down"]
    sh["ident"] = np.eye(128, dtype=ml_dtypes.bfloat16)
    sh["pek_c"] = _pe_table(1, HD).reshape(HD, 1)
    m = np.where(np.arange(128)[:, None] > np.arange(128)[None, :], MASKNEG, 0.0).astype(np.float32)
    sh["maskdiag"] = np.ascontiguousarray(np.tile(m, (1, 4)))
    sh["identf"] = np.eye(128, dtype=np.float32)
    per = []
    pe = _pe_table(T, HD)
    diag = np.tile(m, (1, 4))
    for c in range(NCORES):
        d = {}
        rows = np.concatenate([np.arange(t * 128, (t + 1) * 128) for t in own_tiles(c)])
        d["peqT"] = np.ascontiguousarray(pe[rows].T)
        d["x_own"] = np.ascontiguousarray(sh["x"][rows])
        mk = np.zeros((128, 8 * 512), dtype=np.float32)
        for p in range(8):
            if p == c:
                mk[:, p * 512:(p + 1) * 512] = diag
            elif p > c:
                mk[:, p * 512:(p + 1) * 512] = MASKNEG
        d["mask8"] = mk
        selc = np.zeros((128, TPC), dtype=ml_dtypes.bfloat16)
        for j in range(TPC):
            g = c + 8 * j - 1
            if g >= 0:
                selc[g, j] = 1.0
        d["selC"] = selc
        per.append(d)
    return sh, per



ENGS = ("sync", "tensor", "vector", "scalar", "gpsimd")


class Sem:
    __slots__ = ("h", "val")

    def __init__(self, h):
        self.h = h
        self.val = 0


class Prog:

    def __init__(self, nc, stack):
        self.nc = nc
        self.stack = stack
        self.ops = {e: [] for e in ENGS}
        self._semid = 0
        self.chain = {e: self.new_sem(f"chain_{e}") for e in ("vector", "scalar", "gpsimd")}
        self.nops = 0

    def new_sem(self, name=None):
        self._semid += 1
        nm = f"{name or 's'}_{self._semid}"
        h = self.stack.enter_context(self.nc.semaphore(nm))
        return Sem(h)

    def barrier(self):
        snap = [(self.chain["vector"], self.chain["vector"].val),
                (self.chain["scalar"], self.chain["scalar"].val),
                (self.chain["gpsimd"], self.chain["gpsimd"].val),
                (self.sem_pe, self.sem_pe.val)]
        self._barrier = {e: snap for e in ENGS}

    def emit(self, eng, fn, waits=(), incs=(), chain=True):
        pend = getattr(self, "_barrier", {}).pop(eng, None)
        waits = (list(pend) if pend else []) + list(waits)
        incs = list(incs)
        if eng in self.chain and chain:
            ch = self.chain[eng]
            waits.append((ch, ch.val))
            ch.val += 1
            incs.append((ch, 1))
        self.ops[eng].append((waits, fn, incs))
        self.nops += 1

    def dma(self, eng, out, in_, sem, waits=()):
        sem.val += 16
        self.emit(eng, lambda e, o=out, i=in_: e.dma_start(out=o, in_=i),
                  waits=waits, incs=[(sem, 16)], chain=False)
        return sem.val

    def start_phase(self):
        self.barrier()

    def replay(self, block):
        for eng in ENGS:
            ops = self.ops[eng]
            if not ops:
                continue

            def make(ops):
                def body(e):
                    for waits, fn, incs in ops:
                        for s, v in waits[:-1]:
                            e.wait_ge(s.h, v)
                        inst = fn(e)
                        if inst is None:
                            assert not waits and not incs
                            continue
                        if waits:
                            s, v = waits[-1]
                            if v > 0:
                                inst._wait_ge(s.h, v)
                        for s, v in incs:
                            inst.then_inc(s.h, v)
                return body
            getattr(block, eng)(make(ops))


class ArenaBase:

    def __init__(self, handle, width):
        self.h = handle
        self.width = width
        self.off = 0
        self.marks = []
        self.peak = 0

    def alloc_cols(self, name, bfcols, align=32):
        off = (self.off + align - 1) // align * align
        assert off + bfcols <= self.width, (
            f"arena overflow: {name} needs {bfcols} at {off}, width {self.width}")
        self.off = off + bfcols
        self.peak = max(self.peak, self.off)
        return off

    def mark(self):
        self.marks.append(self.off)

    def release(self):
        self.off = self.marks.pop()


class Arena:
    def __init__(self, base, dt):
        self.base = base
        self.dt = dt

    def alloc(self, name, cols, align=32):
        n = cols * (2 if self.dt == F32 else 1)
        off = self.base.alloc_cols(name, n, align)
        v = self.base.h[:, off:off + n]
        return v.bitcast(F32) if self.dt == F32 else v

    def mark(self):
        self.base.mark()

    def release(self):
        self.base.release()


def dram_in(nc, name, shape, dt=F32):
    return nc.dram_tensor(name, list(shape), dt, kind="ExternalInput").ap()


def dram_out(nc, name, shape, dt=F32):
    return nc.dram_tensor(name, list(shape), dt, kind="ExternalOutput").ap()




def ln_tile(P, src, xnb_out, st, scr, waits=(), extra_norm_waits=()):
    vch, sch = P.chain["vector"], P.chain["scalar"]
    P.emit("vector", lambda e, a=src: e.reduce_sum(st[:, 0:1], a, axis=AX.X), waits=list(waits))
    P.emit("vector", lambda e, a=src: e.scalar_tensor_tensor(
        scr, a, 1.0, a, ALU.mult, ALU.mult, accum_out=st[:, 1:2]))
    P.emit("vector", lambda e: e.tensor_scalar(st[:, 2:3], st[:, 0:1], 1.0 / D, None, ALU.mult))
    P.emit("vector", lambda e: e.tensor_tensor(st[:, 3:4], st[:, 2:3], st[:, 2:3], ALU.mult))
    P.emit("vector", lambda e: e.scalar_tensor_tensor(
        st[:, 4:5], st[:, 1:2], 1.0 / D, st[:, 3:4], ALU.mult, ALU.subtract))
    P.emit("scalar", lambda e: e.activation(st[:, 5:6], st[:, 4:5], AF.Ln, bias=P.epscol),
           waits=[(vch, vch.val)])
    P.emit("scalar", lambda e: e.activation(st[:, 6:7], st[:, 5:6], AF.Exp, scale=-0.5))
    P.emit("vector", lambda e, a=src, o=xnb_out: e.tensor_scalar(
        o, a, st[:, 2:3], st[:, 6:7], ALU.subtract, ALU.mult),
        waits=[(sch, sch.val)] + list(extra_norm_waits))
    return vch.val


def transpose8(P, xnb, dst_slices, ident, pT, tev, norm_wait, extra_first=(), gb=None):
    pe = P.sem_pe
    ev = []
    for et in range(8):
        bank = pT[et % 2]
        tw = [norm_wait] + (list(extra_first) if et == 0 else [])
        if tev[et % 2] is not None:
            tw.append(tev[et % 2])
        P.emit("tensor", lambda e, b=bank, s=xnb[:, et * 128:(et + 1) * 128], ii=ident:
               e.transpose(b[:, 0:128], s, ii), waits=tw, incs=[(P.sem_pe, 1)], chain=False)
        pe.val += 1
        if gb is None:
            eng = "vector" if et % 2 == 0 else "scalar"
            P.emit(eng, lambda e, b=bank, d=dst_slices[et]:
                   (e.tensor_copy if hasattr(e, "tensor_copy") else e.copy)(d, b[:, 0:128]),
                   waits=[(pe, pe.val)])
        else:
            eng = "vector"
            g, bb = gb
            P.emit(eng, lambda e, b=bank, d=dst_slices[et],
                   gc=g[:, et:et + 1], bc=bb[:, et:et + 1]:
                   e.tensor_scalar(d, b[:, 0:128], gc, bc, ALU.mult, ALU.add),
                   waits=[(pe, pe.val)])
        tev[et % 2] = (P.chain[eng], P.chain[eng].val)
        ev.append(tev[et % 2])
    return ev


def phase_l0_kv(P, x_ap, B, bf, f32, psum):
    vch, sch = P.chain["vector"], P.chain["scalar"]
    KT, Vb, ident = B["KT"], B["Vb"], B["ident"]
    P.barrier()
    bf.mark(); f32.mark()
    wk = bf.alloc("WkT", 8 * 128)
    wv = bf.alloc("WvT", 8 * 128)
    pek = f32.alloc("pek", 8)

    wsem = P.new_sem("l0w")
    P.dma("gpsimd", wk.rearrange("p (a d) -> p a d", a=8),
          x_ap["WkT"].rearrange("(a p) d -> p a d", p=128), wsem)
    P.dma("gpsimd", wv.rearrange("p (a d) -> p a d", a=8),
          x_ap["WvT"].rearrange("(a p) d -> p a d", p=128), wsem)
    P.dma("sync", ident, x_ap["ident"], wsem)
    P.dma("sync", pek[:, 0:1], x_ap["pek_c"], wsem)
    w_ready = wsem.val

    xt = [f32.alloc("xt0", 1024), f32.alloc("xt1", 1024)]
    scr = f32.alloc("scr", 1024)
    st = f32.alloc("stats", 8)
    xnb = [bf.alloc("xnb0", 1024), bf.alloc("xnb1", 1024)]
    xnt_t = [bf.alloc("xnt_t0", 1024), bf.alloc("xnt_t1", 1024)]
    xsem = [P.new_sem(f"l0x{k}") for k in range(2)]
    pT = [psum["bt0"], psum["bt1"]]
    pK, pV = psum["b2"], psum["b3"]
    pe = P.sem_pe

    xt_free = [0, 0]
    xnb_free = [0, 0]
    tev = [None, None]
    kev = vev = None
    for i in range(NT):
        k = i % 2
        P.dma("sync", xt[k], x_ap["x"][i * 128:(i + 1) * 128, :], xsem[k],
              waits=[(vch, xt_free[k])])
        nw = ln_tile(P, xt[k], xnb[k], st, scr,
                     waits=[(xsem[k], xsem[k].val)],
                     extra_norm_waits=[(pe, xnb_free[k])])
        xt_free[k] = vch.val
        dsts = [xnt_t[k][:, et * 128:(et + 1) * 128] for et in range(8)]
        ev = transpose8(P, xnb[k], dsts, ident, pT, tev, (vch, nw),
                        extra_first=([(wsem, w_ready)] if i == 0 else []))
        xnb_free[k] = pe.val
        for et in range(8):
            kw = list(ev) if et == 0 else []
            if et == 0 and kev is not None:
                kw.append(kev)
            P.emit("tensor", lambda e, w=wk[:, et * 128:(et + 1) * 128], r=dsts[et], o=et:
                   e.matmul(pK[:, 0:128], w, r, start=(o == 0), stop=(o == 7)),
                   waits=kw, incs=([(pe, 1)] if et == 7 else []), chain=False)
        pe.val += 1
        P.emit("scalar", lambda e, d=KT[:, i * 128:(i + 1) * 128]:
               e.add(d, pK[:, 0:128], pek[:, 0:1]),
               waits=[(pe, pe.val)])
        kev = (sch, sch.val)
        for et in range(8):
            vw = [vev] if (et == 0 and vev is not None) else []
            P.emit("tensor", lambda e, s=dsts[et], w=wv[:, et * 128:(et + 1) * 128], o=et:
                   e.matmul(pV[:, 0:128], s, w, start=(o == 0), stop=(o == 7)),
                   waits=vw, incs=([(pe, 1)] if et == 7 else []), chain=False)
        pe.val += 1
        P.emit("vector", lambda e, d=Vb[:, i * 128:(i + 1) * 128]: e.tensor_copy(d, pV[:, 0:128]),
               waits=[(pe, pe.val)])
        vev = (vch, vch.val)
    bf.release(); f32.release()
    return dict()


def phase_l0_own(P, x_ap, B, bf, f32, psum, compute_q=True, xkey="x_own", tkey="xnT"):
    vch, sch = P.chain["vector"], P.chain["scalar"]
    ident, xnT = B["ident"], B[tkey]
    Qb = B.get("Qb")
    P.barrier()
    bf.mark(); f32.mark()
    wsem = P.new_sem("l0qw")
    if compute_q:
        wq = bf.alloc("WqT", 8 * 1024)
        peq = f32.alloc("peqT", TOK)
        P.dma("gpsimd", wq.rearrange("p (a d) -> p a d", a=8),
              x_ap["WqT"].rearrange("(a p) d -> p a d", p=128), wsem)
        P.dma("sync", peq, x_ap["peqT"], wsem)
    w_ready = wsem.val

    xt = [f32.alloc("xt0", 1024), f32.alloc("xt1", 1024)]
    scr = f32.alloc("scr", 1024)
    st = f32.alloc("stats", 8)
    xnb = [bf.alloc("xnb0", 1024), bf.alloc("xnb1", 1024)]
    xsem = [P.new_sem(f"l0q{k}") for k in range(2)]
    pT = [psum["bt0"], psum["bt1"]]
    pQ = [psum["b4"], psum["b5"]]
    pe = P.sem_pe

    xt_free = [0, 0]
    xnb_free = [0, 0]
    tev = [None, None]
    qev = [None, None]
    for j in range(TPC):
        k = j % 2
        P.dma("sync", xt[k], x_ap[xkey][j * 128:(j + 1) * 128, :], xsem[k],
              waits=[(vch, xt_free[k])])
        nw = ln_tile(P, xt[k], xnb[k], st, scr,
                     waits=[(xsem[k], xsem[k].val)],
                     extra_norm_waits=[(pe, xnb_free[k])])
        xt_free[k] = vch.val
        dsts = [xnT[:, et * 2048 + j * 128: et * 2048 + j * 128 + 128] for et in range(8)]
        ev = transpose8(P, xnb[k], dsts, ident, pT, tev, (vch, nw),
                        extra_first=([(wsem, w_ready)] if j == 0 else []))
        xnb_free[k] = pe.val
        for h in (range(NH) if compute_q else ()):
            bank = pQ[h % 2]
            for et in range(8):
                qw = list(ev) if et == 0 else []
                if et == 0 and qev[h % 2] is not None:
                    qw.append(qev[h % 2])
                P.emit("tensor", lambda e, w=wq[:, et * 1024 + h * 128: et * 1024 + h * 128 + 128],
                       r=dsts[et], o=et, b=bank:
                       e.matmul(b[:, 0:128], w, r, start=(o == 0), stop=(o == 7)),
                       waits=qw, incs=([(pe, 1)] if et == 7 else []), chain=False)
            pe.val += 1
            P.emit("vector", lambda e, b=bank,
                   d=Qb[:, j * 1024 + h * 128: j * 1024 + h * 128 + 128],
                   p=peq[:, j * 128:(j + 1) * 128]:
                   e.tensor_tensor(d, b[:, 0:128], p, ALU.add),
                   waits=[(pe, pe.val)])
            qev[h % 2] = (P.chain["vector"], P.chain["vector"].val)
    bf.release(); f32.release()
    return dict()



ARENA_COLS = 106368


def make_ctx(stack, nc):
    P = Prog(nc, stack)
    P.sem_pe = P.new_sem("pe")
    bfh = stack.enter_context(nc.sbuf_tensor("arena", [128, ARENA_COLS], BF16))
    base = ArenaBase(bfh, ARENA_COLS)
    f32 = Arena(base, F32)
    bf = Arena(base, BF16)
    consts = f32.alloc("consts", 8)
    P.epscol = consts[:, 0:1]
    P.onecol = consts[:, 1:2]
    P.emit("vector", lambda e: e.memset(P.epscol, LNEPS))
    P.emit("vector", lambda e: e.memset(P.onecol, 1.0))
    psum = {}
    for nm in ("b0", "b1", "b2", "b3", "b4", "b5", "b6", "b7"):
        psum[nm] = stack.enter_context(nc.psum_tensor(nm, [128, 512], F32))
    psum["bt0"] = psum["b0"][:].bitcast(BF16)
    psum["bt1"] = psum["b1"][:].bitcast(BF16)
    return P, bf, f32, psum


def declare_inputs(nc, names_shapes):
    io = {}
    for name, arr in names_shapes.items():
        io[name] = dram_in(nc, name, arr.shape, F32 if arr.dtype == np.float32 else BF16)
    return io


def finish_debug_outputs(P, nc, outs):
    ssem = P.new_sem("dbgst")
    vch, sch = P.chain["vector"], P.chain["scalar"]
    for name, (ap, dt) in outs.items():
        o = dram_out(nc, name, [128, ap.shape[-1]] if len(ap.shape) == 2 else ap.shape, dt)
        P.dma("gpsimd", o, ap, ssem,
              waits=[(vch, vch.val), (sch, sch.val), (P.sem_pe, P.sem_pe.val)])
    P.emit("gpsimd", lambda e: e.wait_ge(ssem.h, ssem.val), chain=False)



def phase_attn(P, x_ap, B, bf, f32, psum):
    vch, sch = P.chain["vector"], P.chain["scalar"]
    pe = P.sem_pe
    KT, Vb, Qb = B["KT"], B["Vb"], B["Qb"]
    P.barrier()
    bf.mark(); f32.mark()

    attn = B["attn"]
    mask8 = f32.alloc("mask8", 8 * 512)
    stash = f32.alloc("stash", 8 * 512)
    probs = [[bf.alloc("pr00", 512), bf.alloc("pr01", 512)],
             [bf.alloc("pr10", 512), bf.alloc("pr11", 512)]]
    mcol = [f32.alloc("mcol0", 8), f32.alloc("mcol1", 8)]
    onescol = bf.alloc("onescol", 8)
    onerow = f32.alloc("onerow", 128)
    msc = f32.alloc("msc", 8)
    identf = f32.alloc("identf", 128)
    pacc = [f32.alloc("pacc0", 512), f32.alloc("pacc1", 512)]
    rb = stash[:, 2048:3072]
    pbf = [probs[0][0], probs[1][0]]

    wsem = P.new_sem("atw")
    P.dma("sync", mask8, x_ap["mask8"], wsem)
    P.dma("sync", identf, x_ap["identf"], wsem)
    mask_ready = wsem.val
    P.emit("vector", lambda e: e.memset(onescol, 1.0))
    P.emit("vector", lambda e: e.memset(onerow[0:1, :], 1.0))
    ones_ready = vch.val

    sc = [[psum["b0"], psum["b1"]], [psum["b2"], psum["b3"]]]
    pa = [psum["b4"], psum["b5"]]
    pl = [psum["b6"], psum["b7"]]
    pm = psum["b0"]

    sc_free = [[None, None], [None, None]]
    pa_free = [None, None]
    pl_free = [None, None]
    pm_free = None
    probs_free = [[0, 0], [0, 0]]
    probs_accv = [[0, 0], [0, 0]]

    for j in range(TPC):
        nmain = 8 * j
        for hg in (0, 1):
            qsl = Qb[:, j * 1024 + hg * 512: j * 1024 + (hg + 1) * 512]
            for p in range(8):
                bank = sc[hg][p % 2]
                w = []
                if j == 0 and hg == 0 and p < 2:
                    w.append((wsem, mask_ready))
                if sc_free[hg][p % 2] is not None:
                    w.append(sc_free[hg][p % 2])
                P.emit("tensor", lambda e, kt=KT[:, (8 * j + p) * 128:(8 * j + p + 1) * 128],
                       q=qsl, b=bank: e.matmul(b[:, 0:512], kt, q, start=True, stop=True),
                       waits=w, incs=[(pe, 1)], chain=False)
                pe.val += 1
                P.emit("vector", lambda e, b=bank, o=stash[:, p * 512:(p + 1) * 512],
                       mk=mask8[:, p * 512:(p + 1) * 512]:
                       e.tensor_tensor(o, b[:, 0:512], mk, ALU.add),
                       waits=[(pe, pe.val)])
                sc_free[hg][p % 2] = (vch, vch.val)
            P.emit("vector", lambda e: e.reduce_max(msc[:, 0:1], stash, axis=AX.X))
            mx_v = vch.val
            tw = [(vch, mx_v)]
            if j == 0 and hg == 0:
                tw.append((wsem, mask_ready))
            if pm_free is not None:
                tw.append(pm_free)
            if sc_free[0][0] is not None:
                tw.append(sc_free[0][0])
            P.emit("tensor", lambda e: e.transpose(pm[0:1, 0:128], msc[:, 0:1], identf),
                   waits=tw, incs=[(pe, 1)], chain=False)
            pe.val += 1
            P.emit("vector", lambda e: e.reduce_max(msc[0:1, 1:2], pm[0:1, 0:128], axis=AX.X),
                   waits=[(pe, pe.val)])
            P.emit("vector", lambda e: e.tensor_scalar(
                msc[0:1, 3:4], msc[0:1, 1:2], -SCALE, None, ALU.mult))
            mneg_v = vch.val
            P.emit("tensor", lambda e: e.matmul(pm[:, 33:34], onerow[0:1, :],
                   msc[0:1, 3:4], start=True, stop=True),
                   waits=[(vch, mneg_v), (vch, ones_ready)], incs=[(pe, 1)], chain=False)
            pe.val += 1
            P.emit("vector", lambda e, d=mcol[hg][:, 0:1]: e.tensor_copy(d, pm[:, 33:34]),
                   waits=[(pe, pe.val)])
            pm_free = (vch, vch.val)
            sc_free[0][0] = (vch, vch.val)
            mc_ready = vch.val
            for p in range(8):
                pb = probs[hg][p % 2]
                P.emit("scalar", lambda e, o=pb, i=stash[:, p * 512:(p + 1) * 512],
                       m=mcol[hg][:, 0:1]:
                       e.activation(o, i, AF.Exp, bias=m, scale=SCALE),
                       waits=[(vch, mc_ready), (pe, probs_free[hg][p % 2]),
                              (vch, probs_accv[hg][p % 2])])
                first = (p == 0)
                last = (j == 0 and p == 7)
                if first:
                    P.emit("vector", lambda e, a=pacc[hg], pbuf=pb: e.tensor_copy(a, pbuf),
                           waits=[(sch, sch.val)])
                else:
                    P.emit("vector", lambda e, a=pacc[hg], pbuf=pb:
                           e.tensor_tensor(a, a, pbuf, ALU.add),
                           waits=[(sch, sch.val)])
                acc_v = vch.val
                pw = []
                if first and pa_free[hg] is not None:
                    pw.append(pa_free[hg])
                pw.append((sch, sch.val))
                P.emit("tensor", lambda e, v=Vb[:, (8 * j + p) * 128:(8 * j + p + 1) * 128],
                       pbuf=pb, b=pa[hg], f=first, l=last:
                       e.matmul(b[:, 0:512], v, pbuf, start=f, stop=l),
                       waits=pw, incs=[(pe, 1)], chain=False)
                pe.val += 1
                probs_free[hg][p % 2] = pe.val
                probs_accv[hg][p % 2] = acc_v
        exp_done = {}
        for kb in range(nmain + 1):
            if kb < nmain:
                for hg in (0, 1):
                    bank = sc[hg][kb % 2]
                    qsl = Qb[:, j * 1024 + hg * 512: j * 1024 + (hg + 1) * 512]
                    w = [sc_free[hg][kb % 2]] if sc_free[hg][kb % 2] is not None else []
                    P.emit("tensor", lambda e, kt=KT[:, kb * 128:(kb + 1) * 128], q=qsl, b=bank:
                           e.matmul(b[:, 0:512], kt, q, start=True, stop=True),
                           waits=w, incs=[(pe, 1)], chain=False)
                    pe.val += 1
                    P.emit("scalar", lambda e, b=bank, o=probs[hg][kb % 2], m=mcol[hg][:, 0:1]:
                           e.activation(o, b[:, 0:512], AF.Exp, bias=m, scale=SCALE),
                           waits=[(pe, pe.val), (pe, probs_free[hg][kb % 2]),
                                  (vch, probs_accv[hg][kb % 2])])
                    sc_free[hg][kb % 2] = (sch, sch.val)
                    exp_done[(kb, hg)] = (sch, sch.val)
            if kb > 0:
                kp = kb - 1
                last = (kp == nmain - 1)
                for hg in (0, 1):
                    pb = probs[hg][kp % 2]
                    P.emit("vector", lambda e, a=pacc[hg], pbuf=pb:
                           e.tensor_tensor(a, a, pbuf, ALU.add),
                           waits=[exp_done[(kp, hg)]])
                    probs_accv[hg][kp % 2] = vch.val
                    P.emit("tensor", lambda e, v=Vb[:, kp * 128:(kp + 1) * 128],
                           pbuf=pb, b=pa[hg], l=last:
                           e.matmul(b[:, 0:512], v, pbuf, start=False, stop=l),
                           waits=[exp_done[(kp, hg)]], incs=[(pe, 1)], chain=False)
                    pe.val += 1
                    probs_free[hg][kp % 2] = pe.val
        for hg in (0, 1):
            P.emit("vector", lambda e, d=pbf[hg], a=pacc[hg]: e.tensor_copy(d, a),
                   waits=[(pe, probs_free[hg][0])])
            cast_v = vch.val
            lw = [(vch, cast_v), (vch, ones_ready)]
            if pl_free[hg] is not None:
                lw.append(pl_free[hg])
            P.emit("tensor", lambda e, o=pl[hg][0:1, 0:512], pbuf=pbf[hg]:
                   e.matmul(o, onescol[:, 0:1], pbuf, start=True, stop=True),
                   waits=lw, incs=[(pe, 1)], chain=False)
            pe.val += 1
            probs_free[hg][0] = pe.val
        l_done = pe.val
        P.emit("vector", lambda e: e.reciprocal(stash[0:1, 0:512], pl[0][0:1, 0:512]),
               waits=[(pe, l_done)])
        pl_free[0] = (vch, vch.val)
        P.emit("vector", lambda e: e.reciprocal(stash[0:1, 512:1024], pl[1][0:1, 0:512]))
        pl_free[1] = (vch, vch.val)
        rec_v = vch.val
        for hg in (0, 1):
            bw = [(vch, rec_v)]
            if pm_free is not None:
                bw.append(pm_free)
            if sc_free[0][0] is not None:
                bw.append(sc_free[0][0])
            P.emit("tensor", lambda e, r=stash[0:1, hg * 512:(hg + 1) * 512]:
                   e.matmul(pm[:, 0:512], onerow[0:1, :], r, start=True, stop=True),
                   waits=bw, incs=[(pe, 1)], chain=False)
            pe.val += 1
            P.emit("scalar", lambda e, d=rb[:, hg * 512:(hg + 1) * 512]:
                   e.copy(d, pm[:, 0:512]), waits=[(pe, pe.val)])
            pm_free = (sch, sch.val)
            sc_free[0][0] = (sch, sch.val)
            P.emit("vector", lambda e, b=pa[hg], r=rb[:, hg * 512:(hg + 1) * 512],
                   d=attn[:, j * 1024 + hg * 512: j * 1024 + (hg + 1) * 512]:
                   e.tensor_tensor(d, b[:, 0:512], r, ALU.mult),
                   waits=[(sch, sch.val)])
            pa_free[hg] = (vch, vch.val)
    bf.release(); f32.release()
    return dict(attn=attn)


def phase_wo(P, x_ap, B, bf, f32, psum, s_out):
    vch, sch = P.chain["vector"], P.chain["scalar"]
    pe = P.sem_pe
    attn = B["attn"]
    P.barrier()
    bf.mark(); f32.mark()
    wo = bf.alloc("WoT", 8 * 1024)
    wsem = P.new_sem("wow")
    P.dma("gpsimd", wo.rearrange("p (a d) -> p a d", a=8),
          x_ap["WoT"].rearrange("(a p) d -> p a d", p=128), wsem)
    w_ready = wsem.val
    s_tile = [f32.alloc("s_t0", 1024), f32.alloc("s_t1", 1024)]
    ssem = P.new_sem("s_st")
    ps = [psum["b2"], psum["b3"]]
    ps_free = [None, None]
    s_free = [0, 0]
    for j in range(TPC):
        k = j % 2
        for n in (0, 1):
            bank = ps[n]
            for h in range(8):
                w = []
                if j == 0 and h == 0:
                    w.append((wsem, w_ready))
                if h == 0 and ps_free[n] is not None:
                    w.append(ps_free[n])
                P.emit("tensor", lambda e, a=attn[:, j * 1024 + h * 128: j * 1024 + h * 128 + 128],
                       ww=wo[:, h * 1024 + n * 512: h * 1024 + n * 512 + 512], b=bank, hh=h:
                       e.matmul(b[:, 0:512], a, ww, start=(hh == 0), stop=(hh == 7)),
                       waits=w, incs=([(pe, 1)] if h == 7 else []), chain=False)
            pe.val += 1
            P.emit("vector", lambda e, b=bank, d=s_tile[k][:, n * 512:(n + 1) * 512]:
                   e.tensor_copy(d, b[:, 0:512]),
                   waits=[(pe, pe.val), (ssem, s_free[k])])
            ps_free[n] = (vch, vch.val)
        P.dma("sync", s_out[j * 128:(j + 1) * 128, :], s_tile[k], ssem,
              waits=[(vch, vch.val)])
        s_free[k] = ssem.val
    P.emit("sync", lambda e: e.wait_ge(ssem.h, ssem.val), chain=False)
    bf.release(); f32.release()
    return dict()



def phase_l0_gate(P, x_ap, B, bf, f32, psum):
    vch, sch = P.chain["vector"], P.chain["scalar"]
    pe = P.sem_pe
    xnT = B["xnT"]
    g_buf = B["g_buf"]
    P.barrier()
    bf.mark(); f32.mark()
    gw = bf.alloc("gwT", 8 * 16)
    ebias = f32.alloc("ebias", 16)
    ebc = f32.alloc("ebc", 16)
    onerow = f32.alloc("onerow", 128)
    L = f32.alloc("L", 16)
    Bi = f32.alloc("Bi", 16)
    sel = f32.alloc("sel", 16)
    tmp = f32.alloc("tmp", 16)
    cols = f32.alloc("cols", 8)
    wsem = P.new_sem("gw")
    P.dma("gpsimd", gw.rearrange("p (a d) -> p a d", a=8),
          x_ap["gwT"].rearrange("(a p) d -> p a d", p=128), wsem)
    P.dma("sync", ebias[0:1, :], x_ap["ebias"], wsem)
    w_ready = wsem.val
    P.emit("vector", lambda e: e.memset(onerow[0:1, :], 1.0))
    ones_v = vch.val
    P.emit("tensor", lambda e: e.matmul(psum["b2"][:, 0:16], onerow[0:1, :], ebias[0:1, :],
           start=True, stop=True),
           waits=[(wsem, w_ready), (vch, ones_v)], incs=[(pe, 1)], chain=False)
    pe.val += 1
    P.emit("vector", lambda e: e.tensor_copy(ebc, psum["b2"][:, 0:16]), waits=[(pe, pe.val)])
    ebc_v = vch.val
    lg_free = None
    for j in range(TPC):
        for et in range(8):
            w = [(vch, ebc_v)] if et == 0 else []
            if et == 0 and lg_free is not None:
                w.append(lg_free)
            P.emit("tensor", lambda e, s=xnT[:, et * 2048 + j * 128: et * 2048 + j * 128 + 128],
                   g=gw[:, et * 16:(et + 1) * 16], o=et:
                   e.matmul(psum["b2"][:, 32:48], s, g, start=(o == 0), stop=(o == 7)),
                   waits=w, incs=([(pe, 1)] if et == 7 else []), chain=False)
        pe.val += 1
        P.emit("vector", lambda e: e.tensor_copy(L, psum["b2"][:, 32:48]), waits=[(pe, pe.val)])
        lg_free = (vch, vch.val)
        P.emit("vector", lambda e: e.tensor_tensor(Bi, L, ebc, ALU.add))
        P.emit("vector", lambda e: e.memset(sel, 0.0))
        for r in range(K0):
            P.emit("vector", lambda e: e.reduce_max(cols[:, 0:1], Bi, axis=AX.X))
            P.emit("vector", lambda e: e.tensor_scalar(tmp, Bi, cols[:, 0:1], None, ALU.is_ge))
            P.emit("vector", lambda e: e.tensor_tensor(sel, sel, tmp, ALU.add))
            P.emit("vector", lambda e: e.scalar_tensor_tensor(Bi, tmp, -1e30, Bi, ALU.mult, ALU.add))
        P.emit("vector", lambda e: e.tensor_scalar(tmp, sel, 1e9, -1e9, ALU.mult, ALU.add))
        P.emit("vector", lambda e: e.tensor_tensor(tmp, L, tmp, ALU.add))
        P.emit("vector", lambda e: e.reduce_max(cols[:, 1:2], tmp, axis=AX.X))
        P.emit("vector", lambda e: e.tensor_scalar(cols[:, 2:3], cols[:, 1:2], -1.0, None, ALU.mult))
        P.emit("scalar", lambda e: e.activation(Bi, tmp, AF.Exp, bias=cols[:, 2:3],
               accum_out=cols[:, 3:4]), waits=[(vch, vch.val)])
        P.emit("vector", lambda e: e.reciprocal(cols[:, 4:5], cols[:, 3:4]),
               waits=[(sch, sch.val)])
        P.emit("vector", lambda e, d=g_buf[:, j * 16:(j + 1) * 16]:
               e.tensor_scalar(d, Bi, cols[:, 4:5], None, ALU.mult))
    bf.release(); f32.release()
    return dict()



def phase_l0_ffn(P, x_ap, B, bf, f32, psum):
    vch, sch = P.chain["vector"], P.chain["scalar"]
    pe = P.sem_pe
    xnT, g_buf, y_acc = B["xnT"], B["g_buf"], B["y_acc"]
    P.barrier()
    bf.mark(); f32.mark()
    w1 = bf.alloc("w1T", 8 * 1024)
    w3 = bf.alloc("w3T", 8 * 1024)
    w2 = bf.alloc("w2T", 8 * 1024)
    hh = bf.alloc("hh", 8 * 2048)
    sil = [f32.alloc("sil0", 512), f32.alloc("sil1", 512)]
    P.emit("vector", lambda e: e.memset(y_acc[:, 0:8192], 0.0))
    P.emit("vector", lambda e: e.memset(y_acc[:, 8192:16384], 0.0))
    y_zeroed = vch.val
    wsem = P.new_sem("f0w")
    pu = [(psum["b2"], psum["b3"]), (psum["b4"], psum["b5"])]
    pd = [psum["b6"], psum["b7"]]
    pu_free = [None, None]
    pd_free = [None, None]
    hh_free = 0
    w_free = 0
    sil_free = [0, 0]
    for e0 in range(E0 + 1):
        shared = (e0 == E0)
        pref = "sw" if shared else "w"
        keys = (("sw1T", "sw2T", "sw3T") if shared else ("w1T", "w2T", "w3T"))
        src1 = x_ap[keys[0]] if shared else x_ap[keys[0]][e0]
        src2 = x_ap[keys[1]] if shared else x_ap[keys[1]][e0]
        src3 = x_ap[keys[2]] if shared else x_ap[keys[2]][e0]
        dw = [(P.sem_pe, w_free)] if w_free else []
        P.dma("gpsimd", w1.rearrange("p (a d) -> p a d", a=8),
              src1.rearrange("(a p) d -> p a d", p=128), wsem, waits=dw)
        P.dma("gpsimd", w3.rearrange("p (a d) -> p a d", a=8),
              src3.rearrange("(a p) d -> p a d", p=128), wsem)
        P.dma("gpsimd", w2.rearrange("p (a d) -> p a d", a=8),
              src2.rearrange("(a p) d -> p a d", p=128), wsem)
        w_up = wsem.val - 16
        w_dn = wsem.val
        ctr = 0
        for isub in range(8):
            for tc in range(4):
                b1, b3 = pu[ctr % 2]
                w = [(wsem, w_up)] if (isub == 0 and tc == 0) else []
                if pu_free[ctr % 2] is not None:
                    w.append(pu_free[ctr % 2])
                if isub == 0 and tc == 0 and hh_free:
                    w.append((pe, hh_free))
                for et in range(8):
                    P.emit("tensor", lambda e, ww=w1[:, et * 1024 + isub * 128: et * 1024 + isub * 128 + 128],
                           r=xnT[:, et * 2048 + tc * 512: et * 2048 + tc * 512 + 512],
                           b=b1, o=et:
                           e.matmul(b[:, 0:512], ww, r, start=(o == 0), stop=(o == 7)),
                           waits=(w if et == 0 else []), chain=False)
                for et in range(8):
                    P.emit("tensor", lambda e, ww=w3[:, et * 1024 + isub * 128: et * 1024 + isub * 128 + 128],
                           r=xnT[:, et * 2048 + tc * 512: et * 2048 + tc * 512 + 512],
                           b=b3, o=et:
                           e.matmul(b[:, 0:512], ww, r, start=(o == 0), stop=(o == 7)),
                           incs=([(pe, 1)] if et == 7 else []), chain=False)
                pe.val += 1
                P.emit("scalar", lambda e, b=b1, o=sil[ctr % 2]:
                       e.activation(o, b[:, 0:512], AF.Silu),
                       waits=[(pe, pe.val), (vch, sil_free[ctr % 2])])
                P.emit("vector", lambda e, b=b3, s=sil[ctr % 2],
                       d=hh[:, isub * 2048 + tc * 512: isub * 2048 + tc * 512 + 512]:
                       e.tensor_tensor(d, s, b[:, 0:512], ALU.mult),
                       waits=[(sch, sch.val)])
                pu_free[ctr % 2] = (vch, vch.val)
                sil_free[ctr % 2] = vch.val
                ctr += 1
        hh_v = vch.val
        for j in range(TPC):
            for n in (0, 1):
                bank = pd[n]
                for isub in range(8):
                    w = []
                    if isub == 0:
                        w = [(vch, hh_v), (wsem, w_dn)]
                        if pd_free[n] is not None:
                            w.append(pd_free[n])
                    P.emit("tensor", lambda e, s=hh[:, isub * 2048 + j * 128: isub * 2048 + j * 128 + 128],
                           ww=w2[:, isub * 1024 + n * 512: isub * 1024 + n * 512 + 512],
                           b=bank, o=isub:
                           e.matmul(b[:, 0:512], s, ww, start=(o == 0), stop=(o == 7)),
                           waits=w, incs=([(pe, 1)] if isub == 7 else []), chain=False)
                pe.val += 1
                yw = [(pe, pe.val)]
                if e0 == 0:
                    yw.append((vch, y_zeroed))
                ysl = y_acc[:, j * 1024 + n * 512: j * 1024 + n * 512 + 512]
                if shared:
                    P.emit("vector", lambda e, b=bank, d=ysl:
                           e.tensor_tensor(d, b[:, 0:512], d, ALU.add), waits=yw)
                else:
                    P.emit("vector", lambda e, b=bank, d=ysl, gc=g_buf[:, j * 16 + e0: j * 16 + e0 + 1]:
                           e.scalar_tensor_tensor(d, b[:, 0:512], gc, d, ALU.mult, ALU.add),
                           waits=yw)
                pd_free[n] = (vch, vch.val)
        hh_free = pe.val
        w_free = pe.val
    bf.release(); f32.release()
    return dict()



def phase_combine(P, x_ap, B, bf, f32, psum, s_in, x1_out, xkey="x_own"):
    vch, sch = P.chain["vector"], P.chain["scalar"]
    y_acc = B["y_acc"]
    P.barrier()
    bf.mark(); f32.mark()
    xt = [f32.alloc("cx0", 1024), f32.alloc("cx1", 1024)]
    stl = [f32.alloc("cs0", 1024), f32.alloc("cs1", 1024)]
    gt = f32.alloc("cg", 1024)
    o1 = [f32.alloc("co0", 1024), f32.alloc("co1", 1024)]
    xsem = [P.new_sem("c0x0"), P.new_sem("c0x1")]
    osem = P.new_sem("c0o")
    buf_free = [0, 0]
    out_free = [0, 0]
    for j in range(TPC):
        k = j % 2
        P.dma("sync", xt[k], x_ap[xkey][j * 128:(j + 1) * 128, :], xsem[k],
              waits=[(vch, buf_free[k])])
        P.dma("sync", stl[k], s_in[j * 128:(j + 1) * 128, :], xsem[k])
        P.emit("scalar", lambda e, y=y_acc[:, j * 1024:(j + 1) * 1024]:
               e.activation(gt, y, AF.Silu), waits=[(vch, vch.val)])
        P.emit("vector", lambda e, s=stl[k]: e.tensor_tensor(gt, s, gt, ALU.mult),
               waits=[(sch, sch.val), (xsem[k], xsem[k].val)])
        P.emit("vector", lambda e, x=xt[k], d=o1[k]: e.tensor_tensor(d, x, gt, ALU.add),
               waits=[(osem, out_free[k])])
        buf_free[k] = vch.val
        P.dma("sync", x1_out[j * 128:(j + 1) * 128, :], o1[k], osem,
              waits=[(vch, vch.val)])
        out_free[k] = osem.val
    P.emit("sync", lambda e: e.wait_ge(osem.h, osem.val), chain=False)
    bf.release(); f32.release()
    return dict()



def phase_l1_router(P, x_ap, B, bf, f32, psum):
    vch, sch = P.chain["vector"], P.chain["scalar"]
    pe = P.sem_pe
    xnT, cw_buf = B["xn1T"], B["cw_buf"]
    P.barrier()
    bf.mark(); f32.mark()
    rw = bf.alloc("rwT", 8 * 8)
    L = f32.alloc("L", 8)
    Bi = f32.alloc("Bi", 8)
    sel = f32.alloc("sel", 8)
    tmp = f32.alloc("tmp", 8)
    cols = f32.alloc("cols", 8)
    wsem = P.new_sem("rw")
    P.dma("gpsimd", rw.rearrange("p (a d) -> p a d", a=8),
          x_ap["rwT"].rearrange("(a p) d -> p a d", p=128), wsem)
    w_ready = wsem.val
    lg_free = None
    for j in range(TPC):
        for et in range(8):
            w = [(wsem, w_ready)] if (j == 0 and et == 0) else []
            if et == 0 and lg_free is not None:
                w.append(lg_free)
            P.emit("tensor", lambda e, s=xnT[:, et * 2048 + j * 128: et * 2048 + j * 128 + 128],
                   g=rw[:, et * 8:(et + 1) * 8], o=et:
                   e.matmul(psum["b2"][:, 0:8], s, g, start=(o == 0), stop=(o == 7)),
                   waits=w, incs=([(pe, 1)] if et == 7 else []), chain=False)
        pe.val += 1
        P.emit("vector", lambda e: e.tensor_copy(L, psum["b2"][:, 0:8]), waits=[(pe, pe.val)])
        lg_free = (vch, vch.val)
        P.emit("vector", lambda e: e.tensor_copy(Bi, L))
        P.emit("vector", lambda e: e.memset(sel, 0.0))
        for r in range(K1):
            P.emit("vector", lambda e: e.reduce_max(cols[:, 0:1], Bi, axis=AX.X))
            P.emit("vector", lambda e: e.tensor_scalar(tmp, Bi, cols[:, 0:1], None, ALU.is_ge))
            P.emit("vector", lambda e: e.tensor_tensor(sel, sel, tmp, ALU.add))
            P.emit("vector", lambda e: e.scalar_tensor_tensor(Bi, tmp, -1e30, Bi, ALU.mult, ALU.add))
        P.emit("vector", lambda e: e.tensor_scalar(tmp, sel, 1e9, -1e9, ALU.mult, ALU.add))
        P.emit("vector", lambda e: e.tensor_tensor(tmp, L, tmp, ALU.add))
        P.emit("vector", lambda e: e.reduce_max(cols[:, 1:2], tmp, axis=AX.X))
        P.emit("vector", lambda e: e.tensor_scalar(cols[:, 2:3], cols[:, 1:2], -1.0, None, ALU.mult))
        P.emit("scalar", lambda e: e.activation(Bi, tmp, AF.Exp, bias=cols[:, 2:3],
               accum_out=cols[:, 3:4]), waits=[(vch, vch.val)])
        P.emit("vector", lambda e: e.reciprocal(cols[:, 4:5], cols[:, 3:4]),
               waits=[(sch, sch.val)])
        P.emit("vector", lambda e, d=cw_buf[:, j * 8:(j + 1) * 8]:
               e.tensor_scalar(d, Bi, cols[:, 4:5], None, ALU.mult))
    bf.release(); f32.release()
    return dict()



def phase_lru_scan(P, x_ap, B, bf, f32, psum, cc_in, cc_out):
    vch, sch = P.chain["vector"], P.chain["scalar"]
    pe = P.sem_pe
    ident = B["ident"]
    y_loc, Cown, Ptab, oml = B["y_loc"], B["Cown"], B["Ptab"], B["oml"]
    P.barrier()
    bf.mark(); f32.mark()
    win = bf.alloc("winT", 8 * 1024)
    lamc = f32.alloc("lamc", 8)
    pw = f32.alloc("pw", 8)
    onest = f32.alloc("onest", 128)
    lamrep = f32.alloc("lamrep", 8 * 128)
    Lrep = f32.alloc("Lrep", 8 * 128)
    g1c = f32.alloc("g1c", 8)
    b1c = f32.alloc("b1c", 8)
    selC = bf.alloc("selC", 16)
    Fown = f32.alloc("Fown", 128)
    Fg = f32.alloc("Fg", 128)
    Fscan = f32.alloc("Fscan", 128)
    Cinc = f32.alloc("Cinc", 8 * 128)
    Cb = bf.alloc("Cb", 128)
    CT = bf.alloc("CT", 128)
    xt = [f32.alloc("lx0", 1024), f32.alloc("lx1", 1024)]
    xn1t = f32.alloc("xn1t", 1024)
    scr = f32.alloc("scr", 1024)
    st2 = f32.alloc("st2", 8)
    xnb = [bf.alloc("lxb0", 1024), bf.alloc("lxb1", 1024)]
    uT = [bf.alloc("uT0", 1024), bf.alloc("uT1", 1024)]
    wsem = P.new_sem("lruw")
    P.dma("gpsimd", win.rearrange("p (a d) -> p a d", a=8),
          x_ap["winT"].rearrange("(a p) d -> p a d", p=128), wsem)
    P.dma("sync", lamc, x_ap["loglam_c"], wsem)
    P.dma("sync", g1c, x_ap["ln1g_c"], wsem)
    P.dma("sync", b1c, x_ap["ln1b_c"], wsem)
    P.dma("sync", selC, x_ap["selC"], wsem)
    w_ready = wsem.val
    P.emit("scalar", lambda e: e.activation(lamc, lamc, AF.Sigmoid), waits=[(wsem, w_ready)])
    P.emit("vector", lambda e: e.memset(onest, 1.0), waits=[(sch, sch.val)])
    P.emit("vector", lambda e: e.tensor_scalar(oml, lamc, -1.0, 1.0, ALU.mult, ALU.add))
    P.emit("vector", lambda e: e.tensor_copy(pw, lamc))
    for s in range(8):
        P.emit("vector", lambda e, d=lamrep[:, s * 128:(s + 1) * 128], c=lamc[:, s:s + 1]:
               e.tensor_scalar(d, onest, c, None, ALU.mult))
        P.emit("vector", lambda e, d=Ptab[:, s * 128: s * 128 + 1], c=lamc[:, s:s + 1]:
               e.tensor_scalar(d, onest[:, 0:1], c, None, ALU.mult))
    for i in range(7):
        n = 1 << i
        for s in range(8):
            P.emit("vector", lambda e, d=Ptab[:, s * 128 + n: s * 128 + 2 * n],
                   src=Ptab[:, s * 128: s * 128 + n], c=pw[:, s:s + 1]:
                   e.tensor_scalar(d, src, c, None, ALU.mult))
        P.emit("vector", lambda e: e.tensor_tensor(pw, pw, pw, ALU.mult))
    for s in range(8):
        P.emit("vector", lambda e, d=Lrep[:, s * 128:(s + 1) * 128], c=pw[:, s:s + 1]:
               e.tensor_scalar(d, onest, c, None, ALU.mult))
    tbl_v = vch.val
    xsem = [P.new_sem("lrx0"), P.new_sem("lrx1")]
    pT = [psum["bt0"], psum["bt1"]]
    pU = [psum["b2"], psum["b3"]]
    xt_free = [0, 0]
    xnb_free = [0, 0]
    tev = [None, None]
    uev = [None, None]
    evs = {}

    def sprep(j):
        k = j % 2
        P.dma("sync", xt[k], x_ap["x1s"][j * 128:(j + 1) * 128, :], xsem[k],
              waits=[(vch, xt_free[k])])
        ln_tile(P, xt[k], xn1t, st2, scr, waits=[(xsem[k], xsem[k].val)])
        nw = ln_tile(P, xn1t, xnb[k], st2, scr,
                     extra_norm_waits=[(pe, xnb_free[k])])
        xt_free[k] = vch.val
        dsts = [uT[k][:, et * 128:(et + 1) * 128] for et in range(8)]
        evs[j] = (transpose8(P, xnb[k], dsts, ident, pT, tev, (vch, nw), gb=(g1c, b1c)),
                  dsts)
        xnb_free[k] = pe.val

    sprep(0)
    for j in range(TPC):
        if j + 1 < TPC:
            sprep(j + 1)
        ev, dsts = evs.pop(j)
        for s in range(8):
            bank = pU[s % 2]
            for dt in range(8):
                w = list(ev) + [(wsem, w_ready)] if (dt == 0 and s == 0) else []
                if dt == 0 and s > 0:
                    w = list(ev)
                if dt == 0 and uev[s % 2] is not None:
                    w.append(uev[s % 2])
                P.emit("tensor", lambda e, ww=win[:, dt * 1024 + s * 128: dt * 1024 + s * 128 + 128],
                       r=dsts[dt], b=bank, o=dt:
                       e.matmul(b[:, 0:128], ww, r, start=(o == 0), stop=(o == 7)),
                       waits=w, incs=([(pe, 1)] if dt == 7 else []), chain=False)
            pe.val += 1
            P.emit("vector", lambda e, b=bank, d=y_loc[:, s * 2048 + j * 128: s * 2048 + j * 128 + 128],
                   lr=lamrep[:, s * 128:(s + 1) * 128]:
                   e.tensor_tensor_scan(d, lr, b[:, 0:128], 0.0, ALU.mult, ALU.add),
                   waits=[(pe, pe.val), (vch, tbl_v if j == 0 else 0)])
            uev[s % 2] = (vch, vch.val)
    for s in range(8):
        P.emit("vector", lambda e, d=Fown[:, s * 16:(s + 1) * 16],
               i=y_loc[:, s * 2048 + 127: (s + 1) * 2048: 128]:
               e.tensor_copy(d, i))
    csem = P.new_sem("cc")
    P.dma("gpsimd", cc_in, Fown, csem, waits=[(vch, vch.val)])
    P.emit("gpsimd", lambda e: e.collective_compute(
        "AllGather", ALU.bypass, replica_groups=[list(range(NCORES))],
        ins=[cc_in.opt()], outs=[cc_out.opt()]),
        waits=[(csem, csem.val)], incs=[(csem, 1)], chain=False)
    csem.val += 1
    cc_done = csem.val
    fsem = P.new_sem("fb")
    fg_free = 0
    cb_free = 0
    for s in range(8):
        P.dma("sync", Fg.rearrange("p (r j) -> p r j", r=8),
              cc_out[:, s * 16:(s + 1) * 16].rearrange("(r p) j -> p r j", p=128),
              fsem, waits=[(csem, cc_done), (vch, fg_free)])
        P.emit("vector", lambda e, o=Fscan.rearrange("p (j r) -> p j r", r=8),
               i=Fg.rearrange("p (r j) -> p j r", j=16):
               e.tensor_copy(o, i), waits=[(fsem, fsem.val)])
        fg_free = vch.val
        P.emit("vector", lambda e, d=Cinc[:, s * 128:(s + 1) * 128],
               lr=Lrep[:, s * 128:(s + 1) * 128]:
               e.tensor_tensor_scan(d, lr, Fscan, 0.0, ALU.mult, ALU.add))
        P.emit("vector", lambda e, i=Cinc[:, s * 128:(s + 1) * 128]: e.tensor_copy(Cb, i),
               waits=[(pe, cb_free)])
        cb_v = vch.val
        P.emit("tensor", lambda e: e.transpose(pT[0][:, 0:128], Cb, ident),
               waits=[(vch, cb_v)], incs=[(pe, 1)], chain=False)
        pe.val += 1
        cb_free = pe.val
        P.emit("vector", lambda e: e.tensor_copy(CT, pT[0][:, 0:128]), waits=[(pe, pe.val)])
        ct_v = vch.val
        P.emit("tensor", lambda e: e.matmul(psum["b4"][:, 0:16], CT, selC, start=True, stop=True),
               waits=[(vch, ct_v)], incs=[(pe, 1)], chain=False)
        pe.val += 1
        P.emit("vector", lambda e, d=Cown[:, s * 16:(s + 1) * 16]:
               e.tensor_copy(d, psum["b4"][:, 0:16]), waits=[(pe, pe.val)])
    bf.release(); f32.release()
    return dict()



def phase_lru_out(P, x_ap, B, bf, f32, psum, x2_out):
    vch, sch = P.chain["vector"], P.chain["scalar"]
    pe = P.sem_pe
    y_loc, Cown, Ptab, oml = B["y_loc"], B["Cown"], B["Ptab"], B["oml"]
    P.barrier()
    bf.mark(); f32.mark()
    wout = bf.alloc("woutT", 8 * 1024)
    rows = f32.alloc("rows", 4 * 1024)
    gD = f32.alloc("gD", 1024)
    bD = f32.alloc("bD", 1024)
    onerow = f32.alloc("onerow", 128)
    yf = bf.alloc("yf", 1024)
    xt = [f32.alloc("ox0", 1024), f32.alloc("ox1", 1024)]
    xn1t = f32.alloc("oxn", 1024)
    nt = f32.alloc("ont", 1024)
    t3 = f32.alloc("ot3", 1024)
    x2t = [f32.alloc("ox20", 1024), f32.alloc("ox21", 1024)]
    scr = f32.alloc("scr", 1024)
    st2 = f32.alloc("st2", 8)
    wsem = P.new_sem("low")
    P.dma("gpsimd", wout.rearrange("p (a d) -> p a d", a=8),
          x_ap["woutT"].rearrange("(a p) d -> p a d", p=128), wsem)
    P.dma("sync", rows[0:1, 0:1024], x_ap["dskip_r"], wsem)
    P.dma("sync", rows[0:1, 1024:2048], x_ap["ln1g_r"], wsem)
    P.dma("sync", rows[0:1, 2048:3072], x_ap["ln1b_r"], wsem)
    w_ready = wsem.val
    for s in range(8):
        P.emit("vector", lambda e, d=wout[:, s * 1024:(s + 1) * 1024], c=oml[:, s:s + 1]:
               e.tensor_scalar(d, d, c, None, ALU.mult),
               waits=([(wsem, w_ready)] if s == 0 else []))
    wsc_v = vch.val
    P.emit("vector", lambda e: e.memset(onerow[0:1, :], 1.0))
    P.emit("vector", lambda e: e.tensor_tensor(rows[0:1, 3072:4096], rows[0:1, 1024:2048],
           rows[0:1, 0:1024], ALU.mult))
    P.emit("vector", lambda e: e.tensor_tensor(rows[0:1, 0:1024], rows[0:1, 2048:3072],
           rows[0:1, 0:1024], ALU.mult))
    rows_v = vch.val
    bprev = None
    for half, (src, dst) in enumerate([(rows[0:1, 3072:4096], gD), (rows[0:1, 0:1024], bD)]):
        for n in (0, 1):
            bw = [(vch, rows_v)] + ([bprev] if bprev else [])
            P.emit("tensor", lambda e, s=src[:, n * 512:(n + 1) * 512],
                   b=psum["b6"]: e.matmul(b[:, 0:512], onerow[0:1, :], s, start=True, stop=True),
                   waits=bw, incs=[(pe, 1)], chain=False)
            pe.val += 1
            P.emit("vector", lambda e, d=dst[:, n * 512:(n + 1) * 512], b=psum["b6"]:
                   e.tensor_copy(d, b[:, 0:512]), waits=[(pe, pe.val)])
            bprev = (vch, vch.val)
    bc_v = vch.val
    xsem = [P.new_sem("lox0"), P.new_sem("lox1")]
    osem = P.new_sem("x2o")
    pO = [psum["b2"], psum["b3"]]
    xt_free = [0, 0]
    x2_free = [0, 0]
    pO_free = [None, None]
    yf_free = 0
    for j in range(TPC):
        k = j % 2
        P.dma("sync", xt[k], x_ap["x1s"][j * 128:(j + 1) * 128, :], xsem[k],
              waits=[(vch, xt_free[k])])
        for s in range(8):
            w = [(vch, bc_v)] if (j == 0 and s == 0) else []
            if s == 0 and yf_free:
                w.append((pe, yf_free))
            P.emit("vector", lambda e, d=yf[:, s * 128:(s + 1) * 128],
                   p=Ptab[:, s * 128:(s + 1) * 128], c=Cown[:, s * 16 + j: s * 16 + j + 1],
                   z=y_loc[:, s * 2048 + j * 128: s * 2048 + j * 128 + 128]:
                   e.scalar_tensor_tensor(d, p, c, z, ALU.mult, ALU.add), waits=w)
        yf_v = vch.val
        for n in (0, 1):
            bank = pO[n]
            for s in range(8):
                w = []
                if s == 0:
                    w = [(vch, yf_v)] + ([(vch, wsc_v)] if j == 0 else [])
                    if pO_free[n] is not None:
                        w.append(pO_free[n])
                P.emit("tensor", lambda e, yy=yf[:, s * 128:(s + 1) * 128],
                       ww=wout[:, s * 1024 + n * 512: s * 1024 + n * 512 + 512], b=bank, o=s:
                       e.matmul(b[:, 0:512], yy, ww, start=(o == 0), stop=(o == 7)),
                       waits=w, incs=([(pe, 1)] if s == 7 else []), chain=False)
            pe.val += 1
        ln_tile(P, xt[k], xn1t, st2, scr, waits=[(xsem[k], xsem[k].val)])
        ln_tile(P, xn1t, nt, st2, scr)
        P.emit("vector", lambda e: e.tensor_tensor(t3, nt, gD, ALU.mult))
        P.emit("vector", lambda e: e.tensor_tensor(t3, t3, bD, ALU.add))
        P.emit("vector", lambda e: e.tensor_tensor(t3, t3, xn1t, ALU.add),
               waits=[(osem, x2_free[k])])
        for n in (0, 1):
            P.emit("vector", lambda e, d=x2t[k][:, n * 512:(n + 1) * 512],
                   s=t3[:, n * 512:(n + 1) * 512], b=pO[n]:
                   e.tensor_tensor(d, s, b[:, 0:512], ALU.add),
                   waits=[(pe, pe.val)])
            pO_free[n] = (vch, vch.val)
        xt_free[k] = vch.val
        yf_free = pe.val
        P.dma("sync", x2_out[j * 128:(j + 1) * 128, :], x2t[k], osem,
              waits=[(vch, vch.val)])
        x2_free[k] = osem.val
    P.emit("sync", lambda e: e.wait_ge(osem.h, osem.val), chain=False)
    bf.release(); f32.release()
    return dict()



def phase_mlp(P, x_ap, B, bf, f32, psum, x2_in, s1_out):
    vch, sch = P.chain["vector"], P.chain["scalar"]
    pe = P.sem_pe
    ident = B["ident"]
    P.barrier()
    bf.mark(); f32.mark()
    m1 = bf.alloc("m1T", 8 * 2048)
    m2 = bf.alloc("m2T", 16 * 1024)
    g2c = f32.alloc("g2c", 8)
    b2c_ = f32.alloc("b2c_", 8)
    b1c = f32.alloc("b1c", 16)
    b2row = f32.alloc("b2row", 1024)
    b2bc = f32.alloc("b2bc", 1024)
    onerow = f32.alloc("onerow", 128)
    hT = bf.alloc("hT", 16 * 512)
    x2nT = bf.alloc("x2nT", 8 * 512)
    xt = [f32.alloc("mx0", 4 * 1024)]
    x2pb = [f32.alloc("mpb0", 4 * 1024)]
    xnb = bf.alloc("mxb", 1024)
    s1t = [f32.alloc("ms0", 4 * 1024)]
    scr = f32.alloc("scr", 1024)
    st2 = f32.alloc("st2", 8)
    wsem = P.new_sem("mw")
    P.dma("gpsimd", m1.rearrange("p (a d) -> p a d", a=8),
          x_ap["m1T"].rearrange("(a p) d -> p a d", p=128), wsem)
    P.dma("gpsimd", m2.rearrange("p (a d) -> p a d", a=16),
          x_ap["m2T"].rearrange("(a p) d -> p a d", p=128), wsem)
    P.dma("sync", g2c, x_ap["ln2g_c"], wsem)
    P.dma("sync", b2c_, x_ap["ln2b_c"], wsem)
    P.dma("sync", b1c, x_ap["b1_c"], wsem)
    P.dma("sync", b2row[0:1, :], x_ap["b2_r"], wsem)
    w_ready = wsem.val
    P.emit("vector", lambda e: e.memset(onerow[0:1, :], 1.0))
    ones_v = vch.val
    bprev = None
    for n in (0, 1):
        bw = [(vch, ones_v), (wsem, w_ready)] + ([bprev] if bprev else [])
        P.emit("tensor", lambda e, s=b2row[0:1, n * 512:(n + 1) * 512]:
               e.matmul(psum["b6"][:, 0:512], onerow[0:1, :], s, start=True, stop=True),
               waits=bw, incs=[(pe, 1)], chain=False)
        pe.val += 1
        P.emit("vector", lambda e, d=b2bc[:, n * 512:(n + 1) * 512]:
               e.tensor_copy(d, psum["b6"][:, 0:512]), waits=[(pe, pe.val)])
        bprev = (vch, vch.val)
    bc_v = vch.val
    xsem = [P.new_sem("mx0s")]
    osem = P.new_sem("ms1o")
    pT = [psum["bt0"], psum["bt1"]]
    pH = [psum["b2"], psum["b3"]]
    pS = [psum["b4"], psum["b5"]]
    xt_free = [0]
    s1_free = [0]
    tev = [None, None]
    pH_free = [None, None]
    pS_free = [None, None]
    NCH = TPC // 4
    for ch in range(NCH):
        k = 0
        P.dma("sync", xt[k], x2_in[ch * 512:(ch + 1) * 512, :], xsem[k],
              waits=[(vch, xt_free[k])])
        for t4 in range(4):
            xtile = xt[k][:, t4 * 1024:(t4 + 1) * 1024]
            nw = ln_tile(P, xtile, xnb, st2, scr,
                         waits=([(xsem[k], xsem[k].val)] if t4 == 0 else []))
            P.emit("vector", lambda e, d=x2pb[k][:, t4 * 1024:(t4 + 1) * 1024], i=xtile:
                   e.tensor_tensor(d, i, b2bc, ALU.add),
                   waits=[(vch, bc_v)])
            dsts = [x2nT[:, et * 512 + t4 * 128: et * 512 + t4 * 128 + 128] for et in range(8)]
            transpose8(P, xnb, dsts, ident, pT, tev, (vch, nw), gb=(g2c, b2c_))
        ev_v = vch.val
        for ft in range(16):
            bank = pH[ft % 2]
            for dt in range(8):
                w = []
                if dt == 0:
                    w = [(vch, ev_v)]
                    if pH_free[ft % 2] is not None:
                        w.append(pH_free[ft % 2])
                P.emit("tensor", lambda e, ww=m1[:, dt * 2048 + ft * 128: dt * 2048 + ft * 128 + 128],
                       r=x2nT[:, dt * 512:(dt + 1) * 512], b=bank, o=dt:
                       e.matmul(b[:, 0:512], ww, r, start=(o == 0), stop=(o == 7)),
                       waits=w, incs=([(pe, 1)] if dt == 7 else []), chain=False)
            pe.val += 1
            P.emit("scalar", lambda e, b=bank, d=hT[:, ft * 512:(ft + 1) * 512],
                   bc=b1c[:, ft:ft + 1]:
                   e.activation(d, b[:, 0:512], AF.Relu, bias=bc),
                   waits=[(pe, pe.val)])
            pH_free[ft % 2] = (sch, sch.val)
        h_v = sch.val
        for t4 in range(4):
            for n in (0, 1):
                bank = pS[n]
                for ft in range(16):
                    w = []
                    if ft == 0:
                        w = [(sch, h_v)]
                        if pS_free[n] is not None:
                            w.append(pS_free[n])
                    P.emit("tensor", lambda e, s=hT[:, ft * 512 + t4 * 128: ft * 512 + t4 * 128 + 128],
                           ww=m2[:, ft * 1024 + n * 512: ft * 1024 + n * 512 + 512], b=bank, o=ft:
                           e.matmul(b[:, 0:512], s, ww, start=(o == 0), stop=(o == 15)),
                           waits=w, incs=([(pe, 1)] if ft == 15 else []), chain=False)
                pe.val += 1
                P.emit("vector", lambda e, b=bank,
                       d=s1t[k][:, t4 * 1024 + n * 512: t4 * 1024 + n * 512 + 512],
                       p=x2pb[k][:, t4 * 1024 + n * 512: t4 * 1024 + n * 512 + 512]:
                       e.tensor_tensor(d, b[:, 0:512], p, ALU.add),
                       waits=[(pe, pe.val), (osem, s1_free[k])])
                pS_free[n] = (vch, vch.val)
        xt_free[k] = vch.val
        P.dma("sync", s1_out[ch * 512:(ch + 1) * 512, :], s1t[k], osem,
              waits=[(vch, vch.val)])
        s1_free[k] = osem.val
    P.emit("sync", lambda e: e.wait_ge(osem.h, osem.val), chain=False)
    bf.release(); f32.release()
    return dict()



def phase_l1_ffn(P, x_ap, B, bf, f32, psum):
    vch, sch = P.chain["vector"], P.chain["scalar"]
    pe = P.sem_pe
    xnT, cw_buf, y_acc = B["xn1T"], B["cw_buf"], B["y_acc"]
    P.barrier()
    bf.mark(); f32.mark()
    wg = bf.alloc("wg", 8 * 1024)
    wu = bf.alloc("wu", 8 * 1024)
    wd = bf.alloc("wd", 8 * 1024)
    hh = bf.alloc("hh1", 8 * 2048)
    sil = [f32.alloc("sil0", 512), f32.alloc("sil1", 512)]
    P.emit("vector", lambda e: e.memset(y_acc[:, 0:8192], 0.0))
    P.emit("vector", lambda e: e.memset(y_acc[:, 8192:16384], 0.0))
    y_zeroed = vch.val
    wsem = P.new_sem("f1w")
    pu = [(psum["b2"], psum["b3"]), (psum["b4"], psum["b5"])]
    pd = [psum["b6"], psum["b7"]]
    pu_free = [None, None]
    pd_free = [None, None]
    hh_free = 0
    w_free = 0
    sil_free = [0, 0]
    first = True
    for e0 in range(E1):
        for sl in range(4):
            dw = [(P.sem_pe, w_free)] if w_free else []
            P.dma("gpsimd", wg.rearrange("p (a d) -> p a d", a=8),
                  x_ap["wg"][e0][:, sl * 1024:(sl + 1) * 1024].rearrange("(a p) d -> p a d", p=128),
                  wsem, waits=dw)
            P.dma("gpsimd", wu.rearrange("p (a d) -> p a d", a=8),
                  x_ap["wu"][e0][:, sl * 1024:(sl + 1) * 1024].rearrange("(a p) d -> p a d", p=128),
                  wsem)
            P.dma("gpsimd", wd.rearrange("p (a d) -> p a d", a=8),
                  x_ap["wd"][e0][sl * 1024:(sl + 1) * 1024, :].rearrange("(a p) d -> p a d", p=128),
                  wsem)
            w_up = wsem.val - 16
            w_dn = wsem.val
            ctr = 0
            for isub in range(8):
                for tc in range(4):
                    b1, b3 = pu[ctr % 2]
                    w = [(wsem, w_up)] if (isub == 0 and tc == 0) else []
                    if pu_free[ctr % 2] is not None:
                        w.append(pu_free[ctr % 2])
                    if isub == 0 and tc == 0 and hh_free:
                        w.append((pe, hh_free))
                    for et in range(8):
                        P.emit("tensor", lambda e, ww=wg[:, et * 1024 + isub * 128: et * 1024 + isub * 128 + 128],
                               r=xnT[:, et * 2048 + tc * 512: et * 2048 + tc * 512 + 512],
                               b=b1, o=et:
                               e.matmul(b[:, 0:512], ww, r, start=(o == 0), stop=(o == 7)),
                               waits=(w if et == 0 else []), chain=False)
                    for et in range(8):
                        P.emit("tensor", lambda e, ww=wu[:, et * 1024 + isub * 128: et * 1024 + isub * 128 + 128],
                               r=xnT[:, et * 2048 + tc * 512: et * 2048 + tc * 512 + 512],
                               b=b3, o=et:
                               e.matmul(b[:, 0:512], ww, r, start=(o == 0), stop=(o == 7)),
                               incs=([(pe, 1)] if et == 7 else []), chain=False)
                    pe.val += 1
                    P.emit("scalar", lambda e, b=b1, o=sil[ctr % 2]:
                           e.activation(o, b[:, 0:512], AF.Silu),
                           waits=[(pe, pe.val), (vch, sil_free[ctr % 2])])
                    P.emit("vector", lambda e, b=b3, s=sil[ctr % 2],
                           d=hh[:, isub * 2048 + tc * 512: isub * 2048 + tc * 512 + 512]:
                           e.tensor_tensor(d, s, b[:, 0:512], ALU.mult),
                           waits=[(sch, sch.val)])
                    pu_free[ctr % 2] = (vch, vch.val)
                    sil_free[ctr % 2] = vch.val
                    ctr += 1
            hh_v = vch.val
            for j in range(TPC):
                for n in (0, 1):
                    bank = pd[n]
                    for isub in range(8):
                        w = []
                        if isub == 0:
                            w = [(vch, hh_v), (wsem, w_dn)]
                            if pd_free[n] is not None:
                                w.append(pd_free[n])
                        P.emit("tensor", lambda e, s=hh[:, isub * 2048 + j * 128: isub * 2048 + j * 128 + 128],
                               ww=wd[:, isub * 1024 + n * 512: isub * 1024 + n * 512 + 512],
                               b=bank, o=isub:
                               e.matmul(b[:, 0:512], s, ww, start=(o == 0), stop=(o == 7)),
                               waits=w, incs=([(pe, 1)] if isub == 7 else []), chain=False)
                    pe.val += 1
                    yw = [(pe, pe.val)]
                    if first:
                        yw.append((vch, y_zeroed))
                    ysl = y_acc[:, j * 1024 + n * 512: j * 1024 + n * 512 + 512]
                    P.emit("vector", lambda e, b=bank, d=ysl,
                           gc=cw_buf[:, j * 8 + e0: j * 8 + e0 + 1]:
                           e.scalar_tensor_tensor(d, b[:, 0:512], gc, d, ALU.mult, ALU.add),
                           waits=yw)
                    pd_free[n] = (vch, vch.val)
            hh_free = pe.val
            w_free = pe.val
            first = False
    bf.release(); f32.release()
    return dict()



def _build_A():
    nc = bass.Bass("TRN2", target_bir_lowering=False)
    io = {}
    for name, shape in (("x", (T, D)), ("WkT", (D, HD)), ("WvT", (D, HD)),
                        ("WqT", (D, D)), ("WoT", (D, D)), ("identf", (128, 128)),
                        ("pek_c", (HD, 1)), ("peqT", (128, TOK)),
                        ("x_own", (TOK, D)), ("mask8", (128, 8 * 512))):
        io[name] = dram_in(nc, name, shape, F32)
    io["ident"] = dram_in(nc, "ident", (128, 128), BF16)
    s_out = dram_out(nc, "s_out", (TOK, D), F32)
    with ExitStack() as stack:
        P, bf, f32, psum = make_ctx(stack, nc)
        B = dict(ident=bf.alloc("ident", 128), KT=bf.alloc("KT", NT * 128),
                 Vb=bf.alloc("Vb", NT * 128), xnT=bf.alloc("xnT", 8 * TOK),
                 Qb=bf.alloc("Qb", TPC * 1024), attn=bf.alloc("attn", TPC * 1024))
        phase_l0_kv(P, io, B, bf, f32, psum)
        phase_l0_own(P, io, B, bf, f32, psum)
        phase_attn(P, io, B, bf, f32, psum)
        phase_wo(P, io, B, bf, f32, psum, s_out)
        with nc.Block() as block:
            P.replay(block)
    return nc


def _build_B():
    nc = bass.Bass("TRN2", target_bir_lowering=False)
    io = {}
    for name, shape in (("x_own", (TOK, D)), ("s_in", (TOK, D)),
                        ("gwT", (D, E0)), ("ebias", (1, E0)),
                        ("sw1T", (D, D)), ("sw2T", (D, D)), ("sw3T", (D, D))):
        io[name] = dram_in(nc, name, shape, F32)
    for name in ("w1T", "w2T", "w3T"):
        io[name] = dram_in(nc, name, (E0, D, D), F32)
    io["ident"] = dram_in(nc, "ident", (128, 128), BF16)
    x1_out = dram_out(nc, "x1", (TOK, D), F32)
    with ExitStack() as stack:
        P, bf, f32, psum = make_ctx(stack, nc)
        B = dict(ident=bf.alloc("ident", 128), xnT=bf.alloc("xnT", 8 * TOK),
                 g_buf=f32.alloc("g_buf", TPC * 16),
                 y_acc=f32.alloc("y_acc", TPC * 1024))
        P.dma("sync", B["ident"], io["ident"], P.new_sem("idl"))
        phase_l0_own(P, io, B, bf, f32, psum, compute_q=False)
        phase_l0_gate(P, io, B, bf, f32, psum)
        phase_l0_ffn(P, io, B, bf, f32, psum)
        phase_combine(P, io, B, bf, f32, psum, io["s_in"], x1_out, xkey="x_own")
        with nc.Block() as block:
            P.replay(block)
    return nc


def _build_C():
    nc = bass.Bass("TRN2", target_bir_lowering=False)
    io = {}
    for name, shape in (("x1s", (TOK, D)), ("rwT", (D, E1)), ("winT", (D, D)),
                        ("woutT", (D, D)), ("m1T", (D, MLPF)), ("m2T", (MLPF, D)),
                        ("loglam_c", (128, 8)), ("ln1g_c", (128, 8)), ("ln1b_c", (128, 8)),
                        ("ln2g_c", (128, 8)), ("ln2b_c", (128, 8)),
                        ("dskip_r", (1, D)), ("ln1g_r", (1, D)), ("ln1b_r", (1, D)),
                        ("b1_c", (128, 16)), ("b2_r", (1, D))):
        io[name] = dram_in(nc, name, shape, F32)
    for name in ("wg", "wu"):
        io[name] = dram_in(nc, name, (E1, D, FF1), F32)
    io["wd"] = dram_in(nc, "wd", (E1, FF1, D), F32)
    io["ident"] = dram_in(nc, "ident", (128, 128), BF16)
    io["selC"] = dram_in(nc, "selC", (128, TPC), BF16)
    out_ap = dram_out(nc, "out", (TOK, D), F32)
    x2s = nc.dram_tensor("x2s", [TOK, D], F32).ap()
    s1s = nc.dram_tensor("s1s", [TOK, D], F32).ap()
    cc_in = nc.dram_tensor("cc_in", [128, 128], F32).ap()
    cc_out = nc.dram_tensor("cc_out", [1024, 128], F32).ap()
    with ExitStack() as stack:
        P, bf, f32, psum = make_ctx(stack, nc)
        B = dict(ident=bf.alloc("ident", 128), xn1T=bf.alloc("xn1T", 8 * TOK),
                 cw_buf=f32.alloc("cw_buf", TPC * 8),
                 Cown=f32.alloc("Cown", 8 * TPC),
                 Ptab=f32.alloc("Ptab", 8 * 128),
                 oml=f32.alloc("oml", 8))
        P.dma("sync", B["ident"], io["ident"], P.new_sem("idl"))
        phase_l0_own(P, io, B, bf, f32, psum, compute_q=False, xkey="x1s", tkey="xn1T")
        phase_l1_router(P, io, B, bf, f32, psum)
        bf.mark()
        B["y_loc"] = bf.alloc("y_loc", 8 * TOK)
        phase_lru_scan(P, io, B, bf, f32, psum, cc_in, cc_out)
        phase_lru_out(P, io, B, bf, f32, psum, x2s)
        bf.release()
        phase_mlp(P, io, B, bf, f32, psum, x2s, s1s)
        B["y_acc"] = f32.alloc("y_acc", TPC * 1024)
        phase_l1_ffn(P, io, B, bf, f32, psum)
        phase_combine(P, io, B, bf, f32, psum, s1s, out_ap, xkey="x1s")
        with nc.Block() as block:
            P.replay(block)
    return nc


def kernel_3launch(**inputs):
    sh, per = host_prepare(inputs)
    own_rows = [np.concatenate([np.arange(t * 128, (t + 1) * 128) for t in own_tiles(c)])
                for c in range(NCORES)]
    cores = list(range(NCORES))
    A = lambda v: np.ascontiguousarray(v)

    nc = _build_A()
    maps = []
    for c in cores:
        maps.append({"x": sh["x"], "WkT": sh["WkT"], "WvT": sh["WvT"],
                     "WqT": sh["WqT"], "WoT": sh["WoT"], "ident": sh["ident"],
                     "identf": sh["identf"], "pek_c": sh["pek_c"],
                     "peqT": per[c]["peqT"], "x_own": per[c]["x_own"],
                     "mask8": per[c]["mask8"]})
    ra = run_bass_kernel_spmd(nc, maps, core_ids=cores)
    s_parts = [r["s_out"] for r in ra.results]

    nc = _build_B()
    maps = []
    for c in cores:
        maps.append({"x_own": per[c]["x_own"], "s_in": A(s_parts[c]),
                     "gwT": sh["gwT"], "ebias": sh["ebias"],
                     "w1T": sh["w1T"], "w2T": sh["w2T"], "w3T": sh["w3T"],
                     "sw1T": sh["sw1T"], "sw2T": sh["sw2T"], "sw3T": sh["sw3T"],
                     "ident": sh["ident"]})
    rb = run_bass_kernel_spmd(nc, maps, core_ids=cores)
    x1_parts = [r["x1"] for r in rb.results]

    nc = _build_C()
    maps = []
    for c in cores:
        maps.append({"x1s": A(x1_parts[c]), "rwT": sh["rwT"], "winT": sh["winT"],
                     "woutT": sh["woutT"], "m1T": sh["m1T"], "m2T": sh["m2T"],
                     "loglam_c": sh["loglam_c"], "ln1g_c": sh["ln1g_c"],
                     "ln1b_c": sh["ln1b_c"], "ln2g_c": sh["ln2g_c"],
                     "ln2b_c": sh["ln2b_c"], "dskip_r": sh["dskip_r"],
                     "ln1g_r": sh["ln1g_r"], "ln1b_r": sh["ln1b_r"],
                     "b1_c": sh["b1_c"], "b2_r": sh["b2_r"],
                     "wg": sh["wg"], "wu": sh["wu"], "wd": sh["wd"],
                     "ident": sh["ident"], "selC": per[c]["selC"]})
    rc = run_bass_kernel_spmd(nc, maps, core_ids=cores)

    out = np.zeros((T, D), dtype=np.float32)
    for c in cores:
        out[own_rows[c]] = rc.results[c]["out"]
    return out.reshape(1, T, D)



def _build_fused():
    nc = bass.Bass("TRN2", target_bir_lowering=False)
    io = {}
    for name, shape in (("x", (T, D)), ("WkT", (D, HD)), ("WvT", (D, HD)),
                        ("WqT", (D, D)), ("WoT", (D, D)), ("identf", (128, 128)),
                        ("pek_c", (HD, 1)), ("peqT", (128, TOK)),
                        ("x_own", (TOK, D)), ("mask8", (128, 8 * 512)),
                        ("gwT", (D, E0)), ("ebias", (1, E0)),
                        ("sw1T", (D, D)), ("sw2T", (D, D)), ("sw3T", (D, D)),
                        ("rwT", (D, E1)), ("winT", (D, D)), ("woutT", (D, D)),
                        ("m1T", (D, MLPF)), ("m2T", (MLPF, D)),
                        ("loglam_c", (128, 8)), ("ln1g_c", (128, 8)), ("ln1b_c", (128, 8)),
                        ("ln2g_c", (128, 8)), ("ln2b_c", (128, 8)),
                        ("dskip_r", (1, D)), ("ln1g_r", (1, D)), ("ln1b_r", (1, D)),
                        ("b1_c", (128, 16)), ("b2_r", (1, D))):
        io[name] = dram_in(nc, name, shape, F32)
    for name in ("w1T", "w2T", "w3T"):
        io[name] = dram_in(nc, name, (E0, D, D), F32)
    for name in ("wg", "wu"):
        io[name] = dram_in(nc, name, (E1, D, FF1), F32)
    io["wd"] = dram_in(nc, "wd", (E1, FF1, D), F32)
    io["ident"] = dram_in(nc, "ident", (128, 128), BF16)
    io["selC"] = dram_in(nc, "selC", (128, TPC), BF16)
    out_ap = dram_out(nc, "out", (TOK, D), F32)
    s_sc = nc.dram_tensor("s_sc", [TOK, D], F32).ap()
    x1_sc = nc.dram_tensor("x1_sc", [TOK, D], F32).ap()
    x2_sc = nc.dram_tensor("x2_sc", [TOK, D], F32).ap()
    s1_sc = nc.dram_tensor("s1_sc", [TOK, D], F32).ap()
    cc_in = nc.dram_tensor("cc_in", [128, 128], F32).ap()
    cc_out = nc.dram_tensor("cc_out", [1024, 128], F32).ap()
    io["x1s"] = x1_sc
    with ExitStack() as stack:
        P, bf, f32, psum = make_ctx(stack, nc)
        B = dict(ident=bf.alloc("ident", 128), xnT=bf.alloc("xnT", 8 * TOK))
        bf.mark()
        B["KT"] = bf.alloc("KT", NT * 128)
        B["Vb"] = bf.alloc("Vb", NT * 128)
        B["attn"] = bf.alloc("attn", TPC * 1024)
        phase_l0_front(P, io, B, bf, f32, psum)
        phase_wo(P, io, B, bf, f32, psum, s_sc)
        bf.release()
        bf.mark()
        B["g_buf"] = f32.alloc("g_buf", TPC * 16)
        B["y_acc"] = f32.alloc("y_acc", TPC * 1024)
        phase_l0_gate(P, io, B, bf, f32, psum)
        phase_moe2(P, io, B, bf, f32, psum, moe_specs_l0(io), "xnT")
        phase_combine(P, io, B, bf, f32, psum, s_sc, x1_sc, xkey="x_own")
        bf.release()
        B["xn1T"] = B["xnT"]
        phase_l0_own(P, io, B, bf, f32, psum, compute_q=False, xkey="x1s", tkey="xn1T")
        B["cw_buf"] = f32.alloc("cw_buf", TPC * 8)
        B["Cown"] = f32.alloc("Cown", 8 * TPC)
        B["Ptab"] = f32.alloc("Ptab", 8 * 128)
        B["oml"] = f32.alloc("oml", 8)
        phase_l1_router(P, io, B, bf, f32, psum)
        bf.mark()
        B["y_loc"] = bf.alloc("y_loc", 8 * TOK)
        phase_lru_scan(P, io, B, bf, f32, psum, cc_in, cc_out)
        phase_lru_out(P, io, B, bf, f32, psum, x2_sc)
        bf.release()
        phase_mlp(P, io, B, bf, f32, psum, x2_sc, s1_sc)
        bf.mark()
        B["y_acc"] = f32.alloc("y_acc", TPC * 1024)
        phase_moe2(P, io, B, bf, f32, psum, moe_specs_l1(io), "xn1T")
        phase_combine(P, io, B, bf, f32, psum, s1_sc, out_ap, xkey="x1s")
        bf.release()
        with nc.Block() as block:
            P.replay(block)
    return nc


def kernel(**inputs):
    sh, per = host_prepare(inputs)
    own_rows = [np.concatenate([np.arange(t * 128, (t + 1) * 128) for t in own_tiles(c)])
                for c in range(NCORES)]
    cores = list(range(NCORES))
    nc = _build_fused()
    shared_keys = ("x", "WkT", "WvT", "WqT", "WoT", "ident", "identf", "pek_c",
                   "gwT", "ebias", "w1T", "w2T", "w3T", "sw1T", "sw2T", "sw3T",
                   "rwT", "winT", "woutT", "m1T", "m2T", "loglam_c",
                   "ln1g_c", "ln1b_c", "ln2g_c", "ln2b_c", "dskip_r",
                   "ln1g_r", "ln1b_r", "b1_c", "b2_r", "wg", "wu", "wd")
    maps = []
    for c in cores:
        m = {k: sh[k] for k in shared_keys}
        m["peqT"] = per[c]["peqT"]
        m["x_own"] = per[c]["x_own"]
        m["mask8"] = per[c]["mask8"]
        m["selC"] = per[c]["selC"]
        maps.append(m)
    res = run_bass_kernel_spmd(nc, maps, core_ids=cores)
    out = np.zeros((T, D), dtype=np.float32)
    for c in cores:
        out[own_rows[c]] = res.results[c]["out"]
    return out.reshape(1, T, D)


def kernel_unfused(**inputs):
    sh, per = host_prepare(inputs)
    own_rows = [np.concatenate([np.arange(t * 128, (t + 1) * 128) for t in own_tiles(c)])
                for c in range(NCORES)]
    raise NotImplementedError



def phase_moe2(P, x_ap, B, bf, f32, psum, specs, xnT_key):
    vch, sch, gch = P.chain["vector"], P.chain["scalar"], P.chain["gpsimd"]
    pe = P.sem_pe
    xnT, y_acc = B[xnT_key], B["y_acc"]
    P.barrier()
    bf.mark(); f32.mark()
    wgb = [bf.alloc("wg0", 8 * 512), bf.alloc("wg1", 8 * 512)]
    wub = [bf.alloc("wu0", 8 * 512), bf.alloc("wu1", 8 * 512)]
    wdb = [bf.alloc("wd0", 4 * 1024), bf.alloc("wd1", 4 * 1024)]
    hhb = [bf.alloc("hh0", 4 * 2048), bf.alloc("hh1", 4 * 2048)]
    sil = [f32.alloc("sil0", 512), f32.alloc("sil1", 512)]
    P.emit("vector", lambda e: e.memset(y_acc[:, 0:8192], 0.0))
    P.emit("vector", lambda e: e.memset(y_acc[:, 8192:16384], 0.0))
    y_zeroed = vch.val
    wsem = [P.new_sem("mw0"), P.new_sem("mw1")]
    pu = [(psum["b2"], psum["b3"]), (psum["b4"], psum["b5"])]
    pd = [psum["b6"], psum["b7"]]
    pu_free = [None, None]
    pd_free = [None, None]
    dn_done = [0, 0]
    up_done = [0, 0]
    sil_free = [0, 0]
    nsl = len(specs)
    w_up_v = {}
    w_dn_v = {}

    def issue_gu(s):
        k = s % 2
        gcol, sg, su, sd = specs[s]
        dw = [(pe, up_done[k])] if up_done[k] else []
        P.dma("gpsimd", wgb[k].rearrange("p (a d) -> p a d", a=8),
              sg.rearrange("(a p) d -> p a d", p=128), wsem[k], waits=dw)
        P.dma("gpsimd", wub[k].rearrange("p (a d) -> p a d", a=8),
              su.rearrange("(a p) d -> p a d", p=128), wsem[k])
        w_up_v[s] = wsem[k].val

    def issue_d(s):
        k = s % 2
        gcol, sg, su, sd = specs[s]
        dw = [(pe, dn_done[k])] if dn_done[k] else []
        P.dma("gpsimd", wdb[k].rearrange("p (a d) -> p a d", a=4),
              sd.rearrange("(a p) d -> p a d", p=128), wsem[k], waits=dw)
        w_dn_v[s] = wsem[k].val

    issue_gu(0); issue_d(0)
    issue_gu(1); issue_d(1)
    for s in range(nsl):
        k = s % 2
        gcol = specs[s][0]
        w_up = w_up_v[s]
        w_dn = w_dn_v[s]
        ctr = 0
        for isub in range(4):
            for tc in range(4):
                b1, b3 = pu[ctr % 2]
                w = [(wsem[k], w_up)] if (isub == 0 and tc == 0) else []
                if pu_free[ctr % 2] is not None:
                    w.append(pu_free[ctr % 2])
                for et in range(8):
                    P.emit("tensor", lambda e, ww=wgb[k][:, et * 512 + isub * 128: et * 512 + isub * 128 + 128],
                           r=xnT[:, et * 2048 + tc * 512: et * 2048 + tc * 512 + 512],
                           b=b1, o=et:
                           e.matmul(b[:, 0:512], ww, r, start=(o == 0), stop=(o == 7)),
                           waits=(w if et == 0 else []), chain=False)
                for et in range(8):
                    P.emit("tensor", lambda e, ww=wub[k][:, et * 512 + isub * 128: et * 512 + isub * 128 + 128],
                           r=xnT[:, et * 2048 + tc * 512: et * 2048 + tc * 512 + 512],
                           b=b3, o=et:
                           e.matmul(b[:, 0:512], ww, r, start=(o == 0), stop=(o == 7)),
                           incs=([(pe, 1)] if et == 7 else []), chain=False)
                pe.val += 1
                P.emit("scalar", lambda e, b=b1, o=sil[ctr % 2]:
                       e.activation(o, b[:, 0:512], AF.Silu),
                       waits=[(pe, pe.val), (vch, sil_free[ctr % 2])])
                P.emit("vector", lambda e, b=b3, si=sil[ctr % 2],
                       d=hhb[k][:, isub * 2048 + tc * 512: isub * 2048 + tc * 512 + 512]:
                       e.tensor_tensor(d, si, b[:, 0:512], ALU.mult),
                       waits=[(sch, sch.val)])
                pu_free[ctr % 2] = (vch, vch.val)
                sil_free[ctr % 2] = vch.val
                ctr += 1
        hh_v = vch.val
        up_done[k] = pe.val
        if s + 2 < nsl:
            issue_gu(s + 2)
        for j in range(TPC):
            for n in (0, 1):
                bank = pd[n]
                for isub in range(4):
                    w = []
                    if isub == 0:
                        w = [(vch, hh_v)]
                        if j == 0 and n == 0:
                            w.append((wsem[k], w_dn))
                        if pd_free[n] is not None:
                            w.append(pd_free[n])
                    P.emit("tensor", lambda e, st=hhb[k][:, isub * 2048 + j * 128: isub * 2048 + j * 128 + 128],
                           ww=wdb[k][:, isub * 1024 + n * 512: isub * 1024 + n * 512 + 512],
                           b=bank, o=isub:
                           e.matmul(b[:, 0:512], st, ww, start=(o == 0), stop=(o == 3)),
                           waits=w, incs=([(pe, 1)] if isub == 3 else []), chain=False)
                pe.val += 1
                yw = [(pe, pe.val)]
                if s == 0:
                    yw.append((vch, y_zeroed))
                ysl = y_acc[:, j * 1024 + n * 512: j * 1024 + n * 512 + 512]
                if gcol is None:
                    P.emit("vector", lambda e, b=bank, d=ysl:
                           e.tensor_tensor(d, b[:, 0:512], d, ALU.add), waits=yw)
                else:
                    col = j * gcol[1] + gcol[2]
                    gc = B[gcol[0]][:, col:col + 1]
                    P.emit("vector", lambda e, b=bank, d=ysl, g=gc:
                           e.scalar_tensor_tensor(d, b[:, 0:512], g, d, ALU.mult, ALU.add),
                           waits=yw)
                pd_free[n] = (vch, vch.val)
        dn_done[k] = pe.val
        if s + 2 < nsl:
            issue_d(s + 2)
    bf.release(); f32.release()
    return dict()


def moe_specs_l0(x_ap, j_of=None):
    specs = []
    for e0 in range(E0):
        for sl in range(2):
            specs.append((("g_buf", 16, e0),
                          x_ap["w1T"][e0][:, sl * 512:(sl + 1) * 512],
                          x_ap["w3T"][e0][:, sl * 512:(sl + 1) * 512],
                          x_ap["w2T"][e0][sl * 512:(sl + 1) * 512, :]))
    for sl in range(2):
        specs.append((None,
                      x_ap["sw1T"][:, sl * 512:(sl + 1) * 512],
                      x_ap["sw3T"][:, sl * 512:(sl + 1) * 512],
                      x_ap["sw2T"][sl * 512:(sl + 1) * 512, :]))
    return specs


def moe_specs_l1(x_ap):
    specs = []
    for e0 in range(E1):
        for sl in range(8):
            specs.append((("cw_buf", 8, e0),
                          x_ap["wg"][e0][:, sl * 512:(sl + 1) * 512],
                          x_ap["wu"][e0][:, sl * 512:(sl + 1) * 512],
                          x_ap["wd"][e0][sl * 512:(sl + 1) * 512, :]))
    return specs



def phase_l0_front(P, x_ap, B, bf, f32, psum):
    vch, sch = P.chain["vector"], P.chain["scalar"]
    pe = P.sem_pe
    ident = B["ident"]
    KT, Vb, xnT, attn = B["KT"], B["Vb"], B["xnT"], B["attn"]
    P.barrier()
    bf.mark(); f32.mark()
    Qb = bf.alloc("Qb2", 2 * 1024)
    qslot_free = [0, 0]
    wk = bf.alloc("WkT", 8 * 128)
    wv = bf.alloc("WvT", 8 * 128)
    wq = bf.alloc("WqT", 8 * 1024)
    pek = f32.alloc("pek", 8)
    peq = bf.alloc("peqT", TOK)
    xt = [f32.alloc("xt0", 1024), f32.alloc("xt1", 1024)]
    scr = f32.alloc("scr", 1024)
    st = f32.alloc("stats", 8)
    xnb = [bf.alloc("xnb0", 1024), bf.alloc("xnb1", 1024)]
    xnt_t = [bf.alloc("xnt_t0", 1024)]
    wsem = P.new_sem("fw")
    P.dma("gpsimd", wk.rearrange("p (a d) -> p a d", a=8),
          x_ap["WkT"].rearrange("(a p) d -> p a d", p=128), wsem)
    P.dma("gpsimd", wv.rearrange("p (a d) -> p a d", a=8),
          x_ap["WvT"].rearrange("(a p) d -> p a d", p=128), wsem)
    P.dma("gpsimd", wq.rearrange("p (a d) -> p a d", a=8),
          x_ap["WqT"].rearrange("(a p) d -> p a d", p=128), wsem)
    P.dma("sync", pek[:, 0:1], x_ap["pek_c"], wsem)
    P.dma("gpsimd", peq, x_ap["peqT"], wsem)
    P.dma("sync", ident, x_ap["ident"], wsem)
    w_ready = wsem.val
    xsem = [P.new_sem("fx0"), P.new_sem("fx1")]
    osem = P.new_sem("fox")
    pT = [psum["bt0"], psum["bt1"]]
    pK, pV = psum["b2"], psum["b3"]
    pQ = [psum["b2"], psum["b3"]]
    xt_free = [0, 0]
    xnb_free = [0, 0]
    mask8 = bf.alloc("mask8", 8 * 512)
    stash = f32.alloc("stash", 8 * 512)
    probs = [[bf.alloc("pr00", 512), bf.alloc("pr01", 512)],
             [bf.alloc("pr10", 512), bf.alloc("pr11", 512)]]
    mcol = [f32.alloc("mcol0", 8), f32.alloc("mcol1", 8)]
    onescol = bf.alloc("onescol", 8)
    onerow = f32.alloc("onerow", 128)
    msc = f32.alloc("msc", 8)
    identf = f32.alloc("identf", 128)
    pacc = [f32.alloc("pacc0", 512), f32.alloc("pacc1", 512)]
    rb = stash[:, 2048:3072]
    pbf = [probs[0][0], probs[1][0]]
    P.dma("gpsimd", mask8, x_ap["mask8"], wsem)
    P.dma("sync", identf, x_ap["identf"], wsem)
    am_ready = wsem.val
    P.emit("vector", lambda e: e.memset(onescol, 1.0))
    P.emit("vector", lambda e: e.memset(onerow[0:1, :], 1.0))
    ones_ready = vch.val
    scb = [psum["b0"], psum["b1"], psum["b2"], psum["b3"]]
    pa = [psum["b4"], psum["b5"]]
    pl = [psum["b6"], psum["b7"]]
    bfree = [None, None, None, None]
    plfree = [None, None]
    pa_free = [None, None]
    probs_free = [[0, 0], [0, 0]]
    probs_accv = [[0, 0], [0, 0]]

    kvs_l = {}
    kvv_l = {}
    qrdy_l = {}

    def emit_prep(j):
        for p8 in range(8):
            i = 8 * j + p8
            k = i % 2
            P.dma("sync", xt[k], x_ap["x"][i * 128:(i + 1) * 128, :], xsem[k],
                  waits=[(vch, xt_free[k])])
            nw = ln_tile(P, xt[k], xnb[k], st, scr,
                         waits=[(xsem[k], xsem[k].val)],
                         extra_norm_waits=[(pe, xnb_free[k])])
            xt_free[k] = vch.val
            dsts = [xnt_t[0][:, et * 128:(et + 1) * 128] for et in range(8)]
            tev = [bfree[0], bfree[1]]
            ev = transpose8(P, xnb[k], dsts, ident, pT, tev, (vch, nw),
                            extra_first=([(wsem, w_ready)] if i == 0 else []))
            bfree[0], bfree[1] = tev
            xnb_free[k] = pe.val
            for et in range(8):
                kw = list(ev) if et == 0 else []
                if et == 0 and bfree[2] is not None:
                    kw.append(bfree[2])
                P.emit("tensor", lambda e, w=wk[:, et * 128:(et + 1) * 128], r=dsts[et], o=et:
                       e.matmul(pK[:, 0:128], w, r, start=(o == 0), stop=(o == 7)),
                       waits=kw, incs=([(pe, 1)] if et == 7 else []), chain=False)
            pe.val += 1
            P.emit("scalar", lambda e, d=KT[:, i * 128:(i + 1) * 128]:
                   e.add(d, pK[:, 0:128], pek[:, 0:1]),
                   waits=[(pe, pe.val)])
            bfree[2] = (sch, sch.val)
            for et in range(8):
                vw = [bfree[3]] if (et == 0 and bfree[3] is not None) else []
                P.emit("tensor", lambda e, s=dsts[et], w=wv[:, et * 128:(et + 1) * 128], o=et:
                       e.matmul(pV[:, 0:128], s, w, start=(o == 0), stop=(o == 7)),
                       waits=vw, incs=([(pe, 1)] if et == 7 else []), chain=False)
            pe.val += 1
            P.emit("vector", lambda e, d=Vb[:, i * 128:(i + 1) * 128]: e.tensor_copy(d, pV[:, 0:128]),
                   waits=[(pe, pe.val)])
            bfree[3] = (vch, vch.val)
        P.dma("sync", xt[0], x_ap["x_own"][j * 128:(j + 1) * 128, :], osem,
              waits=[(vch, xt_free[0])])
        nw = ln_tile(P, xt[0], xnb[0], st, scr,
                     waits=[(osem, osem.val)],
                     extra_norm_waits=[(pe, xnb_free[0])])
        xt_free[0] = vch.val
        odsts = [xnT[:, et * 2048 + j * 128: et * 2048 + j * 128 + 128] for et in range(8)]
        tev = [bfree[0], bfree[1]]
        oev = transpose8(P, xnb[0], odsts, ident, pT, tev, (vch, nw))
        bfree[0], bfree[1] = tev
        xnb_free[0] = pe.val
        for h in range(NH):
            bank = pQ[h % 2]
            for et in range(8):
                qw = list(oev) if et == 0 else []
                if et == 0 and bfree[2 + h % 2] is not None:
                    qw.append(bfree[2 + h % 2])
                P.emit("tensor", lambda e, w=wq[:, et * 1024 + h * 128: et * 1024 + h * 128 + 128],
                       r=odsts[et], o=et, b=bank:
                       e.matmul(b[:, 0:128], w, r, start=(o == 0), stop=(o == 7)),
                       waits=qw, incs=([(pe, 1)] if et == 7 else []), chain=False)
            pe.val += 1
            P.emit("vector", lambda e, b=bank,
                   d=Qb[:, (j % 2) * 1024 + h * 128: (j % 2) * 1024 + h * 128 + 128],
                   p=peq[:, j * 128:(j + 1) * 128]:
                   e.tensor_tensor(d, b[:, 0:128], p, ALU.add),
                   waits=[(pe, pe.val), (pe, qslot_free[j % 2])])
            bfree[2 + h % 2] = (vch, vch.val)
        q_ready = vch.val
        kvs_l[j] = sch.val
        kvv_l[j] = vch.val
        qrdy_l[j] = vch.val

    def emit_attn(j):
        kv_s = kvs_l[j]
        kv_v = kvv_l[j]
        q_ready = qrdy_l[j]
        nmain = 8 * j
        for hg in (0, 1):
            qsl = Qb[:, (j % 2) * 1024 + hg * 512: (j % 2) * 1024 + (hg + 1) * 512]
            for p in range(8):
                bank = scb[p % 4]
                w = [(sch, kv_s), (vch, q_ready)]
                if j == 0 and hg == 0 and p < 4:
                    w.append((wsem, am_ready))
                if bfree[p % 4] is not None:
                    w.append(bfree[p % 4])
                P.emit("tensor", lambda e, kt=KT[:, (8 * j + p) * 128:(8 * j + p + 1) * 128],
                       q=qsl, b=bank: e.matmul(b[:, 0:512], kt, q, start=True, stop=True),
                       waits=w, incs=[(pe, 1)], chain=False)
                pe.val += 1
                P.emit("vector", lambda e, b=bank, o=stash[:, p * 512:(p + 1) * 512],
                       mk=mask8[:, p * 512:(p + 1) * 512]:
                       e.tensor_tensor(o, b[:, 0:512], mk, ALU.add),
                       waits=[(pe, pe.val)])
                bfree[p % 4] = (vch, vch.val)
            P.emit("vector", lambda e: e.reduce_max(msc[:, 0:1], stash, axis=AX.X))
            mx_v = vch.val
            tw = [(vch, mx_v)]
            if j == 0 and hg == 0:
                tw.append((wsem, am_ready))
            if plfree[0] is not None:
                tw.append(plfree[0])
            P.emit("tensor", lambda e: e.transpose(pl[0][0:1, 0:128], msc[:, 0:1], identf),
                   waits=tw, incs=[(pe, 1)], chain=False)
            pe.val += 1
            P.emit("vector", lambda e: e.reduce_max(msc[0:1, 1:2], pl[0][0:1, 0:128], axis=AX.X),
                   waits=[(pe, pe.val)])
            P.emit("vector", lambda e: e.tensor_scalar(
                msc[0:1, 3:4], msc[0:1, 1:2], -SCALE, None, ALU.mult))
            mneg_v = vch.val
            P.emit("tensor", lambda e: e.matmul(pl[0][:, 33:34], onerow[0:1, :],
                   msc[0:1, 3:4], start=True, stop=True),
                   waits=[(vch, mneg_v), (vch, ones_ready)], incs=[(pe, 1)], chain=False)
            pe.val += 1
            P.emit("vector", lambda e, d=mcol[hg][:, 0:1]: e.tensor_copy(d, pl[0][:, 33:34]),
                   waits=[(pe, pe.val)])
            plfree[0] = (vch, vch.val)
            mc_ready = vch.val
            for p in range(8):
                pb = probs[hg][p % 2]
                P.emit("scalar", lambda e, o=pb, i=stash[:, p * 512:(p + 1) * 512],
                       m=mcol[hg][:, 0:1]:
                       e.activation(o, i, AF.Exp, bias=m, scale=SCALE),
                       waits=[(vch, mc_ready), (pe, probs_free[hg][p % 2]),
                              (vch, probs_accv[hg][p % 2])])
                first = (p == 0)
                last = (j == 0 and p == 7)
                if hg == 0:
                    if first:
                        P.emit("vector", lambda e, a=pacc[0], pbuf=pb: e.tensor_copy(a, pbuf),
                               waits=[(sch, sch.val)])
                    else:
                        P.emit("vector", lambda e, a=pacc[0], pbuf=pb:
                               e.tensor_tensor(a, a, pbuf, ALU.add),
                               waits=[(sch, sch.val)])
                    acc_v = vch.val
                else:
                    lw = [(sch, sch.val)]
                    if first and plfree[1] is not None:
                        lw.append(plfree[1])
                    P.emit("tensor", lambda e, o=pl[1][0:1, 0:512], pbuf=pb, f=first, l=last:
                           e.matmul(o, onescol[:, 0:1], pbuf, start=f, stop=l),
                           waits=lw, chain=False)
                    acc_v = 0
                pw = [(sch, sch.val), (vch, kv_v)]
                if first and pa_free[hg] is not None:
                    pw.append(pa_free[hg])
                P.emit("tensor", lambda e, v=Vb[:, (8 * j + p) * 128:(8 * j + p + 1) * 128],
                       pbuf=pb, b=pa[hg], f=first, l=last:
                       e.matmul(b[:, 0:512], v, pbuf, start=f, stop=l),
                       waits=pw, incs=[(pe, 1)], chain=False)
                pe.val += 1
                probs_free[hg][p % 2] = pe.val
                probs_accv[hg][p % 2] = acc_v
        exp_done = {}
        for kb in range(nmain + 1):
            if kb < nmain:
                for hg in (0, 1):
                    bi = 2 * hg + kb % 2
                    bank = scb[bi]
                    qsl = Qb[:, (j % 2) * 1024 + hg * 512: (j % 2) * 1024 + (hg + 1) * 512]
                    w = [bfree[bi]] if bfree[bi] is not None else []
                    P.emit("tensor", lambda e, kt=KT[:, kb * 128:(kb + 1) * 128], q=qsl, b=bank:
                           e.matmul(b[:, 0:512], kt, q, start=True, stop=True),
                           waits=w, incs=[(pe, 1)], chain=False)
                    pe.val += 1
                    P.emit("scalar", lambda e, b=bank, o=probs[hg][kb % 2], m=mcol[hg][:, 0:1]:
                           e.activation(o, b[:, 0:512], AF.Exp, bias=m, scale=SCALE),
                           waits=[(pe, pe.val), (pe, probs_free[hg][kb % 2]),
                                  (vch, probs_accv[hg][kb % 2])])
                    bfree[bi] = (sch, sch.val)
                    exp_done[(kb, hg)] = (sch, sch.val)
            if kb > 0:
                kp = kb - 1
                last = (kp == nmain - 1)
                for hg in (0, 1):
                    pb = probs[hg][kp % 2]
                    if hg == 0:
                        P.emit("vector", lambda e, a=pacc[0], pbuf=pb:
                               e.tensor_tensor(a, a, pbuf, ALU.add),
                               waits=[exp_done[(kp, hg)]])
                        probs_accv[0][kp % 2] = vch.val
                    else:
                        P.emit("tensor", lambda e, o=pl[1][0:1, 0:512], pbuf=pb,
                               l=(kp == nmain - 1):
                               e.matmul(o, onescol[:, 0:1], pbuf, start=False, stop=l),
                               waits=[exp_done[(kp, hg)]], chain=False)
                        probs_accv[1][kp % 2] = 0
                    P.emit("tensor", lambda e, v=Vb[:, kp * 128:(kp + 1) * 128],
                           pbuf=pb, b=pa[hg], l=last:
                           e.matmul(b[:, 0:512], v, pbuf, start=False, stop=l),
                           waits=[exp_done[(kp, hg)]], incs=[(pe, 1)], chain=False)
                    pe.val += 1
                    probs_free[hg][kp % 2] = pe.val
        P.emit("vector", lambda e, d=pbf[0], a=pacc[0]: e.tensor_copy(d, a),
               waits=[(pe, probs_free[0][0])])
        cast_v = vch.val
        lw = [(vch, cast_v), (vch, ones_ready)]
        if plfree[0] is not None:
            lw.append(plfree[0])
        P.emit("tensor", lambda e, o=pl[0][0:1, 0:512], pbuf=pbf[0]:
               e.matmul(o, onescol[:, 0:1], pbuf, start=True, stop=True),
               waits=lw, incs=[(pe, 1)], chain=False)
        pe.val += 1
        probs_free[0][0] = pe.val
        P.emit("tensor", lambda e: e.nop(), incs=[(pe, 1)], chain=False)
        pe.val += 1
        l_done = pe.val
        P.emit("vector", lambda e: e.reciprocal(stash[0:1, 0:512], pl[0][0:1, 0:512]),
               waits=[(pe, l_done)])
        P.emit("vector", lambda e: e.reciprocal(stash[0:1, 512:1024], pl[1][0:1, 0:512]))
        rec_v = vch.val
        plfree[0] = (vch, rec_v)
        plfree[1] = (vch, rec_v)
        for hg in (0, 1):
            bw = [(vch, rec_v), plfree[hg]]
            P.emit("tensor", lambda e, r=stash[0:1, hg * 512:(hg + 1) * 512], b=pl[hg]:
                   e.matmul(b[:, 0:512], onerow[0:1, :], r, start=True, stop=True),
                   waits=bw, incs=[(pe, 1)], chain=False)
            pe.val += 1
            P.emit("scalar", lambda e, d=rb[:, hg * 512:(hg + 1) * 512], b=pl[hg]:
                   e.copy(d, b[:, 0:512]), waits=[(pe, pe.val)])
            plfree[hg] = (sch, sch.val)
            P.emit("vector", lambda e, b=pa[hg], r=rb[:, hg * 512:(hg + 1) * 512],
                   d=attn[:, j * 1024 + hg * 512: j * 1024 + (hg + 1) * 512]:
                   e.tensor_tensor(d, b[:, 0:512], r, ALU.mult),
                   waits=[(sch, sch.val)])
            pa_free[hg] = (vch, vch.val)
        qslot_free[j % 2] = pe.val

    for jj in range(TPC + 1):
        if jj < TPC:
            emit_prep(jj)
        if jj > 0:
            emit_attn(jj - 1)
    bf.release(); f32.release()
    return dict()
